# version 15 (final)
# baseline (speedup 1.0000x reference)
import contextlib
import math
import numpy as np

import concourse.bass as bass
import concourse.mybir as mybir
from concourse.bass_types import AP
from concourse.bass_utils import run_bass_kernel_spmd

F32 = mybir.dt.float32
BF16 = mybir.dt.bfloat16
AF = mybir.ActivationFunctionType
ALU = mybir.AluOpType

NCORES = 8
T = 4096
D = 1024
TB = 512
NT = T // TB
ND = D // 128

ENGINES = ("sync", "vector", "scalar", "tensor", "gpsimd")


class Sem:
    __slots__ = ("h", "n", "name")

    def __init__(self, h, name):
        self.h = h
        self.n = 0
        self.name = name


class Ctx:
    def __init__(self, nc):
        self.nc = nc
        self.ops = []
        self.sems = {}
        self.estack = contextlib.ExitStack()
        self.consts = {}
        self.uid = 0
        self.dram_cache = {}
        self.wbf = {}
        self.wbf_gate = {}
        self.xdr = None

    SEM_ALIAS = {}
    for _n in ["wl0a40", "wl1a40", "wl_3f0", "wl_3f30", "wl_3f40", "wl_3f60",
               "wl_l0a0", "wl_l0f0", "wl_l1a0", "wl_l1f0", "wl_l2a0",
               "wl_l2f0", "wl_l2o0", "wl_l3a0", "wl0a_dt"]:
        SEM_ALIAS[_n] = "wlA"
    for _n in ["wl0a41", "wl1a41", "wl_3f1", "wl_3f31", "wl_3f41", "wl_3f61",
               "wl_l0a1", "wl_l0f1", "wl_l1a1", "wl_l1f1", "wl_l2a1",
               "wl_l2f1", "wl_l2o1", "wl_l3a1", "wl0a_b"]:
        SEM_ALIAS[_n] = "wlB"
    SEM_ALIAS["wl0a_c"] = "wlC"
    for _n in ["sez0a", "sez1a", "seh0f", "seh2f", "seh3f"]:
        SEM_ALIAS[_n] = "seh"
    for _n in ["seo0a", "seo0f", "seo1a", "seo2a", "seo2f", "seo3f",
               "seo3f6"]:
        SEM_ALIAS[_n] = "seo"
    for _n in ["seq2a", "seq3a", "seq3f"]:
        SEM_ALIAS[_n] = "seq"
    for _n in ["sev2a", "sev3a", "sev3f", "sek3f"]:
        SEM_ALIAS[_n] = "sev"
    for _n in ["set0f", "set2f", "set3f"]:
        SEM_ALIAS[_n] = "setx"
    for _n in ["sdy3f"]:
        SEM_ALIAS[_n] = "sdy"
    for _n in ["se0a2"]:
        SEM_ALIAS[_n] = "se2x"
    for _n in ["sef2a"]:
        SEM_ALIAS[_n] = "sefx"
    for _n in ["sex1a"]:
        SEM_ALIAS[_n] = "sexx"
    for _n in ["seu3f"]:
        SEM_ALIAS[_n] = "seu"
    for _n in ["dqq2a", "dqq3a"]:
        SEM_ALIAS[_n] = "dqq"
    for _n in ["drd2a", "drd3a", "drd3f"]:
        SEM_ALIAS[_n] = "drd"
    for _n in ["dxl3a"]:
        SEM_ALIAS[_n] = "dxl"
    for _n in ["dkk2a0", "dkk3a0"]:
        SEM_ALIAS[_n] = "dkk0"
    for _n in ["dkk2a1", "dkk3a1"]:
        SEM_ALIAS[_n] = "dkk1"
    for _n in ["dkk3a2"]:
        SEM_ALIAS[_n] = "dkk2"
    for _n in ["dvv2a0", "dvv3a0", "dvv3f0"]:
        SEM_ALIAS[_n] = "dvv0"
    for _n in ["dvv2a1", "dvv3a1", "dvv3f1"]:
        SEM_ALIAS[_n] = "dvv1"
    for _n in ["dvv3a2"]:
        SEM_ALIAS[_n] = "dvv2"
    for _n in ["dkv3f"]:
        SEM_ALIAS[_n] = "dkv"
    for _n in ["dq3f"]:
        SEM_ALIAS[_n] = "dq3x"

    def sem(self, name):
        name = self.SEM_ALIAS.get(name, name)
        if name not in self.sems:
            h = self.estack.enter_context(self.nc.semaphore(name))
            self.sems[name] = Sem(h, name)
        return self.sems[name]

    def sb(self, name, shape, dtype):
        return self.estack.enter_context(self.nc.sbuf_tensor(name, list(shape), dtype))

    def scope(self):
        return contextlib.ExitStack()

    def sb_in(self, scope, name, shape, dtype):
        return scope.enter_context(self.nc.sbuf_tensor(name, list(shape), dtype))

    def ps_in(self, scope, name, shape, dtype=F32):
        return scope.enter_context(self.nc.psum_tensor(name, list(shape), dtype))

    def dram(self, name, shape, dtype, kind="Internal"):
        if name not in self.dram_cache:
            self.dram_cache[name] = self.nc.dram_tensor(
                name, list(shape), dtype, kind=kind).ap()
        return self.dram_cache[name]

    def fresh(self, prefix):
        self.uid += 1
        return f"{prefix}_{self.uid}"

    def emit(self, engine, fn):
        self.ops.append((engine, [], fn))

    def op(self, engine, name, *args, waits=(), inc=None, inc_k=1, **kwargs):
        if inc is not None:
            inc.n += inc_k
        wl = [(s, (s.n if v is None else v)) for (s, v) in waits]

        def fn(e):
            ins = getattr(e, name)(*args, **kwargs)
            if inc is not None:
                ins.then_inc(inc.h, inc_k)
        self.emit2(engine, wl, fn)
        return inc.n if inc is not None else None

    def emit2(self, engine, wl, fn):
        self.ops.append((engine, wl, fn))

    def wait(self, engine, sem, val=None):
        v = sem.n if val is None else val
        self.emit2(engine, [(sem, v)], None)

    def dma(self, queue, out, in_, waits=(), inc=None, **kw):
        return self.op(queue, "dma_start", waits=waits, inc=inc, inc_k=16,
                       out=out, in_=in_, **kw)

    def barrier(self):
        targets = []
        for eng in ENGINES:
            s = self.sem("bar_" + eng)
            s.n += 1
            targets.append((eng, s, s.n))

            def fn(e, s=s):
                e.drain().then_inc(s.h, 1)
            self.emit(eng, fn)
        for eng in ENGINES:
            wl = [(s, v) for oeng, s, v in targets if oeng != eng]
            self.emit2(eng, wl, None)

    def memset(self, engine, ap, val, waits=()):
        s = self.sem("ms_" + engine)
        s.n += 1
        myv = s.n
        wl = [(ss, (ss.n if v is None else v)) for (ss, v) in waits]

        def fn(e):
            e.memset(ap, val).then_inc(s.h, 1)
            e.wait_ge(s.h, myv)
        self.emit2(engine, wl, fn)

    def bounce_bcast(self, row_src_ap, ncols, out_tile_ap, nparts,
                     waits=(), inc=None, dtype=F32, waits2=()):
        nm = self.fresh("bnc")
        dr = self.dram(nm, [ncols], dtype)
        s = self.sem("bncw")
        self.dma("sync", AP(dr.tensor, 0, [[ncols, 1], [1, ncols]]), row_src_ap, waits=waits, inc=s)
        src = AP(dr.tensor, 0, [[0, nparts], [1, ncols]])
        self.dma("sync", out_tile_ap, src,
                 waits=[(s, s.n)] + list(waits2), inc=inc)

    def x_slice_src(self, t0, t1):
        return self.xdr.rearrange("(dc p) t -> p dc t", p=128)[:, :, t0:t1]

    def load_x(self, h, t0, t1, waits=(), inc=None, queue="sync"):
        dst = h[:].rearrange("p (dc t) -> p dc t", dc=ND)
        self.dma(queue, dst, self.x_slice_src(t0, t1), waits=waits, inc=inc)

    def store_x(self, h, t0, t1, waits=(), inc=None, queue="sync"):
        src = h[:].rearrange("p (dc t) -> p dc t", dc=ND)
        self.dma(queue, self.x_slice_src(t0, t1), src, waits=waits, inc=inc)


def _replay(ctx, e, engine):
    seen = {}
    for eng, wl, fn in ctx.ops:
        if eng != engine:
            continue
        for s, v in wl:
            if v > 0 and v > seen.get(s.name, 0):
                e.wait_ge(s.h, v)
                seen[s.name] = v
        if fn is not None:
            fn(e)


def build_block_program(ctx):
    nc = ctx.nc
    with nc.Block() as block:
        @block.sync
        def _(e):
            _replay(ctx, e, "sync")

        @block.vector
        def _(e):
            _replay(ctx, e, "vector")

        @block.scalar
        def _(e):
            _replay(ctx, e, "scalar")

        @block.tensor
        def _(e):
            _replay(ctx, e, "tensor")

        @block.gpsimd
        def _(e):
            _replay(ctx, e, "gpsimd")



BLOCK_META = {}


def load_w_tiles(c, scope, w_ap, kin, cols_lo, cols_hi, tag, queue="gpsimd",
                 waits=(), inc=None, dtype=BF16, handle=None, row_lo=0):
    w = cols_hi - cols_lo
    kc = kin // 128
    h = handle
    if h is None:
        h = c.sb_in(scope, c.fresh(tag), [128, kc * w], dtype)
    name = w_ap.tensor.name
    if name in BLOCK_META:
        bkin, mblk, ncols = BLOCK_META[name]
        assert bkin == kin and w <= mblk and cols_lo % mblk == 0, (name, kin, cols_lo, w)
        nmb = ncols // mblk
        idx = (row_lo // kin) * nmb + cols_lo // mblk
        if name in c.wbf:
            w_ap = c.wbf[name]
            queue = "sync"
            g = c.wbf_gate.get(name)
            if g is not None:
                waits = list(waits) + [g]
        src = w_ap[idx]
        if w < mblk:
            src = w_ap[idx].rearrange("p (kc m) -> p kc m", kc=kc)[:, :, 0:w]
            dst = h[:, 0:kc * w].rearrange("p (kc m) -> p kc m", kc=kc)
        else:
            dst = h[:, 0:kc * w]
        c.dma(queue, dst, src, waits=waits, inc=inc)
    else:
        src = w_ap[row_lo:row_lo + kin, cols_lo:cols_hi] \
            .rearrange("(kc kp) m -> kp kc m", kp=128)
        dst = h[:, 0:kc * w].rearrange("p (kc m) -> p kc m", kc=kc)
        c.dma(queue, dst, src, waits=waits, inc=inc)
    return h, kc, w


def load_row(c, scope, vec_ap, n, tag, dtype=F32, queue="sync", waits=(), inc=None):
    h = c.sb_in(scope, c.fresh(tag), [1, n], dtype)
    if queue == "sync" and dtype != F32:
        queue = "gpsimd"
    c.dma(queue, h[:], AP(vec_ap.tensor, vec_ap.offset, [[n, 1], [1, n]]), waits=waits, inc=inc)
    return h


def load_mat(c, scope, mat_ap, rows, cols, tag, dtype=F32, queue="sync",
             waits=(), inc=None):
    h = c.sb_in(scope, c.fresh(tag), [rows, cols], dtype)
    if dtype != F32:
        queue = "gpsimd"
    c.dma(queue, h[:], mat_ap, waits=waits, inc=inc)
    return h



def ln_rows(c, scope, tag, src_override=None, eps=1e-5, nd=ND, scale_dim=None):
    sv, sa, sp = c.sem("sv"), c.sem("sa"), c.sem("sp")
    dxs = c.sem("dxs")
    dim = scale_dim or (nd * 128)
    rows = c.sb_in(scope, c.fresh(f"lnrow{tag}"), [1, 2 * T], F32)
    inner = scope.enter_context(c.scope()) if False else c.scope()
    scr = c.sb_in(inner, c.fresh(f"lnscr{tag}"), [1, 2 * T], F32)
    xsl = c.sb_in(inner, c.fresh(f"lnx{tag}"), [128, 2 * nd * TB], F32)
    xb = c.sb_in(inner, c.fresh(f"lnxb{tag}"), [128, 2 * nd * TB], BF16)
    xq = c.sb_in(inner, c.fresh(f"lnxq{tag}"), [128, 2 * nd * TB], BF16)
    pstat = c.ps_in(inner, c.fresh(f"lnps{tag}"), [1, 4 * TB], F32)
    ones_col = c.consts["ones_col"]

    cast_done = [None, None]
    cast_guard = [0, 0]
    psum_guard = [0, 0]
    for tb in range(NT):
        par = tb % 2
        xsl_v = xsl[:, par * nd * TB:(par + 1) * nd * TB]
        xb_v = xb[:, par * nd * TB:(par + 1) * nd * TB]
        xq_v = xq[:, par * nd * TB:(par + 1) * nd * TB]
        ps_s = pstat[0:1, par * 2 * TB: par * 2 * TB + TB]
        ps_q = pstat[0:1, par * 2 * TB + TB: par * 2 * TB + 2 * TB]
        wl = []
        if cast_done[par]:
            wl = [(sv, cast_done[par][0]), (sa, cast_done[par][1])]
        if src_override is None:
            dst = xsl_v.rearrange("p (dc t) -> p dc t", dc=nd)
            c.dma("sync", dst, c.x_slice_src(tb * TB, (tb + 1) * TB),
                  waits=wl, inc=dxs)
        else:
            src_override(xsl_v, tb, wl, dxs)
        lded = dxs.n
        wcast = [(dxs, lded)]
        if cast_guard[par]:
            wcast.append((sp, cast_guard[par]))
        c.op("vector", "tensor_copy", xb_v, xsl_v, waits=wcast, inc=sv)
        c.op("scalar", "activation", xq_v, xsl_v, AF.Square, waits=wcast, inc=sa)
        cast_done[par] = (sv.n, sa.n)
        v_done, a_done = sv.n, sa.n
        wmm = [(sv, v_done), (sa, a_done)]
        if psum_guard[par]:
            wmm.append((sv, psum_guard[par]))
        for d in range(nd):
            c.op("tensor", "matmul", ps_s, ones_col[:, 0:1],
                 xb_v.rearrange("p (dc t) -> p dc t", dc=nd)[:, d],
                 start=(d == 0), stop=(d == nd - 1),
                 waits=(wmm if d == 0 else ()), inc=sp)
        for d in range(nd):
            c.op("tensor", "matmul", ps_q, ones_col[:, 0:1],
                 xq_v.rearrange("p (dc t) -> p dc t", dc=nd)[:, d],
                 start=(d == 0), stop=(d == nd - 1), inc=sp)
        cast_guard[par] = sp.n
        pe_mm = sp.n
        a_seg = rows[0:1, tb * TB: tb * TB + TB]
        b_seg = rows[0:1, T + tb * TB: T + tb * TB + TB]
        m_seg = scr[0:1, tb * TB: tb * TB + TB]
        v_seg = scr[0:1, T + tb * TB: T + tb * TB + TB]
        c.op("vector", "tensor_scalar_mul", m_seg, ps_s, 1.0 / dim,
             waits=[(sp, pe_mm)], inc=sv)
        c.op("vector", "tensor_scalar_mul", v_seg, ps_q, 1.0 / dim, inc=sv)
        psum_guard[par] = sv.n
        c.op("vector", "tensor_tensor", b_seg, m_seg, m_seg, ALU.mult, inc=sv)
        c.op("vector", "tensor_sub", v_seg, v_seg, b_seg, inc=sv)
        c.op("scalar", "activation", b_seg, v_seg, AF.Sqrt, bias=float(eps),
             waits=[(sv, sv.n)], inc=sa)
        c.op("vector", "reciprocal", a_seg, b_seg, waits=[(sa, sa.n)], inc=sv)
        c.op("vector", "tensor_tensor", b_seg, m_seg, a_seg, ALU.mult, inc=sv)
    c.barrier()
    inner.close()
    return rows, sv.n



def emit_xn_slice(c, xn_v, xsl_v, ab_bc, t0, width, waits=(), inc=None, nd=ND):
    sv = c.sem("sv")
    a3 = ab_bc[:, t0:t0 + width].unsqueeze(1).broadcast_to([128, nd, width])
    b3 = ab_bc[:, T + t0:T + t0 + width].unsqueeze(1).broadcast_to([128, nd, width])
    x3 = xsl_v.rearrange("p (dc t) -> p dc t", dc=nd)
    n3 = xn_v.rearrange("p (dc t) -> p dc t", dc=nd)
    c.op("vector", "tensor_tensor", n3, x3, a3, ALU.mult, waits=waits, inc=sv)
    c.op("vector", "tensor_tensor", n3, n3, b3, ALU.subtract, inc=inc or sv)



def proj(c, scope, rhs_at, w_ap, kin, m_lo, m_hi, nt_blocks, evac, tag,
         mblk=512, extra_k=None, evac_sem=None, w_row_lo=0, first_waits=(),
         pp=None, wbufs=None, wl_sems=None):
    sp = c.sem("sp")
    sp_entry = sp.n
    if evac_sem is None:
        evac_sem = c.sem("se_" + tag[:4])
    if pp is None:
        pp = c.ps_in(scope, c.fresh(f"pp{tag}"), [128, 4 * TB], F32)
    npar = pp.shape[1] // TB
    nmb = (m_hi - m_lo + mblk - 1) // mblk
    kc = kin // 128
    wl_sem = wl_sems or [c.sem("wl_" + tag[:4] + "0"), c.sem("wl_" + tag[:4] + "1")]
    wbuf = wbufs or [c.sb_in(scope, c.fresh(f"wb{tag}"), [128, kc * mblk], BF16)
                     for _ in range(2)]
    free_at = [sp_entry, sp_entry]
    loaded = [0, 0]
    ev_hist = {}
    gi = 0
    fw = list(first_waits)
    for mb in range(nmb):
        slot = mb % 2
        lo = m_lo + mb * mblk
        hi = min(m_lo + (mb + 1) * mblk, m_hi)
        load_w_tiles(c, scope, w_ap, kin, lo, hi, tag, queue="gpsimd",
                     waits=([(sp, free_at[slot])] if free_at[slot] else ())
                     + (list(first_waits) if mb < 2 else []),
                     inc=wl_sem[slot], handle=wbuf[slot], row_lo=w_row_lo)
        loaded[slot] = wl_sem[slot].n
        wcols = hi - lo
        nmi = (wcols + 127) // 128
        groups = [(mi, tb) for mi in range(nmi) for tb in range(nt_blocks)]
        batches = []
        for mi in range(nmi):
            for t0_ in range(0, nt_blocks, npar):
                batches.append([(mi, tb) for tb in
                                range(t0_, min(t0_ + npar, nt_blocks))])
        for batch in batches:
            gis = {}
            for gmi in batch:
                gis[gmi] = gi
                gi += 1
            for k in range(kc):
                for (mi, tb) in batch:
                    g = gis[(mi, tb)]
                    mw = min(128, wcols - mi * 128)
                    par = g % npar
                    ps = pp[0:mw, par * TB:(par + 1) * TB]
                    w0 = []
                    if k == 0:
                        w0 = [(wl_sem[slot], loaded[slot])] + fw
                        fw = []
                        if g - npar in ev_hist:
                            w0.append((evac_sem, ev_hist[g - npar]))
                    c.op("tensor", "matmul", ps,
                         wbuf[slot][:, k * wcols + mi * 128:
                                    k * wcols + mi * 128 + mw],
                         rhs_at(k, tb),
                         start=(k == 0),
                         stop=(k == kc - 1 and extra_k is None),
                         waits=w0, inc=sp)
            for (mi, tb) in batch:
                g = gis[(mi, tb)]
                mw = min(128, wcols - mi * 128)
                mi_abs = lo + mi * 128
                par = g % npar
                ps = pp[0:mw, par * TB:(par + 1) * TB]
                if extra_k is not None:
                    lr, rr = extra_k
                    c.op("tensor", "matmul", ps, lr(mi_abs)[:, 0:mw],
                         rr(tb), start=False, stop=True, inc=sp)
                mmc = sp.n
                evac(mi_abs, tb, ps, mmc)
                ev_hist[g] = evac_sem.n
        free_at[slot] = sp.n



def st_l1f(c, W):
    TCH = 512
    NCH = T // TCH
    NB = TCH // TB
    sv, sa, sp = c.sem("sv"), c.sem("sa"), c.sem("sp")
    dg, dxs, dst_ = c.sem("dg"), c.sem("dxs2"), c.sem("dst")
    with c.scope() as sc:
        rows, rows_done = ln_rows(c, sc, "l1f")
        ab_bc = c.sb_in(sc, c.fresh("abbc"), [128, 2 * T], F32)
        dwb = c.sem("dwb")
        c.bounce_bcast(rows[0:1, :], 2 * T, ab_bc[:], 128,
                       waits=[(sv, rows_done)], inc=dwb)
        bc_done = dwb.n
        b1c = load_mat(c, sc, W["L1_ffn_b1c"], 128, 4096 // 128, "b1c", inc=dg)
        b2r = load_row(c, sc, W["L1_ffn_b2"], D, "b2r", dtype=BF16, inc=dg)
        bias_done = dg.n
        xsl = c.sb_in(sc, c.fresh("xsl"), [128, ND * TCH], F32)
        xn = c.sb_in(sc, c.fresh("xn"), [128, ND * TCH], BF16)
        h = c.sb_in(sc, c.fresh("h"), [128, 32 * TCH], BF16)
        seh = c.sem("seh")
        seo = c.sem("seo")
        ones_row = c.consts["ones_row"]
        pp = c.ps_in(sc, c.fresh("ppl1f"), [128, 4 * TB], F32)
        wbufs = [c.sb_in(sc, c.fresh("wbl1f"), [128, 4096], BF16) for _ in range(2)]
        wl_sems = [c.sem("wl_l1f0"), c.sem("wl_l1f1")]

        for ch in range(NCH):
            t0 = ch * TCH
            wl = [(dst_, dst_.n)] if ch else []
            c.load_x(xsl, t0, t0 + TCH, waits=wl, inc=dxs)
            emit_xn_slice(c, xn[:], xsl[:], ab_bc, t0, TCH,
                          waits=[(dxs, dxs.n), (dwb, bc_done)])
            xn_done = sv.n

            def rhs_xn(k, tb):
                return xn[:, k * TCH + tb * TB: k * TCH + (tb + 1) * TB]

            def evac_h(mi, tb, ps, mmc):
                c.op("scalar", "activation",
                     h[:, (mi // 128) * TCH + tb * TB:(mi // 128) * TCH + (tb + 1) * TB],
                     ps, AF.Gelu, bias=b1c[:, mi // 128: mi // 128 + 1],
                     waits=[(sp, mmc), (dg, bias_done)], inc=seh)

            proj(c, sc, rhs_xn, W["L1_ffn_w1"], D, 0, 4096, NB, evac_h,
                 "w1", evac_sem=seh, first_waits=[(sv, xn_done), (seo, seo.n)],
                 pp=pp, wbufs=wbufs, wl_sems=wl_sems)

            def rhs_h(k, tb):
                return h[:, k * TCH + tb * TB: k * TCH + (tb + 1) * TB]

            def evac_o(mi, tb, ps, mmc):
                d = mi // 128
                sl = slice(d * TCH + tb * TB, d * TCH + (tb + 1) * TB)
                c.op("vector", "scalar_tensor_tensor", xsl[:, sl], ps, 0.5,
                     xsl[:, sl], ALU.mult, ALU.add, waits=[(sp, mmc)], inc=seo)

            proj(c, sc, rhs_h, W["L1_ffn_w2"], 4096, 0, D, NB, evac_o,
                 "w2", mblk=128, evac_sem=seo, first_waits=[(seh, seh.n)],
                 pp=pp, wbufs=wbufs, wl_sems=wl_sems,
                 extra_k=(lambda mi: b2r[0:1, mi: mi + 128],
                          lambda tb: ones_row[0:1, 0:TB]))
            c.store_x(xsl, t0, t0 + TCH, waits=[(seo, seo.n)], inc=dst_)
        c.wait("sync", dst_, dst_.n)
        c.barrier()



def st_l3a(c, W):
    TCH = 512
    NCH = T // TCH
    sv, sa, sp = c.sem("sv"), c.sem("sa"), c.sem("sp")
    dxs, dst_, dqs = c.sem("dxs2"), c.sem("dst"), c.sem("dqs")
    dkv = c.sem("dkv")
    qd = c.dram(c.fresh("qd"), [D, T], BF16)
    kd = c.dram(c.fresh("kd"), [D, T], BF16)
    vd = c.dram(c.fresh("vd"), [T, D], BF16)
    with c.scope() as sc:
        rows, rows_done = ln_rows(c, sc, "l3a")
        ab_bc = c.sb_in(sc, c.fresh("abbc"), [128, 2 * T], F32)
        dwb = c.sem("dwb")
        c.bounce_bcast(rows[0:1, :], 2 * T, ab_bc[:], 128,
                       waits=[(sv, rows_done)], inc=dwb)
        bc_done = dwb.n
        with c.scope() as p1:
            xsl = c.sb_in(p1, c.fresh("xsl"), [128, ND * TCH], F32)
            xn = c.sb_in(p1, c.fresh("xn"), [128, ND * TCH], BF16)
            cosb = c.sb_in(p1, c.fresh("cosb"), [128, ND * TCH], F32)
            sinb = c.sb_in(p1, c.fresh("sinb"), [128, ND * TCH], F32)
            qtmp = c.sb_in(p1, c.fresh("qtmp"), [128, TCH], F32)
            qsh = c.sb_in(p1, c.fresh("qsh"), [128, TCH], F32)
            qacc = c.sb_in(p1, c.fresh("qacc"), [128, ND * TCH], BF16)
            vtmp = c.sb_in(p1, c.fresh("vtmp"), [128, (TCH // 128) * D], BF16)
            pp = c.ps_in(p1, c.fresh("ppl3a"), [128, 2 * TB], F32)
            wbufs = [c.sb_in(p1, c.fresh("wbl3a"), [128, 8 * 512], BF16)
                     for _ in range(2)]
            wl_sems = [c.sem("wl_l3a0"), c.sem("wl_l3a1")]
            swap_mask = []
            for i in range(0, 32, 2):
                swap_mask += [i + 1, i]
            seq = c.sem("seq3a")
            for ch in range(NCH):
                t0 = ch * TCH
                c.load_x(xsl, t0, t0 + TCH,
                         waits=[(seq, seq.n), (dqs, dqs.n)], inc=dxs)
                emit_xn_slice(c, xn[:], xsl[:], ab_bc, t0, TCH,
                              waits=[(dxs, dxs.n), (dwb, bc_done), (sp, sp.n)])
                xn_done = sv.n
                csrc = W["c_cos_l3a"].rearrange("(dc p) t -> p dc t", p=128)
                ssrc = W["c_ssin_l3a"].rearrange("(dc p) t -> p dc t", p=128)
                c.dma("sync", cosb[:].rearrange("p (dc t) -> p dc t", dc=ND),
                      csrc[:, :, t0:t0 + TCH], waits=[(seq, seq.n)], inc=dxs)
                c.dma("sync", sinb[:].rearrange("p (dc t) -> p dc t", dc=ND),
                      ssrc[:, :, t0:t0 + TCH], inc=dxs)
                tab_done = dxs.n

                for which, wname, dst in (
                        (0, "L3_attn_wq", qd), (1, "L3_attn_wk", kd)):
                    def evac_rope(mi, tb, ps, mmc, which=which):
                        m = mi // 128
                        c.op("vector", "tensor_copy", qtmp[:], ps,
                             waits=[(sp, mmc), (dxs, tab_done),
                                    (dqs, dqs.n)], inc=sv)
                        c.op("vector", "stream_shuffle", qsh[:], qtmp[:],
                             swap_mask, inc=sv)
                        cs = cosb[:, m * TCH:(m + 1) * TCH]
                        sn = sinb[:, m * TCH:(m + 1) * TCH]
                        qa = qacc[:, m * TCH:(m + 1) * TCH]
                        c.op("vector", "tensor_tensor", qa, qtmp[:], cs,
                             ALU.mult, inc=sv)
                        c.op("vector", "tensor_tensor", qsh[:], qsh[:], sn,
                             ALU.mult, inc=sv)
                        c.op("vector", "tensor_tensor", qa, qa, qsh[:],
                             ALU.add, inc=seq)

                    sev = c.sem("sev3a")
                    proj(c, sc, lambda k, tb: xn[:, k * TCH:(k + 1) * TCH],
                         W[wname], D, 0, D, 1, evac_rope, "l3aqk",
                         evac_sem=seq,
                         first_waits=[(sv, xn_done), (seq, seq.n), (sev, sev.n)],
                         pp=pp, wbufs=wbufs, wl_sems=wl_sems, mblk=512)
                    c.dma("sync",
                          dst.rearrange("(dc p) t -> p dc t", p=128)[:, :, t0:t0 + TCH],
                          qacc[:].rearrange("p (dc t) -> p dc t", dc=ND),
                          waits=[(seq, seq.n)], inc=dqs)
                sev = c.sem("sev3a")
                ntt = TCH // 128
                load_w_tiles(c, sc, W["L3_attn_wv"], D, 0, 512, "wv",
                             waits=[(sp, sp.n)], inc=wl_sems[0], handle=wbufs[0])
                load_w_tiles(c, sc, W["L3_attn_wv"], D, 512, 1024, "wv",
                             waits=[(sp, sp.n)], inc=wl_sems[1], handle=wbufs[1])
                ev_hist = [seq.n and 0, 0]
                ev_hist = {}
                g = 0
                for nb in range(2):
                    for tt in range(ntt):
                        par = g % 2
                        ps = pp[:, par * TB: par * TB + TB]
                        w0 = [(wl_sems[nb], wl_sems[nb].n), (seq, seq.n)]
                        if g >= 2:
                            w0.append((sev, ev_hist[g - 2]))
                        for k in range(ND):
                            lhsT = xn[:, k * TCH + tt * 128: k * TCH + tt * 128 + 128]
                            c.op("tensor", "matmul", ps,
                                 lhsT, wbufs[nb][:, k * 512:(k + 1) * 512],
                                 start=(k == 0), stop=(k == ND - 1),
                                 waits=(w0 if k == 0 else ()), inc=sp)
                        c.op("vector", "tensor_copy",
                             vtmp[:, tt * D + nb * 512: tt * D + nb * 512 + 512],
                             ps, waits=[(sp, sp.n), (dkv, dkv.n)], inc=sev)
                        ev_hist[g] = sev.n
                        g += 1
                c.dma("sync",
                      vd[t0:t0 + TCH, :].rearrange("(tt p) dd -> p tt dd", p=128),
                      vtmp[:].rearrange("p (tt dd) -> p tt dd", tt=ntt),
                      waits=[(sev, sev.n)], inc=dkv)
            c.wait("sync", dqs, dqs.n)
            c.wait("sync", dkv, dkv.n)
            c.barrier()
        import os
        if os.environ.get("L3A_PH") == "1":
            c.barrier()
            return
        with c.scope() as p2:
            TQ = 512
            NKMAX = T // 128
            qbuf = c.sb_in(p2, c.fresh("qbuf"), [128, ND * TQ], BF16)
            kbuf = c.sb_in(p2, c.fresh("kbuf"), [128, 3 * ND * 128], BF16)
            vbuf = c.sb_in(p2, c.fresh("vbuf"), [128, 3 * D], BF16)
            probs = c.sb_in(p2, c.fresh("probs"), [128, (T // 128) * TQ], BF16)
            rden = c.sb_in(p2, c.fresh("rden"), [1, TQ], F32)
            rdbc = c.sb_in(p2, c.fresh("rdbc"), [128, TQ], F32)
            yv = c.sb_in(p2, c.fresh("yv"), [128, TQ], F32)
            xsl2 = c.sb_in(p2, c.fresh("xsl2"), [128, ND * TQ], F32)
            trim = c.sb_in(p2, c.fresh("trim"), [128, 4 * TQ], BF16)
            ps_y = c.ps_in(p2, c.fresh("psy"), [128, 4 * TQ], F32)
            ps_sc = c.ps_in(p2, c.fresh("pssc"), [128, 2 * TQ], F32)
            ps_dn = c.ps_in(p2, c.fresh("psdn"), [1, TQ], F32)
            c.dma("gpsimd", trim[:], W["c_tri512"], inc=dkv)
            tri_done = dkv.n
            dqq = c.sem("dqq3a")
            dkk = [c.sem(f"dkk3a{i}") for i in range(3)]
            dvv = [c.sem(f"dvv3a{i}") for i in range(3)]
            drd = c.sem("drd3a")
            dxl = c.sem("dxl3a")
            ones_col = c.consts["ones_col"]
            prev_yv3a = [0]
            for tqb in range(T // TQ):
                q0 = tqb * TQ
                ntk = 4 * tqb + 4
                c.dma("sync",
                      qbuf[:].rearrange("p (dc t) -> p dc t", dc=ND),
                      qd.rearrange("(dc p) t -> p dc t", p=128)[:, :, q0:q0 + TQ],
                      waits=[(sp, sp.n)], inc=dqq)
                qld = dqq.n
                rd_done = xld = 0
                for half in range(2):
                    mlo = half * 4
                    kslot_free = {}
                    vslot_free = {}
                    exp_d = {}
                    pw_d = {}
                    qk_d = {}
                    kvld_m = {}

                    def emit_ld3a(tkb):
                        t0 = tkb * 128
                        vslot = tkb % 3
                        kslot = tkb % 3
                        wk = [(dqq, qld)]
                        if tkb - 3 in vslot_free:
                            wk.append((sp, vslot_free[tkb - 3]))
                        elif half == 1 and tkb < 3:
                            wk.append((sp, sp.n))
                        c.dma("sync", vbuf[:, vslot * D:(vslot + 1) * D],
                              vd[t0:t0 + 128, :], waits=wk, inc=dvv[vslot])
                        if half == 0:
                            kw = []
                            if tkb - 3 in kslot_free:
                                kw.append((sp, kslot_free[tkb - 3]))
                            elif tkb < 3:
                                kw.append((sp, sp.n))
                            c.dma("sync",
                                  kbuf[:, kslot * ND * 128:
                                       (kslot + 1) * ND * 128]
                                  .rearrange("p (dc t) -> p dc t", dc=ND),
                                  kd.rearrange("(dc p) t -> p dc t",
                                               p=128)[:, :, t0:t0 + 128],
                                  waits=kw, inc=dkk[kslot])
                        kvld_m[tkb] = (dvv[vslot].n, dkk[kslot].n)

                    def emit_qk3a(tkb):
                        t0 = tkb * 128
                        kslot = tkb % 3
                        vslot = tkb % 3
                        spar = tkb % 2
                        scs = ps_sc[:, spar * TQ:(spar + 1) * TQ]
                        for k in range(ND):
                            c.op("tensor", "matmul", scs,
                                 kbuf[:, kslot * ND * 128 + k * 128:
                                      kslot * ND * 128 + (k + 1) * 128],
                                 qbuf[:, k * TQ:(k + 1) * TQ],
                                 start=(k == 0), stop=(k == ND - 1),
                                 waits=([(dkk[kslot], kvld_m[tkb][1]),
                                         (dvv[vslot], kvld_m[tkb][0]),
                                         (sa, exp_d.get(tkb - 2, 0))]
                                        if k == 0 else ()), inc=sp)
                        qk_d[tkb] = sp.n
                        pslice = probs[:, tkb * TQ:(tkb + 1) * TQ]
                        c.op("scalar", "activation", pslice, scs,
                             AF.Exp, scale=1.0 / 32.0,
                             waits=[(sp, qk_d[tkb]), (dkv, tri_done)], inc=sa)
                        exp_d[tkb] = sa.n
                        if t0 + 128 > q0:
                            moff = (t0 - q0) // 128 * TQ
                            c.op("vector", "tensor_tensor", pslice,
                                 pslice, trim[:, moff:moff + TQ], ALU.mult,
                                 waits=[(sa, exp_d[tkb])], inc=sv)
                            pw_d[tkb] = (sv, sv.n)
                        else:
                            pw_d[tkb] = (sa, exp_d[tkb])

                    def emit_pv3a(tkb):
                        vslot = tkb % 3
                        for mi, m in enumerate(range(mlo, mlo + 4)):
                            w_av = []
                            if mi == 0:
                                w_av.append((dvv[vslot], kvld_m[tkb][0]))
                                if half == 0:
                                    w_av.append(pw_d[tkb])
                                if tkb == 0:
                                    w_av.append((sv, sv.n))
                            c.op("tensor", "matmul",
                                 ps_y[:, mi * TQ:(mi + 1) * TQ],
                                 vbuf[:, vslot * D + m * 128:
                                      vslot * D + (m + 1) * 128],
                                 probs[:, tkb * TQ:(tkb + 1) * TQ],
                                 start=(tkb == 0), stop=(tkb == ntk - 1),
                                 waits=w_av, inc=sp)
                        if half == 0:
                            c.op("tensor", "matmul", ps_dn[:],
                                 ones_col[:, 0:1],
                                 probs[:, tkb * TQ:(tkb + 1) * TQ],
                                 start=(tkb == 0), stop=(tkb == ntk - 1),
                                 inc=sp)
                        kslot_free[tkb] = sp.n
                        vslot_free[tkb] = sp.n

                    if half == 0:
                        for t_ in range(ntk + 1):
                            if t_ < ntk:
                                emit_ld3a(t_)
                                emit_qk3a(t_)
                            if t_ >= 1:
                                emit_pv3a(t_ - 1)
                    else:
                        for t_ in range(ntk):
                            emit_ld3a(t_)
                            emit_pv3a(t_)
                    av_done = sp.n
                    if half == 0:
                        c.op("vector", "reciprocal", rden[:], ps_dn[:],
                             waits=[(sp, av_done), (drd, drd.n)], inc=sv)
                        c.bounce_bcast(rden[:], TQ, rdbc[:], 128,
                                       waits=[(sv, sv.n)],
                                       waits2=[(sv, prev_yv3a[0])], inc=drd)
                        rd_done = drd.n
                        c.load_x(xsl2, q0, q0 + TQ,
                                 waits=[(dst_, dst_.n)], inc=dxl)
                        xld = dxl.n
                    for mi, m in enumerate(range(mlo, mlo + 4)):
                        c.op("vector", "tensor_tensor", yv[:],
                             ps_y[:, mi * TQ:(mi + 1) * TQ], rdbc[:], ALU.mult,
                             waits=[(drd, rd_done), (sp, av_done),
                                    (dxl, xld)], inc=sv)
                        prev_yv3a[0] = sv.n
                        c.op("vector", "tensor_tensor",
                             xsl2[:, m * TQ:(m + 1) * TQ],
                             xsl2[:, m * TQ:(m + 1) * TQ], yv[:], ALU.add,
                             inc=sv)
                c.store_x(xsl2, q0, q0 + TQ, waits=[(sv, sv.n)], inc=dst_)
            c.wait("sync", dst_, dst_.n)
            c.barrier()



def st_l2a(c, W):
    import os
    dbg = os.environ.get("L2A_DBG") == "1"
    kind = "ExternalOutput" if dbg else "Internal"
    TCH = 512
    NCH = T // TCH
    sv, sa, sp = c.sem("sv"), c.sem("sa"), c.sem("sp")
    dxs, dst_, dqs = c.sem("dxs2"), c.sem("dst"), c.sem("dqs")
    qd = c.dram("dbg_qd2", [D, T], BF16, kind=kind)
    kd = c.dram("dbg_kd2", [D, T], BF16, kind=kind)
    vd = c.dram("dbg_vd2", [T, D], BF16, kind=kind)
    yd = c.dram("dbg_yd2", [D, T], BF16, kind=kind)
    c2d = c.dram("dbg_c2", [128, 128], BF16, kind=kind) if dbg else None
    dba = c.dram("dbg_a", [33, 1], F32, kind=kind) if dbg else None
    dbr = c.dram("dbg_ar", [1, 33], F32, kind=kind) if dbg else None
    dbf = c.dram("dbg_f", [64, 128], BF16, kind=kind) if dbg else None
    with c.scope() as sc:
        rows_sc = c.scope()
        rows, rows_done = ln_rows(c, rows_sc, "l2a")
        abd = c.dram(c.fresh("abd"), [2 * T], F32)
        dab = c.sem("dab")
        c.dma("sync", AP(abd.tensor, 0, [[2 * T, 1], [1, 2 * T]]), rows[0:1, :],
              waits=[(sv, rows_done)], inc=dab)
        c.wait("sync", dab, dab.n)
        c.barrier()
        rows_sc.close()
        C2 = c.sb_in(sc, c.fresh("C2"), [128, 128], BF16)
        with c.scope() as s0:
            mrow = load_row(c, s0, W["L2_attn_mask_logits"], 33, "ml", inc=dxs)
            wf = load_row(c, s0, W["c_fftw"], 33, "wf", inc=dxs)
            Fc = load_mat(c, s0, W["c_fftc"], 33, 64, "Fc", inc=dxs)
            Fs = load_mat(c, s0, W["c_ffts"], 33, 64, "Fs", inc=dxs)
            Fcb = c.sb_in(s0, c.fresh("Fcb"), [64, 64], BF16)
            Fsb = c.sb_in(s0, c.fresh("Fsb"), [64, 64], BF16)
            Fcs = c.sb_in(s0, c.fresh("Fcs"), [64, 64], BF16)
            Fss = c.sb_in(s0, c.fresh("Fss"), [64, 64], BF16)
            for t_ in (Fcb, Fsb, Fcs, Fss):
                c.memset("vector", t_[:], 0.0)
            arow = c.sb_in(s0, c.fresh("arow"), [1, 33], F32)
            arowb = c.sb_in(s0, c.fresh("arowb"), [1, 128], BF16)
            onect = c.sb_in(s0, c.fresh("one1"), [1, 1], BF16)
            acol = c.sb_in(s0, c.fresh("acol"), [33, 1], F32)
            c64 = c.sb_in(s0, c.fresh("c64"), [64, 64], BF16)
            psa = c.ps_in(s0, c.fresh("psa"), [128, 1], F32)
            psc = c.ps_in(s0, c.fresh("psc"), [64, 64], F32)
            ld0 = dxs.n
            c.memset("vector", onect[:], 1.0)
            c.op("scalar", "activation", arow[:], mrow[:], AF.Sigmoid,
                 waits=[(dxs, ld0)], inc=sa)
            c.op("vector", "tensor_tensor", arow[:], arow[:], wf[:], ALU.mult,
                 waits=[(sa, sa.n)], inc=sv)
            c.memset("vector", arowb[:], 0.0)
            c.op("vector", "tensor_copy", arowb[0:1, 0:33], arow[:], inc=sv)
            c.op("tensor", "matmul", psa[:], arowb[0:1, :], onect[0:1, :],
                 start=True, stop=True, waits=[(sv, sv.n)], inc=sp)
            c.op("vector", "tensor_copy", acol[:], psa[0:33, :],
                 waits=[(sp, sp.n)], inc=sv)
            c.op("vector", "tensor_copy", Fcb[0:33, :], Fc[:], waits=[(dxs, ld0)], inc=sv)
            c.op("vector", "tensor_copy", Fsb[0:33, :], Fs[:], inc=sv)
            acb = acol[:, 0:1].broadcast_to([33, 64])
            c.op("vector", "tensor_tensor", Fcs[0:33, :], Fc[:], acb,
                 ALU.mult, waits=[(sv, sv.n)], inc=sv)
            c.op("vector", "tensor_tensor", Fss[0:33, :], Fs[:], acb,
                 ALU.mult, inc=sv)
            c.op("tensor", "matmul", psc[:], Fcs[0:64, :], Fcb[0:64, :],
                 start=True, stop=False, waits=[(sv, sv.n)], inc=sp)
            c.op("tensor", "matmul", psc[:], Fss[0:64, :], Fsb[0:64, :],
                 start=False, stop=True, inc=sp)
            c.op("vector", "tensor_copy", c64[:], psc[:], waits=[(sp, sp.n)], inc=sv)
            c.memset("vector", C2[:], 0.0, waits=[(sv, sv.n)])
            c.op("vector", "tensor_copy", C2[0:64, 0:64], c64[:], inc=sv)
            c.op("vector", "tensor_copy", C2[64:128, 64:128], c64[:], inc=sv)
            c2_done = sv.n
            if dbg:
                c.dma("sync", c2d, C2[:], waits=[(sv, c2_done)], inc=dxs)
                c.dma("sync", dba, acol[:], waits=[(sv, c2_done)], inc=dxs)
                c.dma("sync", dbr, arow[:], waits=[(sv, c2_done)], inc=dxs)
                c.dma("sync", dbf[:, 0:64], Fcs[:], waits=[(sv, c2_done)], inc=dxs)
                c.dma("sync", dbf[:, 64:128], Fcb[:], waits=[(sv, c2_done)], inc=dxs)
            c.barrier()
        with c.scope() as p1:
            ab_bc = c.sb_in(p1, c.fresh("abbc"), [128, 2 * T], F32)
            dwb = c.sem("dwb")
            c.dma("sync", ab_bc[:], AP(abd.tensor, 0, [[0, 128], [1, 2 * T]]),
                  waits=[(dab, dab.n)], inc=dwb)
            bc_done = dwb.n
            xsl = c.sb_in(p1, c.fresh("xsl"), [128, ND * TCH], F32)
            xn = c.sb_in(p1, c.fresh("xn"), [128, ND * TCH], BF16)
            qraw = c.sb_in(p1, c.fresh("qraw"), [128, TCH], BF16)
            qacc = c.sb_in(p1, c.fresh("qacc"), [128, ND * TCH], BF16)
            vtmp = c.sb_in(p1, c.fresh("vtmp"), [128, (TCH // 128) * D], BF16)
            pp = c.ps_in(p1, c.fresh("ppl2a"), [128, 4 * TB], F32)
            ppf = c.ps_in(p1, c.fresh("ppf"), [128, 2 * TB], F32)
            wbufs = [c.sb_in(p1, c.fresh("wbl2a"), [128, 8 * 512], BF16)
                     for _ in range(2)]
            wl_sems = [c.sem("wl_l2a0"), c.sem("wl_l2a1")]
            seq = c.sem("seq2a")
            sef = c.sem("sef2a")
            sev = c.sem("sev2a")
            for ch in range(NCH):
                t0 = ch * TCH
                c.load_x(xsl, t0, t0 + TCH,
                         waits=[(seq, seq.n), (sev, sev.n)], inc=dxs)
                emit_xn_slice(c, xn[:], xsl[:], ab_bc, t0, TCH,
                              waits=[(dxs, dxs.n), (dwb, bc_done), (sp, sp.n)])
                xn_done = sv.n
                for which, lo, dst in ((0, 0, qd), (1, D, kd)):
                    def evac_filt(mi, tb, ps, mmc, which=which, lo=lo):
                        m = (mi - lo) // 128
                        c.op("vector", "tensor_copy", qraw[:], ps,
                             waits=[(sp, mmc), (sef, sef.n)], inc=sv)
                        cpd = sv.n
                        par = m % 2
                        fps = ppf[:, par * TB:(par + 1) * TB]
                        c.op("tensor", "matmul", fps, C2[:], qraw[:],
                             start=True, stop=True,
                             waits=[(sv, cpd), (sef, max(0, sef.n - 1))], inc=sp)
                        c.op("vector", "tensor_copy",
                             qacc[:, m * TCH:(m + 1) * TCH], fps,
                             waits=[(sp, sp.n)], inc=seq)
                        sef.n += 1
                        def _fn(e, s=sef):
                            e.sem_inc(s.h, 1)
                        c.emit("vector", _fn)

                    proj(c, sc, lambda k, tb: xn[:, k * TCH:(k + 1) * TCH],
                         W["L2_attn_w_qkv"], D, lo, lo + D, 1, evac_filt,
                         "l2aqk", evac_sem=seq,
                         first_waits=[(sv, xn_done), (seq, seq.n)],
                         pp=pp, wbufs=wbufs, wl_sems=wl_sems, mblk=512)
                    c.dma("sync",
                          dst.rearrange("(dc p) t -> p dc t", p=128)[:, :, t0:t0 + TCH],
                          qacc[:].rearrange("p (dc t) -> p dc t", dc=ND),
                          waits=[(seq, seq.n)], inc=dqs)
                ntt = TCH // 128
                load_w_tiles(c, sc, W["L2_attn_w_qkv"], D, 2 * D, 2 * D + 512,
                             "wv2", waits=[(sp, sp.n)], inc=wl_sems[0],
                             handle=wbufs[0])
                load_w_tiles(c, sc, W["L2_attn_w_qkv"], D, 2 * D + 512, 3 * D,
                             "wv2", waits=[(sp, sp.n)], inc=wl_sems[1],
                             handle=wbufs[1])
                ev_hist = {}
                g = 0
                for nb in range(2):
                    for tt in range(ntt):
                        par = g % 2
                        ps = pp[:, par * TB: par * TB + TB]
                        w0 = [(wl_sems[nb], wl_sems[nb].n), (seq, seq.n)]
                        if g >= 2:
                            w0.append((sev, ev_hist[g - 2]))
                        for k in range(ND):
                            lhsT = xn[:, k * TCH + tt * 128: k * TCH + tt * 128 + 128]
                            c.op("tensor", "matmul", ps,
                                 lhsT, wbufs[nb][:, k * 512:(k + 1) * 512],
                                 start=(k == 0), stop=(k == ND - 1),
                                 waits=(w0 if k == 0 else ()), inc=sp)
                        c.op("vector", "tensor_copy",
                             vtmp[:, tt * D + nb * 512: tt * D + nb * 512 + 512],
                             ps, waits=[(sp, sp.n), (dqs, dqs.n)], inc=sev)
                        ev_hist[g] = sev.n
                        g += 1
                c.dma("sync",
                      vd[t0:t0 + TCH, :].rearrange("(tt p) dd -> p tt dd", p=128),
                      vtmp[:].rearrange("p (tt dd) -> p tt dd", tt=ntt),
                      waits=[(sev, sev.n)], inc=dqs)
            c.wait("sync", dqs, dqs.n)
            c.barrier()
        with c.scope() as p2:
            TQ = 512
            NKB = T // 512
            NTK = T // 128
            qbuf = c.sb_in(p2, c.fresh("qbuf"), [64, 2 * TQ], BF16)
            kbuf = c.sb_in(p2, c.fresh("kbuf"), [64, 4 * 512], BF16)
            vbuf = c.sb_in(p2, c.fresh("vbuf"), [128, 4 * 4 * 96], BF16)
            prob = c.sb_in(p2, c.fresh("prob"), [128, 3 * TQ], BF16)
            rden = c.sb_in(p2, c.fresh("rden"), [65, 2 * TQ], F32)
            rdbc = c.sb_in(p2, c.fresh("rdbc"), [64, 2 * TQ], F32)
            ytmp = c.sb_in(p2, c.fresh("ytmp"), [64, 2 * TQ], BF16)
            ps_y = c.ps_in(p2, c.fresh("psy"), [96, 2 * TQ], F32)
            ps_sc = c.ps_in(p2, c.fresh("pssc"), [128, 3 * TQ], F32)
            dqq = c.sem("dqq2a")
            dkk = [c.sem(f"dkk2a{i}") for i in range(4)]
            dvv = [c.sem(f"dvv2a{i}") for i in range(4)]
            drd = c.sem("drd2a")
            sdy = c.sem("sdy")
            c.memset("vector", vbuf[:], 0.0)
            vones = vbuf[:].rearrange("p (s dd) -> p s dd", s=16)[:, :, 64:65]
            c.op("vector", "memset", vones, 1.0, inc=sv)
            ones_done = sv.n
            seq_list = [(h, tqb) for h in range(16) for tqb in range(T // TQ)]
            NIT = len(seq_list)
            it_hist = {}
            kfree = {}
            kvld_map = {}
            qld_map = {}
            exp_done = {}
            pv_done = {}
            qk_done = {}

            def emit_qload(j):
                if j >= NIT:
                    return
                h, tqb = seq_list[j]
                ipj = j % 2
                h2 = it_hist.get(j - 2, {})
                c.dma("sync", qbuf[0:64, ipj * TQ:(ipj + 1) * TQ],
                      qd[h * 64:h * 64 + 64, tqb * TQ:tqb * TQ + TQ],
                      waits=[(sp, h2.get("avdone", 0))], inc=dqq)
                qld_map[j] = dqq.n

            def emit_kvload(j, kB):
                if j >= NIT:
                    return
                h, _ = seq_list[j]
                hr = h * 64
                gkB = j * NKB + kB
                kslot = gkB % 4
                kw = []
                if gkB - 4 in kfree:
                    kw.append((sp, kfree[gkB - 4]))
                if gkB == 0:
                    kw.append((sv, ones_done))
                c.dma("sync", kbuf[:, kslot * 512:(kslot + 1) * 512],
                      kd[hr:hr + 64, kB * 512:(kB + 1) * 512],
                      waits=kw, inc=dkk[kslot])
                c.dma("sync",
                      vbuf[:, kslot * 384:(kslot + 1) * 384]
                      .rearrange("p (tt dd) -> p tt dd", tt=4)[:, :, 0:64],
                      vd[kB * 512:(kB + 1) * 512, hr:hr + 64]
                      .rearrange("(tt p) dd -> p tt dd", p=128),
                      inc=dvv[kslot])
                kvld_map[gkB] = (dkk[kslot].n, dvv[kslot].n)

            def emit_qk(g):
                j, tkb = divmod(g, NTK)
                kB, sub = divmod(tkb, 4)
                gkB = j * NKB + kB
                kslot = gkB % 4
                ipj = j % 2
                spar = g % 3
                scs = ps_sc[:, spar * TQ:(spar + 1) * TQ]
                w = [(dkk[kslot], kvld_map[gkB][0]), (dqq, qld_map[j])]
                if g - 3 in exp_done:
                    w.append((sa, exp_done[g - 3]))
                c.op("tensor", "matmul", scs,
                     kbuf[0:64, kslot * 512 + sub * 128:
                          kslot * 512 + sub * 128 + 128],
                     qbuf[0:64, ipj * TQ:(ipj + 1) * TQ],
                     start=True, stop=True, waits=w, inc=sp)
                qk_done[g] = sp.n
                if sub == 0:
                    if kB + 1 < NKB:
                        emit_kvload(j, kB + 1)
                    else:
                        emit_kvload(j + 1, 0)
                if tkb == 4:
                    emit_qload(j + 1)
                pslice = prob[:, spar * TQ:(spar + 1) * TQ]
                ew = [(sp, qk_done[g])]
                if g - 3 in pv_done:
                    ew.append((sp, pv_done[g - 3]))
                c.op("scalar", "activation", pslice, scs, AF.Exp,
                     scale=1.0 / 8.0, waits=ew, inc=sa)
                exp_done[g] = sa.n

            def emit_tail(j):
                h, tqb = seq_list[j]
                ipj = j % 2
                av_done = sp.n
                rdv = rden[64:65, ipj * TQ:(ipj + 1) * TQ]
                rbv = rdbc[0:64, ipj * TQ:(ipj + 1) * TQ]
                ytv = ytmp[0:64, ipj * TQ:(ipj + 1) * TQ]
                h2 = it_hist.get(j - 2, {})
                c.op("vector", "reciprocal", rdv,
                     ps_y[64:65, ipj * TQ:(ipj + 1) * TQ],
                     waits=[(sp, av_done), (drd, h2.get("rddone", 0))],
                     inc=sv)
                c.bounce_bcast(rdv, TQ, rbv, 64, waits=[(sv, sv.n)],
                               waits2=[(sv, h2.get("ytdone", 0))], inc=drd)
                rd_done = drd.n
                c.op("vector", "tensor_tensor", ytv,
                     ps_y[0:64, ipj * TQ:(ipj + 1) * TQ], rbv, ALU.mult,
                     waits=[(drd, rd_done), (sdy, h2.get("stdone", 0))],
                     inc=sv)
                yt_done = sv.n
                c.dma("sync", yd[h * 64:h * 64 + 64, tqb * TQ:tqb * TQ + TQ],
                      ytv, waits=[(sv, yt_done)], inc=sdy)
                it_hist[j] = {"avdone": av_done, "ytdone": yt_done,
                              "stdone": sdy.n, "rddone": rd_done}

            def emit_pv(g):
                j, tkb = divmod(g, NTK)
                kB, sub = divmod(tkb, 4)
                gkB = j * NKB + kB
                kslot = gkB % 4
                ipj = j % 2
                spar = g % 3
                w = [(sa, exp_done[g]), (dvv[kslot], kvld_map[gkB][1])]
                if tkb == 0:
                    h2 = it_hist.get(j - 2, {})
                    w.append((sv, h2.get("ytdone", 0)))
                c.op("tensor", "matmul", ps_y[:, ipj * TQ:(ipj + 1) * TQ],
                     vbuf[:, kslot * 384 + sub * 96:
                          kslot * 384 + sub * 96 + 96],
                     prob[:, spar * TQ:(spar + 1) * TQ],
                     start=(tkb == 0), stop=(tkb == NTK - 1),
                     waits=w, inc=sp)
                pv_done[g] = sp.n
                if sub == 3:
                    kfree[gkB] = sp.n
                if tkb == NTK - 1:
                    emit_tail(j)

            emit_qload(0)
            emit_kvload(0, 0)
            G = NIT * NTK
            for g in range(G + 3):
                if g < G:
                    emit_qk(g)
                if g >= 3:
                    emit_pv(g - 3)
            c.wait("sync", sdy, sdy.n)
            c.barrier()
        with c.scope() as p3:
            ybuf = c.sb_in(p3, c.fresh("ybuf"), [128, ND * T], BF16)
            xsl3 = c.sb_in(p3, c.fresh("xsl3"), [128, ND * TB], F32)
            pp3 = c.ps_in(p3, c.fresh("pp3"), [128, 4 * TB], F32)
            wbufs3 = [c.sb_in(p3, c.fresh("wb3"), [128, 8 * 512], BF16)
                      for _ in range(2)]
            wl3 = [c.sem("wl_l2o0"), c.sem("wl_l2o1")]
            bo = load_row(c, p3, W["L2_attn_b_out"], D, "bo", dtype=BF16, inc=dxs)
            c.dma("sync", ybuf[:].rearrange("p (dc t) -> p dc t", dc=ND),
                  yd.rearrange("(dc p) t -> p dc t", p=128), inc=dxs)
            yld = dxs.n
            seo = c.sem("seo2a")
            ones_row = c.consts["ones_row"]
            for tb in range(NT):
                t0 = tb * TB
                c.load_x(xsl3, t0, t0 + TB, waits=[(dst_, dst_.n)], inc=dxs)
                xld = dxs.n

                def evac_o(mi, tbb, ps, mmc):
                    d = mi // 128
                    c.op("vector", "scalar_tensor_tensor",
                         xsl3[:, d * TB:(d + 1) * TB], ps, 1.0,
                         xsl3[:, d * TB:(d + 1) * TB], ALU.mult, ALU.add,
                         waits=[(sp, mmc), (dxs, xld)], inc=seo)

                proj(c, p3, lambda k, tbb: ybuf[:, k * T + t0:k * T + t0 + TB],
                     W["L2_attn_w_out"], D, 0, D, 1, evac_o, "l2ao",
                     evac_sem=seo, first_waits=[(dxs, yld), (seo, seo.n)],
                     pp=pp3, wbufs=wbufs3, wl_sems=wl3,
                     extra_k=(lambda mi: bo[0:1, mi: mi + 128],
                              lambda tbb: ones_row[0:1, 0:TB]))
                c.store_x(xsl3, t0, t0 + TB, waits=[(seo, seo.n)], inc=dst_)
            c.wait("sync", dst_, dst_.n)
            c.barrier()



def st_l2f(c, W):
    TCH = 1024
    NCH = T // TCH
    NB = TCH // TB
    NTT = TCH // 128
    sv, sa, sp = c.sem("sv"), c.sem("sa"), c.sem("sp")
    dxs, dst_ = c.sem("dxs2"), c.sem("dst")
    with c.scope() as sc:
        rows, rows_done = ln_rows(c, sc, "l2f")
        ab_bc = c.sb_in(sc, c.fresh("abbc"), [128, 2 * T], F32)
        dwb = c.sem("dwb")
        c.bounce_bcast(rows[0:1, :], 2 * T, ab_bc[:], 128,
                       waits=[(sv, rows_done)], inc=dwb)
        bc_done = dwb.n
        rw = c.sb_in(sc, c.fresh("rw"), [128, 8 * ND], BF16)
        drw = c.sem("drw")
        load_w_tiles(c, sc, W["L2_ffn_router_w"], D, 0, 8, "rw", inc=drw,
                     handle=rw)
        rw_done = drw.n
        xsl = c.sb_in(sc, c.fresh("xsl"), [128, ND * TCH], F32)
        xn = c.sb_in(sc, c.fresh("xn"), [128, ND * TCH], BF16)
        h = c.sb_in(sc, c.fresh("h"), [128, 8 * TCH], BF16)
        rtmp = c.sb_in(sc, c.fresh("rtmp"), [128, TB], BF16)
        acc = c.sb_in(sc, c.fresh("acc"), [128, ND * TCH], F32)
        web = c.sb_in(sc, c.fresh("web"), [128, TCH], BF16)
        probs = c.sb_in(sc, c.fresh("probs"), [128, NTT * 8], F32)
        cwb = c.sb_in(sc, c.fresh("cwb"), [128, NTT * 8], BF16)
        cwT = c.sb_in(sc, c.fresh("cwT"), [8, TCH], BF16)
        g1 = c.sb_in(sc, c.fresh("g1"), [128, 8], F32)
        g2 = c.sb_in(sc, c.fresh("g2"), [128, 8], F32)
        gm = c.sb_in(sc, c.fresh("gm"), [128, 4], F32)
        pp = c.ps_in(sc, c.fresh("ppl2f"), [128, 4 * TB], F32)
        psr = c.ps_in(sc, c.fresh("psr"), [128, 8], F32)
        pst = c.ps_in(sc, c.fresh("pst"), [8, 128], F32)
        wbufs = [c.sb_in(sc, c.fresh("wbl2f"), [128, 8 * 512], BF16)
                 for _ in range(2)]
        wl_sems = [c.sem("wl_l2f0"), c.sem("wl_l2f1")]
        identb = c.consts["identb"]
        seh = c.sem("seh2f")
        seo = c.sem("seo2f")
        set_ = c.sem("set2f")
        dcw = c.sem("dcw")
        import os
        l2fdbg = os.environ.get("L2F_DBG") == "1"
        cwd = c.dram("dbg_cwd2f", [8, T], BF16,
                     kind="ExternalOutput" if l2fdbg else "Internal")
        cbd = c.dram("dbg_cbd2f", [NCH, 128, NTT * 8], BF16,
                     kind="ExternalOutput" if l2fdbg else "Internal")
        pbd = c.dram("dbg_pbd2f", [NCH, 128, NTT * 8], F32,
                     kind="ExternalOutput" if l2fdbg else "Internal")

        for ch in range(NCH):
            t0 = ch * TCH
            wl = [(dst_, dst_.n)] if ch else []
            c.load_x(xsl, t0, t0 + TCH, waits=wl, inc=dxs)
            emit_xn_slice(c, xn[:], xsl[:], ab_bc, t0, TCH,
                          waits=[(dxs, dxs.n), (dwb, bc_done), (sp, sp.n)])
            xn_done = sv.n
            for tt in range(NTT):
                for k in range(ND):
                    c.op("tensor", "matmul", psr[:],
                         xn[:, k * TCH + tt * 128: k * TCH + tt * 128 + 128],
                         rw[:, k * 8:(k + 1) * 8],
                         start=(k == 0), stop=(k == ND - 1),
                         waits=([(sv, xn_done), (drw, rw_done),
                                 (sv, sv.n)] if k == 0 else ()), inc=sp)
                pv = probs[:, tt * 8:(tt + 1) * 8]
                c.op("scalar", "activation", pv, psr[:], AF.Sigmoid,
                     waits=[(sp, sp.n)], inc=sa)
                c.op("vector", "reduce_max", gm[:, 0:1], pv,
                     axis=mybir.AxisListType.X, waits=[(sa, sa.n)], inc=sv)
                c.op("vector", "tensor_scalar", g1[:, 0:8], pv, gm[:, 0:1],
                     None, ALU.is_equal, waits=[(sv, sv.n)], inc=sv)
                c.op("vector", "scalar_tensor_tensor", g2[:, 0:8], g1[:, 0:8],
                     -1e30, pv, ALU.mult, ALU.add, waits=[(sv, sv.n)], inc=sv)
                c.op("vector", "reduce_max", gm[:, 1:2], g2[:, 0:8],
                     axis=mybir.AxisListType.X, waits=[(sv, sv.n)], inc=sv)
                c.op("vector", "tensor_scalar", g2[:, 0:8], g2[:, 0:8],
                     gm[:, 1:2], None, ALU.is_equal, waits=[(sv, sv.n)], inc=sv)
                c.op("vector", "tensor_tensor", gm[:, 2:3], gm[:, 0:1],
                     gm[:, 1:2], ALU.add, waits=[(sv, sv.n)], inc=sv)
                c.op("vector", "tensor_scalar_add", gm[:, 2:3], gm[:, 2:3],
                     1e-20, waits=[(sv, sv.n)], inc=sv)
                c.op("vector", "reciprocal", gm[:, 3:4], gm[:, 2:3],
                     waits=[(sv, sv.n)], inc=sv)
                c.op("vector", "tensor_tensor", gm[:, 0:1], gm[:, 0:1],
                     gm[:, 3:4], ALU.mult, waits=[(sv, sv.n)], inc=sv)
                c.op("vector", "tensor_tensor", gm[:, 1:2], gm[:, 1:2],
                     gm[:, 3:4], ALU.mult, waits=[(sv, sv.n)], inc=sv)
                c.op("vector", "tensor_scalar", g1[:, 0:8], g1[:, 0:8],
                     gm[:, 0:1], None, ALU.mult, waits=[(sv, sv.n)], inc=sv)
                c.op("vector", "tensor_scalar", g2[:, 0:8], g2[:, 0:8],
                     gm[:, 1:2], None, ALU.mult, waits=[(sv, sv.n)], inc=sv)
                c.op("vector", "tensor_tensor", cwb[:, tt * 8:(tt + 1) * 8],
                     g1[:, 0:8], g2[:, 0:8], ALU.add, waits=[(sv, sv.n)], inc=sv)
                c.op("tensor", "matmul", pst[:],
                     cwb[:, tt * 8:(tt + 1) * 8], identb[:],
                     start=True, stop=True,
                     waits=[(sv, sv.n), (set_, set_.n)], inc=sp)
                c.op("vector", "tensor_copy", cwT[:, tt * 128:(tt + 1) * 128],
                     pst[:], waits=[(sp, sp.n), (dcw, dcw.n)], inc=set_)
            c.dma("sync", cwd[:, t0:t0 + TCH], cwT[:],
                  waits=[(set_, set_.n), (dcw, dcw.n)], inc=dcw)
            if l2fdbg:
                c.dma("sync", cbd[ch], cwb[:], inc=dcw)
                c.dma("sync", pbd[ch], probs[:], inc=dcw)
            cw_st = dcw.n
            for e in range(8):
                c.dma("sync", web[:],
                      AP(cwd.tensor, e * T + t0, [[0, 128], [1, TCH]]),
                      waits=[(dcw, cw_st), (seo, seo.n)], inc=dcw)
                web_done = dcw.n

                def evac_h(mi, tb, ps, mmc, e=e):
                    m = (mi - e * 1024) // 128
                    c.op("vector", "tensor_relu", rtmp[:], ps,
                         waits=[(sp, mmc), (seh, seh.n)], inc=sv)
                    c.op("scalar", "activation",
                         h[:, m * TCH + tb * TB: m * TCH + (tb + 1) * TB],
                         rtmp[:], AF.Square, waits=[(sv, sv.n)], inc=seh)

                proj(c, sc, lambda k, tb: xn[:, k * TCH + tb * TB:
                                             k * TCH + (tb + 1) * TB],
                     W["L2_ffn_w1"], D, e * 1024, (e + 1) * 1024, NB, evac_h,
                     "w1f", evac_sem=seh,
                     first_waits=[(sv, xn_done), (seo, seo.n), (set_, set_.n)],
                     pp=pp, wbufs=wbufs, wl_sems=wl_sems)

                def evac_o(mi, tb, ps, mmc, e=e):
                    d = mi // 128
                    sl = slice(d * TCH + tb * TB, d * TCH + (tb + 1) * TB)
                    wsl = web[:, tb * TB:(tb + 1) * TB]
                    if e == 0:
                        c.op("vector", "tensor_tensor", acc[:, sl], ps, wsl,
                             ALU.mult, waits=[(sp, mmc), (dcw, web_done)],
                             inc=seo)
                    else:
                        c.op("vector", "tensor_tensor", rtmp[:], ps, wsl,
                             ALU.mult, waits=[(sp, mmc), (dcw, web_done)],
                             inc=sv)
                        c.op("vector", "tensor_tensor", acc[:, sl], acc[:, sl],
                             rtmp[:], ALU.add, inc=seo)

                proj(c, sc, lambda k, tb: h[:, k * TCH + tb * TB:
                                            k * TCH + (tb + 1) * TB],
                     W["L2_ffn_w2"], 1024, 0, D, NB, evac_o, "w2f",
                     evac_sem=seo, w_row_lo=e * 1024,
                     first_waits=[(seh, seh.n)],
                     pp=pp, wbufs=wbufs, wl_sems=wl_sems)
            for d in range(ND):
                sl = slice(d * TCH, (d + 1) * TCH)
                c.op("vector", "scalar_tensor_tensor", xsl[:, sl], acc[:, sl],
                     0.5, xsl[:, sl], ALU.mult, ALU.add,
                     waits=[(seo, seo.n)], inc=sv)
            c.store_x(xsl, t0, t0 + TCH, waits=[(sv, sv.n)], inc=dst_)
        c.wait("sync", dst_, dst_.n)
        c.barrier()



def st_l0a(c, W):
    TCH = 512
    NCH = T // TCH
    NM = 16
    sv, sa, sp = c.sem("sv"), c.sem("sa"), c.sem("sp")
    dxs, dst_, dzy = c.sem("dxs2"), c.sem("dst"), c.sem("dzy")
    zd = c.dram(c.fresh("zd0a"), [2048, T], BF16)
    yd = c.dram(c.fresh("yd0a"), [2048, T], BF16)
    with c.scope() as sc:
        rows_sc = c.scope()
        rows, rows_done = ln_rows(c, rows_sc, "l0a")
        abd = c.dram(c.fresh("abd0a"), [2 * T], F32)
        dab = c.sem("dab")
        c.dma("sync", AP(abd.tensor, 0, [[2 * T, 1], [1, 2 * T]]), rows[0:1, :],
              waits=[(sv, rows_done)], inc=dab)
        c.wait("sync", dab, dab.n)
        c.barrier()
        rows_sc.close()
        dgp = c.sem("dgp")
        ipb = load_mat(c, sc, W["L0_attn_in_proj_bc"], 128, 32, "ipb", inc=dgp)
        cwc = load_mat(c, sc, W["L0_attn_conv_wc"], 128, 64, "cwc", inc=dgp)
        cbc = load_mat(c, sc, W["L0_attn_conv_bc"], 128, 16, "cbc", inc=dgp)
        dtb = load_mat(c, sc, W["L0_attn_dt_bc"], 128, 16, "dtb", inc=dgp)
        alc = load_mat(c, sc, W["L0_attn_A_logc"], 128, 16, "alc", inc=dgp)
        dpc = load_mat(c, sc, W["L0_attn_D_paramc"], 128, 16, "dpc", inc=dgp)
        par_done = dgp.n
        negA = c.sb_in(sc, c.fresh("negA"), [128, 16], F32)
        biasA = c.sb_in(sc, c.fresh("biasA"), [128, 16], F32)
        c.op("scalar", "activation", negA[:], alc[:], AF.Exp, scale=1.0,
             waits=[(dgp, par_done)], inc=sa)
        c.op("vector", "tensor_scalar_mul", negA[:], negA[:], -1.0,
             waits=[(sa, sa.n)], inc=sv)
        c.op("vector", "tensor_scalar_mul", biasA[:], negA[:], 1e-4,
             waits=[(sv, sv.n)], inc=sv)
        prm_done = sv.n
        u = c.sb_in(sc, c.fresh("u0a"), [128, NM * T], BF16)
        with c.scope() as p1:
            ab_bc = c.sb_in(p1, c.fresh("abbc"), [128, 2 * TCH], F32)
            xsl = c.sb_in(p1, c.fresh("xsl"), [128, ND * TCH], F32)
            xn = c.sb_in(p1, c.fresh("xn"), [128, ND * TCH], BF16)
            zbuf = c.sb_in(p1, c.fresh("zbuf"), [128, TCH], BF16)
            pp = c.ps_in(p1, c.fresh("pp0a"), [128, 4 * TB], F32)
            wbufs = [c.sb_in(p1, c.fresh("wb0a"), [128, 8 * 512], BF16)
                     for _ in range(2)]
            wl_sems = [c.sem("wl_l0a0"), c.sem("wl_l0a1")]
            sez = c.sem("sez0a")
            dwb = c.sem("dwb")
            for ch in range(NCH):
                t0 = ch * TCH
                asrc = AP(abd.tensor, t0, [[0, 128], [1, TCH]])
                bsrc = AP(abd.tensor, T + t0, [[0, 128], [1, TCH]])
                c.dma("sync", ab_bc[:, 0:TCH], asrc,
                      waits=[(sv, sv.n), (sez, sez.n)], inc=dwb)
                c.dma("sync", ab_bc[:, TCH:2 * TCH], bsrc, inc=dwb)
                c.load_x(xsl, t0, t0 + TCH, inc=dxs)
                emit_xn_slice2(c, xn[:], xsl[:], ab_bc, TCH,
                               waits=[(dxs, dxs.n), (dwb, dwb.n), (sp, sp.n)])
                xn_done = sv.n

                def evac_xp(mi, tb, ps, mmc):
                    m = mi // 128
                    if m < NM:
                        c.op("scalar", "activation",
                             u[:, m * T + t0 + tb * TB: m * T + t0 + (tb + 1) * TB],
                             ps, AF.Identity, bias=ipb[:, m:m + 1],
                             waits=[(sp, mmc), (dgp, par_done)], inc=sez)
                    else:
                        mz = m - NM
                        c.op("scalar", "activation", zbuf[:, tb * TB:(tb + 1) * TB],
                             ps, AF.Sigmoid, bias=ipb[:, m:m + 1],
                             waits=[(sp, mmc), (dzy, dzy.n)], inc=sez)
                        c.dma("sync",
                              zd[mz * 128:(mz + 1) * 128, t0 + tb * TB:t0 + (tb + 1) * TB],
                              zbuf[:, tb * TB:(tb + 1) * TB],
                              waits=[(sez, sez.n)], inc=dzy)

                proj(c, p1, lambda k, tb: xn[:, k * TCH + tb * TB:
                                             k * TCH + (tb + 1) * TB],
                     W["L0_attn_in_proj_w"], D, 0, 4096, TCH // TB, evac_xp,
                     "ip0a", evac_sem=sez,
                     first_waits=[(sv, xn_done), (sez, sez.n)],
                     pp=pp, wbufs=wbufs, wl_sems=wl_sems)
            c.wait("sync", dzy, dzy.n)
            c.barrier()
        with c.scope() as p2:
            acc = c.sb_in(p2, c.fresh("acc0a"), [128, T], F32)
            for m in range(NM):
                uv = u[:, m * T:(m + 1) * T]
                c.op("vector", "tensor_scalar", acc[:, 0:T], uv,
                     cwc[:, m * 4 + 3: m * 4 + 4], None, ALU.mult,
                     waits=[(sa, sa.n), (dgp, par_done)], inc=sv)
                for kk in range(3):
                    sh = 3 - kk
                    c.op("vector", "scalar_tensor_tensor", acc[:, sh:T],
                         u[:, m * T: m * T + T - sh],
                         cwc[:, m * 4 + kk: m * 4 + kk + 1],
                         acc[:, sh:T], ALU.mult, ALU.add,
                         waits=[(sv, sv.n)], inc=sv)
                c.op("scalar", "activation", uv, acc[:, 0:T], AF.Silu,
                     bias=cbc[:, m:m + 1], waits=[(sv, sv.n)], inc=sa)
            c.barrier()
        with c.scope() as p3:
            dcy = c.sb_in(p3, c.fresh("dcy"), [128, T], BF16)
            Bt = c.sb_in(p3, c.fresh("Bt"), [128, T], BF16)
            Ct = c.sb_in(p3, c.fresh("Ct"), [128, T], BF16)
            om = c.sb_in(p3, c.fresh("om"), [128, T], BF16)
            stt_ = c.sb_in(p3, c.fresh("stt"), [128, T], BF16)
            yv = c.sb_in(p3, c.fresh("yv"), [128, T], BF16)
            zy = om
            spl = c.sb_in(p3, c.fresh("spl"), [128, T], BF16)
            pp2 = c.ps_in(p3, c.fresh("pp0a2"), [128, 2 * TB], F32)
            wb2 = [c.sb_in(p3, c.fresh("wb0a2"), [128, 16 * 128], BF16)
                   for _ in range(3)]
            wsem = [c.sem("wl0a_dt"), c.sem("wl0a_b"), c.sem("wl0a_c")]
            se2 = c.sem("se0a2")
            bt_free = ct_free = scan_done = None
            for m in range(NM):
                for wi, wname in enumerate(["L0_attn_dt_w", "L0_attn_b_w",
                                            "L0_attn_c_w"]):
                    load_w_tiles(c, p3, W[wname], 2048, m * 128, (m + 1) * 128,
                                 "w0a2", queue="gpsimd",
                                 waits=[(sp, sp.n)], inc=wsem[wi],
                                 handle=wb2[wi])
                for wi, dest, ldd in ((0, spl, None), (1, Bt, None), (2, Ct, None)):
                    for tb in range(NT):
                        par = tb % 2
                        ps = pp2[:, par * TB:(par + 1) * TB]
                        w0 = [(wsem[wi], wsem[wi].n), (se2, max(0, se2.n - 1))]
                        for k in range(16):
                            c.op("tensor", "matmul", ps,
                                 wb2[wi][:, k * 128:(k + 1) * 128],
                                 u[:, k * T + tb * TB: k * T + (tb + 1) * TB],
                                 start=(k == 0), stop=(k == 15),
                                 waits=(w0 if k == 0 else ()), inc=sp)
                        sl = slice(tb * TB, (tb + 1) * TB)
                        if wi == 0:
                            c.op("scalar", "activation", spl[:, sl], ps,
                                 AF.Exp, bias=dtb[:, m:m + 1],
                                 waits=[(sp, sp.n)], inc=se2)
                        elif wi == 1:
                            c.op("scalar", "activation", dest[:, sl], ps,
                                 AF.Tanh,
                                 waits=[(sp, sp.n),
                                        (sv, bt_free if bt_free is not None
                                         else sv.n)],
                                 inc=se2)
                        else:
                            c.op("scalar", "activation", dest[:, sl], ps,
                                 AF.Tanh,
                                 waits=[(sp, sp.n),
                                        (sv, ct_free if ct_free is not None
                                         else sv.n)],
                                 inc=se2)
                c.op("scalar", "activation", spl[:], spl[:], AF.Ln, bias=1.0,
                     waits=[(se2, se2.n)], inc=sa)
                c.op("scalar", "activation", dcy[:], spl[:], AF.Exp,
                     scale=negA[:, m:m + 1], bias=biasA[:, m:m + 1],
                     waits=[(sa, sa.n),
                            (sv, scan_done if scan_done is not None
                             else sv.n)], inc=sa)
                c.op("vector", "tensor_scalar", om[:], dcy[:], -1.0, 1.0,
                     ALU.mult, ALU.add, waits=[(sa, sa.n)], inc=sv)
                c.op("vector", "tensor_tensor", om[:], om[:], Bt[:], ALU.mult,
                     waits=[(sv, sv.n)], inc=sv)
                bt_free = sv.n
                c.op("vector", "tensor_tensor", om[:], om[:],
                     u[:, m * T:(m + 1) * T], ALU.mult, waits=[(sv, sv.n)],
                     inc=sv)
                c.op("vector", "tensor_tensor_scan", stt_[:], dcy[:], om[:],
                     0.0, ALU.mult, ALU.add, waits=[(sv, sv.n)], inc=sv)
                scan_done = sv.n
                c.op("vector", "tensor_tensor", yv[:], Ct[:], stt_[:],
                     ALU.mult, waits=[(sv, sv.n)], inc=sv)
                ct_free = sv.n
                c.op("vector", "scalar_tensor_tensor", yv[:],
                     u[:, m * T:(m + 1) * T], dpc[:, m:m + 1], yv[:],
                     ALU.mult, ALU.add, waits=[(sv, sv.n)], inc=sv)
                c.dma("sync", zy[:], zd[m * 128:(m + 1) * 128, :],
                      waits=[(sv, sv.n), (dzy, dzy.n)], inc=dzy)
                c.op("vector", "tensor_tensor", zy[:], yv[:], zy[:], ALU.mult,
                     waits=[(dzy, dzy.n), (sv, sv.n)], inc=sv)
                c.dma("sync", yd[m * 128:(m + 1) * 128, :], zy[:],
                      waits=[(sv, sv.n)], inc=dzy)
            c.wait("sync", dzy, dzy.n)
            c.barrier()
    with c.scope() as p4:
        sv, sa, sp = c.sem("sv"), c.sem("sa"), c.sem("sp")
        ybuf = c.sb_in(p4, c.fresh("ybuf0a"), [128, 16 * T], BF16)
        xsl4 = c.sb_in(p4, c.fresh("xsl4"), [128, ND * TB], F32)
        ob = load_row(c, p4, W["L0_attn_out_b"], D, "ob", dtype=BF16, inc=dxs)
        c.dma("sync", ybuf[:].rearrange("p (mc t) -> p mc t", mc=16),
              yd.rearrange("(mc p) t -> p mc t", p=128), inc=dxs)
        yld = dxs.n
        seo = c.sem("seo0a")
        ones_row = c.consts["ones_row"]
        pp4 = c.ps_in(p4, c.fresh("pp0a4"), [128, 4 * TB], F32)
        wb4 = [c.sb_in(p4, c.fresh("wb0a4"), [128, 16 * 128], BF16)
               for _ in range(2)]
        wl4 = [c.sem("wl0a40"), c.sem("wl0a41")]
        for tb in range(NT):
            t0 = tb * TB
            c.load_x(xsl4, t0, t0 + TB, waits=[(dst_, dst_.n)], inc=dxs)
            xld = dxs.n

            def evac_o(mi, tbb, ps, mmc):
                d = mi // 128
                c.op("vector", "scalar_tensor_tensor",
                     xsl4[:, d * TB:(d + 1) * TB], ps, 1.0,
                     xsl4[:, d * TB:(d + 1) * TB], ALU.mult, ALU.add,
                     waits=[(sp, mmc), (dxs, xld)], inc=seo)

            proj(c, p4, lambda k, tbb: ybuf[:, k * T + t0:k * T + t0 + TB],
                 W["L0_attn_out_w"], 2048, 0, D, 1, evac_o, "o0a",
                 evac_sem=seo, mblk=128,
                 first_waits=[(dxs, yld), (seo, seo.n)],
                 pp=pp4, wbufs=wb4, wl_sems=wl4,
                 extra_k=(lambda mi: ob[0:1, mi: mi + 128],
                          lambda tbb: ones_row[0:1, 0:TB]))
            c.store_x(xsl4, t0, t0 + TB, waits=[(seo, seo.n)], inc=dst_)
        c.wait("sync", dst_, dst_.n)
        c.barrier()


def emit_xn_slice2(c, xn_v, xsl_v, ab_bc, width, waits=(), inc=None, nd=ND):
    sv = c.sem("sv")
    a3 = ab_bc[:, 0:width].unsqueeze(1).broadcast_to([128, nd, width])
    b3 = ab_bc[:, width:2 * width].unsqueeze(1).broadcast_to([128, nd, width])
    x3 = xsl_v.rearrange("p (dc t) -> p dc t", dc=nd)
    n3 = xn_v.rearrange("p (dc t) -> p dc t", dc=nd)
    c.op("vector", "tensor_tensor", n3, x3, a3, ALU.mult, waits=waits, inc=sv)
    c.op("vector", "tensor_tensor", n3, n3, b3, ALU.subtract, inc=inc or sv)



def st_l1a(c, W):
    TCH = 512
    NCH = T // TCH
    NM = 16
    NS = 16
    TC2 = 256
    NC2 = T // TC2
    sv, sa, sp = c.sem("sv"), c.sem("sa"), c.sem("sp")
    dxs, dst_, dzy = c.sem("dxs2"), c.sem("dst"), c.sem("dzy")
    import os
    l1adbg = os.environ.get("L1A_DBG") == "1"
    kind = "ExternalOutput" if l1adbg else "Internal"
    zd = c.dram("dbg_zd1a", [2048, T], BF16, kind=kind)
    yd = c.dram("dbg_yd1a", [2048, T], BF16, kind=kind)
    ssmd = c.dram("dbg_ssm1a", [64, T], F32, kind=kind) if l1adbg else None
    xcd = c.dram("dbg_xc1a", [2048, T], BF16, kind=kind) if l1adbg else None
    bncd = c.dram(c.fresh("bnc1a"), [33, TC2], F32)
    with c.scope() as sc:
        rows_sc = c.scope()
        rows, rows_done = ln_rows(c, rows_sc, "l1a")
        abd = c.dram(c.fresh("abd1a"), [2 * T], F32)
        dab = c.sem("dab")
        c.dma("sync", AP(abd.tensor, 0, [[2 * T, 1], [1, 2 * T]]), rows[0:1, :],
              waits=[(sv, rows_done)], inc=dab)
        c.wait("sync", dab, dab.n)
        c.barrier()
        rows_sc.close()
        dgp = c.sem("dgp")
        cwc = load_mat(c, sc, W["L1_attn_conv_wc"], 128, 64, "cwc", inc=dgp)
        cbc = load_mat(c, sc, W["L1_attn_conv_bc"], 128, 16, "cbc", inc=dgp)
        wdt = load_mat(c, sc, W["L1_attn_w_dtc"], 128, 16, "wdt", inc=dgp)
        bdt = load_mat(c, sc, W["L1_attn_b_dtc"], 128, 16, "bdt", inc=dgp)
        alc = load_mat(c, sc, W["L1_attn_A_logc"], 128, 256, "alc", inc=dgp)
        dpc = load_mat(c, sc, W["L1_attn_D_paramc"], 128, 16, "dpc", inc=dgp)
        nwc = load_mat(c, sc, W["L1_attn_norm_wc"], 128, 8, "nwc", inc=dgp)
        par_done = dgp.n
        negA = c.sb_in(sc, c.fresh("negA1"), [128, 256], BF16)
        negAf = c.sb_in(sc, c.fresh("negA1f"), [128, 256], F32)
        c.op("scalar", "activation", negAf[:], alc[:], AF.Exp,
             waits=[(dgp, par_done)], inc=sa)
        c.op("vector", "tensor_scalar_mul", negAf[:], negAf[:], -1.0,
             waits=[(sa, sa.n)], inc=sv)
        prm_done = sv.n
        xc = c.sb_in(sc, c.fresh("xc1a"), [128, NM * T], BF16)
        ssm = c.sb_in(sc, c.fresh("ssm1a"), [64, T], F32)
        with c.scope() as p1:
            ab_bc = c.sb_in(p1, c.fresh("abbc"), [128, 2 * TCH], F32)
            rrb = c.sb_in(p1, c.fresh("rrb"), [128, TCH], F32)
            rrow = c.sb_in(p1, c.fresh("rrow"), [1, TCH], F32)
            xsl = c.sb_in(p1, c.fresh("xsl"), [128, ND * TCH], F32)
            xn = c.sb_in(p1, c.fresh("xn"), [128, ND * TCH], BF16)
            xq = xsl
            zbuf = c.sb_in(p1, c.fresh("zbuf"), [128, TCH], BF16)
            pp = c.ps_in(p1, c.fresh("pp1a"), [128, 4 * TB], F32)
            psq = c.ps_in(p1, c.fresh("psq1a"), [1, TB], F32)
            wbufs = [c.sb_in(p1, c.fresh("wb1a"), [128, 8 * 256], BF16)
                     for _ in range(2)]
            wl_sems = [c.sem("wl_l1a0"), c.sem("wl_l1a1")]
            sez = c.sem("sez1a")
            dwb = c.sem("dwb")
            ones_colf = c.consts["ones_colf"]
            for ch in range(NCH):
                t0 = ch * TCH
                asrc = AP(abd.tensor, t0, [[0, 128], [1, TCH]])
                bsrc = AP(abd.tensor, T + t0, [[0, 128], [1, TCH]])
                c.dma("sync", ab_bc[:, 0:TCH], asrc,
                      waits=[(sv, sv.n), (sez, sez.n)], inc=dwb)
                c.dma("sync", ab_bc[:, TCH:2 * TCH], bsrc, inc=dwb)
                c.load_x(xsl, t0, t0 + TCH, inc=dxs)
                emit_xn_slice2(c, xn[:], xsl[:], ab_bc, TCH,
                               waits=[(dxs, dxs.n), (dwb, dwb.n), (sp, sp.n)])
                xn_done = sv.n
                c.op("scalar", "activation", xq[:], xn[:], AF.Square,
                     waits=[(sv, xn_done)], inc=sa)
                sq_done = sa.n
                for d in range(ND):
                    c.op("tensor", "matmul", psq[0:1, :], ones_colf[:, 0:1],
                         xq[:, d * TCH:(d + 1) * TCH],
                         start=(d == 0), stop=(d == ND - 1),
                         waits=([(sa, sq_done), (sv, sv.n)] if d == 0 else ()),
                         inc=sp)
                c.op("vector", "tensor_scalar_mul", rrow[:], psq[0:1, :],
                     1.0 / D, waits=[(sp, sp.n)], inc=sv)
                c.op("scalar", "activation", rrow[:], rrow[:], AF.Sqrt,
                     bias=1e-6, waits=[(sv, sv.n)], inc=sa)
                c.op("vector", "reciprocal", rrow[:], rrow[:],
                     waits=[(sa, sa.n)], inc=sv)
                c.bounce_bcast(rrow[0:1, :], TCH, rrb[:], 128,
                               waits=[(sv, sv.n)], inc=dwb)
                rr_done = dwb.n
                for d in range(ND):
                    c.op("vector", "scalar_tensor_tensor",
                         xn[:, d * TCH:(d + 1) * TCH],
                         xn[:, d * TCH:(d + 1) * TCH], nwc[:, d:d + 1],
                         rrb[:, 0:TCH], ALU.mult, ALU.mult,
                         waits=[(dwb, rr_done), (sv, sv.n)], inc=sv)
                xn_done = sv.n

                def evac_xz(mi, tb, ps, mmc):
                    m = mi // 128
                    if m < NM:
                        c.op("scalar", "activation",
                             xc[:, m * T + t0 + tb * TB: m * T + t0 + (tb + 1) * TB],
                             ps, AF.Identity, waits=[(sp, mmc)], inc=sez)
                    else:
                        mz = m - NM
                        c.op("scalar", "activation", zbuf[:, tb * TB:(tb + 1) * TB],
                             ps, AF.Sigmoid if False else AF.Silu,
                             waits=[(sp, mmc), (dzy, dzy.n)], inc=sez)
                        c.dma("sync",
                              zd[mz * 128:(mz + 1) * 128, t0 + tb * TB:t0 + (tb + 1) * TB],
                              zbuf[:, tb * TB:(tb + 1) * TB],
                              waits=[(sez, sez.n)], inc=dzy)

                proj(c, p1, lambda k, tb: xn[:, k * TCH + tb * TB:
                                             k * TCH + (tb + 1) * TB],
                     W["L1_attn_W_in"], D, 0, 4096, TCH // TB, evac_xz,
                     "ip1a", evac_sem=sez, mblk=256,
                     first_waits=[(sv, xn_done), (sez, sez.n)],
                     pp=pp, wbufs=wbufs, wl_sems=wl_sems)
            c.wait("sync", dzy, dzy.n)
            c.barrier()
        with c.scope() as p2:
            acc = c.sb_in(p2, c.fresh("acc1a"), [128, T], F32)
            for m in range(NM):
                uv = xc[:, m * T:(m + 1) * T]
                c.op("vector", "tensor_scalar", acc[:, 0:T], uv,
                     cwc[:, m * 4 + 3: m * 4 + 4], None, ALU.mult,
                     waits=[(sa, sa.n), (dgp, par_done)], inc=sv)
                for kk in range(3):
                    sh = 3 - kk
                    c.op("vector", "scalar_tensor_tensor", acc[:, sh:T],
                         xc[:, m * T: m * T + T - sh],
                         cwc[:, m * 4 + kk: m * 4 + kk + 1],
                         acc[:, sh:T], ALU.mult, ALU.add,
                         waits=[(sv, sv.n)], inc=sv)
                c.op("scalar", "activation", uv, acc[:, 0:T], AF.Silu,
                     bias=cbc[:, m:m + 1], waits=[(sv, sv.n)], inc=sa)
            c.barrier()
        with c.scope() as p2b:
            wx = c.sb_in(p2b, c.fresh("wx1a"), [128, 16 * 64], BF16)
            dwx = c.sem("dwx")
            load_w_tiles(c, p2b, W["L1_attn_W_xp"], 2048, 0, 64, "wx",
                         inc=dwx, handle=wx)
            psx = c.ps_in(p2b, c.fresh("psx1a"), [64, TB], F32)
            sex = c.sem("sex1a")
            for tb in range(NT):
                for k in range(16):
                    c.op("tensor", "matmul", psx[:],
                         wx[:, k * 64:(k + 1) * 64],
                         xc[:, k * T + tb * TB: k * T + (tb + 1) * TB],
                         start=(k == 0), stop=(k == 15),
                         waits=([(dwx, dwx.n), (sa, sa.n),
                                 (sex, sex.n)] if k == 0 else ()), inc=sp)
                c.op("vector", "tensor_copy",
                     ssm[:, tb * TB:(tb + 1) * TB], psx[:],
                     waits=[(sp, sp.n)], inc=sex)
            if l1adbg:
                c.dma("sync", ssmd, ssm[:], waits=[(sv, sv.n), (sa, sa.n)], inc=dxs)
                c.dma("sync", xcd.rearrange("(mc p) t -> p mc t", p=128),
                      xc[:].rearrange("p (mc t) -> p mc t", mc=16), inc=dxs)
            c.barrier()
        with c.scope() as p3:
            Brep = c.sb_in(p3, c.fresh("Brep"), [128, NS * TC2], BF16)
            Crep = c.sb_in(p3, c.fresh("Crep"), [128, NS * TC2], BF16)
            dtr = c.sb_in(p3, c.fresh("dtr"), [128, TC2], F32)
            dtm = c.sb_in(p3, c.fresh("dtm"), [128, TC2], F32)
            dtxc = c.sb_in(p3, c.fresh("dtxc"), [128, TC2], BF16)
            abar = c.sb_in(p3, c.fresh("abar"), [128, NS * TC2], BF16)
            bx = c.sb_in(p3, c.fresh("bx"), [128, NS * TC2], BF16)
            hbuf = c.sb_in(p3, c.fresh("hbuf"), [128, NS * TC2], BF16)
            hC = c.sb_in(p3, c.fresh("hC"), [128, NS * TC2], BF16)
            yv = c.sb_in(p3, c.fresh("yv1"), [128, TC2], BF16)
            hc = c.sb_in(p3, c.fresh("hc1"), [128, NM * NS], F32)
            dbc = c.sem("dbc")
            c.memset("vector", hc[:], 0.0)
            sv0 = c.sem("sv")
            prev_dtxc = [0]
            prev_scan = {}
            for tc in range(NC2):
                t0 = tc * TC2
                c.dma("sync", bncd[:, :], ssm[0:33, t0:t0 + TC2],
                      waits=[(sv, sv.n), (sa, sa.n), (dbc, dbc.n)], inc=dbc)
                c.wait("sync", dbc, dbc.n)
                bsrc = AP(bncd.tensor, 1 * TC2, [[0, 128], [TC2, NS], [1, TC2]])
                csrc = AP(bncd.tensor, 17 * TC2, [[0, 128], [TC2, NS], [1, TC2]])
                dsrc = AP(bncd.tensor, 0, [[0, 128], [1, TC2]])
                c.dma("gpsimd", Brep[:].rearrange("p (s t) -> p s t", s=NS),
                      bsrc, waits=[(dbc, dbc.n)], inc=dbc)
                c.dma("gpsimd", Crep[:].rearrange("p (s t) -> p s t", s=NS),
                      csrc, inc=dbc)
                c.dma("sync", dtr[:], dsrc, inc=dbc)
                rep_done = dbc.n
                for m in range(NM):
                    xcs = xc[:, m * T + t0: m * T + t0 + TC2]
                    c.op("scalar", "activation", dtm[:], dtr[:], AF.Exp,
                         scale=wdt[:, m:m + 1], bias=bdt[:, m:m + 1],
                         waits=[(dbc, rep_done), (sv, prev_dtxc[0])], inc=sa)
                    c.op("scalar", "activation", dtm[:], dtm[:], AF.Ln,
                         bias=1.0, inc=sa)
                    c.op("vector", "tensor_tensor", dtxc[:], dtm[:], xcs,
                         ALU.mult, waits=[(sa, sa.n)], inc=sv)
                    prev_dtxc[0] = sv.n
                    abar_done = {}
                    for s in range(NS):
                        aw = ([(sv, prev_scan[s])] if s in prev_scan else [])
                        c.op("scalar", "activation",
                             abar[:, s * TC2:(s + 1) * TC2], dtm[:], AF.Exp,
                             scale=negAf[:, m * NS + s:m * NS + s + 1],
                             waits=aw, inc=sa)
                        abar_done[s] = sa.n
                    for s in range(NS):
                        c.op("vector", "tensor_tensor",
                             bx[:, s * TC2:(s + 1) * TC2], dtxc[:],
                             Brep[:, s * TC2:(s + 1) * TC2], ALU.mult,
                             inc=sv)
                    for s in range(NS):
                        c.op("vector", "tensor_tensor_scan",
                             hbuf[:, s * TC2:(s + 1) * TC2],
                             abar[:, s * TC2:(s + 1) * TC2],
                             bx[:, s * TC2:(s + 1) * TC2],
                             hc[:, m * NS + s: m * NS + s + 1],
                             ALU.mult, ALU.add,
                             waits=[(sa, abar_done[s])],
                             inc=sv)
                        prev_scan[s] = sv.n
                    c.op("vector", "tensor_copy", hc[:, m * NS:(m + 1) * NS],
                         hbuf[:].rearrange("p (s t) -> p s t", s=NS)[:, :, TC2 - 1],
                         inc=sv)
                    c.op("vector", "tensor_tensor", hC[:], hbuf[:], Crep[:],
                         ALU.mult, inc=sv)
                    c.op("vector", "tensor_tensor", hbuf[:, 0:8 * TC2],
                         hC[:, 0:8 * TC2], hC[:, 8 * TC2:16 * TC2],
                         ALU.add, inc=sv)
                    c.op("vector", "tensor_tensor",
                         hbuf[:, 8 * TC2:12 * TC2], hbuf[:, 0:4 * TC2],
                         hbuf[:, 4 * TC2:8 * TC2], ALU.add, inc=sv)
                    c.op("vector", "tensor_tensor",
                         hbuf[:, 12 * TC2:14 * TC2], hbuf[:, 8 * TC2:10 * TC2],
                         hbuf[:, 10 * TC2:12 * TC2], ALU.add, inc=sv)
                    c.op("vector", "tensor_tensor", yv[:],
                         hbuf[:, 12 * TC2:13 * TC2],
                         hbuf[:, 13 * TC2:14 * TC2], ALU.add, inc=sv)
                    c.op("vector", "scalar_tensor_tensor", yv[:], xcs,
                         dpc[:, m:m + 1], yv[:], ALU.mult, ALU.add,
                         waits=[(sv, sv.n)], inc=sv)
                    c.dma("sync", yd[m * 128:(m + 1) * 128, t0:t0 + TC2],
                          yv[:], waits=[(sv, sv.n), (dzy, dzy.n)], inc=dzy)
            c.wait("sync", dzy, dzy.n)
            c.barrier()
    zgate_hist = {}
    with c.scope() as p4:
        sv, sa, sp = c.sem("sv"), c.sem("sa"), c.sem("sp")
        ybuf = c.sb_in(p4, c.fresh("ybuf1a"), [128, 16 * T], BF16)
        xsl4 = c.sb_in(p4, c.fresh("xsl41"), [128, ND * TB], F32)
        ab4 = c.sb_in(p4, c.fresh("ab41"), [128, 2 * TB], F32)
        tmp4 = c.sb_in(p4, c.fresh("tmp41"), [128, ND * TB], F32)
        ztl = c.sb_in(p4, c.fresh("ztl1a"), [128, 2 * T], BF16)
        c.dma("sync", ybuf[:].rearrange("p (mc t) -> p mc t", mc=16),
              yd.rearrange("(mc p) t -> p mc t", p=128), inc=dxs)
        yld = dxs.n
        dzl = c.sem("dzy")
        for mt in range(16):
            zpar = mt % 2
            zv = ztl[:, zpar * T:(zpar + 1) * T]
            wl_ = [(sv, zgate_hist[mt - 2])] if mt >= 2 else []
            c.dma("sync", zv, zd[mt * 128:(mt + 1) * 128, :],
                  waits=wl_, inc=dzl)
            c.op("vector", "tensor_tensor", ybuf[:, mt * T:(mt + 1) * T],
                 ybuf[:, mt * T:(mt + 1) * T], zv, ALU.mult,
                 waits=[(dzl, dzl.n), (dxs, yld)], inc=sv)
            zgate_hist[mt] = sv.n
        gate_done = sv.n
        seo = c.sem("seo1a")
        pp4 = c.ps_in(p4, c.fresh("pp1a4"), [128, 4 * TB], F32)
        wb4 = [c.sb_in(p4, c.fresh("wb1a4"), [128, 16 * 128], BF16)
               for _ in range(2)]
        wl4 = [c.sem("wl1a40"), c.sem("wl1a41")]
        for tb in range(NT):
            t0 = tb * TB
            c.load_x(xsl4, t0, t0 + TB, waits=[(dst_, dst_.n)], inc=dxs)
            c.dma("sync", ab4[:, 0:TB], AP(abd.tensor, t0, [[0, 128], [1, TB]]),
                  waits=[(sv, sv.n)], inc=dxs)
            c.dma("sync", ab4[:, TB:2 * TB],
                  AP(abd.tensor, T + t0, [[0, 128], [1, TB]]), inc=dxs)
            xld = dxs.n
            a3 = ab4[:, 0:TB].unsqueeze(1).broadcast_to([128, ND, TB])
            b3 = ab4[:, TB:2 * TB].unsqueeze(1).broadcast_to([128, ND, TB])
            x3 = xsl4[:].rearrange("p (dc t) -> p dc t", dc=ND)
            t3 = tmp4[:].rearrange("p (dc t) -> p dc t", dc=ND)
            c.op("vector", "tensor_tensor", t3, x3, a3, ALU.mult,
                 waits=[(dxs, xld)], inc=sv)
            c.op("vector", "tensor_tensor", t3, t3, b3, ALU.subtract, inc=sv)
            c.op("vector", "tensor_tensor", xsl4[:], xsl4[:], tmp4[:],
                 ALU.add, inc=sv)
            xn_add = sv.n

            def evac_o(mi, tbb, ps, mmc):
                d = mi // 128
                c.op("vector", "scalar_tensor_tensor",
                     xsl4[:, d * TB:(d + 1) * TB], ps, 1.0,
                     xsl4[:, d * TB:(d + 1) * TB], ALU.mult, ALU.add,
                     waits=[(sp, mmc), (sv, xn_add)], inc=seo)

            proj(c, p4, lambda k, tbb: ybuf[:, k * T + t0:k * T + t0 + TB],
                 W["L1_attn_W_out"], 2048, 0, D, 1, evac_o, "o1a",
                 evac_sem=seo, mblk=128,
                 first_waits=[(dxs, yld), (sv, gate_done), (seo, seo.n)],
                 pp=pp4, wbufs=wb4, wl_sems=wl4)
            c.store_x(xsl4, t0, t0 + TB, waits=[(seo, seo.n)], inc=dst_)
        c.wait("sync", dst_, dst_.n)
        c.barrier()



def st_l0f(c, W, inst=0):
    TCH = 1024
    NCH = T // TCH
    NB = TCH // TB
    NTT = T // 128
    KSEL = 8192
    sv, sa, sp = c.sem("sv"), c.sem("sa"), c.sem("sp")
    dxs, dst_ = c.sem("dxs2"), c.sem("dst")
    dcw = c.sem("dcw")
    cc = c.sem("cc_" + str(inst))
    ag_in = c.dram(c.fresh("agin"), [T, 8], F32)
    ag_out = c.dram(c.fresh("agout"), [NCORES * T, 8], F32)
    cwd = c.dram(c.fresh("cwd0f"), [8, T], BF16)
    with c.scope() as sc:
        rows_sc = c.scope()
        rows, rows_done = ln_rows(c, rows_sc, "l0f")
        abd = c.dram(c.fresh("abd0f"), [2 * T], F32)
        dab = c.sem("dab")
        c.dma("sync", AP(abd.tensor, 0, [[2 * T, 1], [1, 2 * T]]), rows[0:1, :],
              waits=[(sv, rows_done)], inc=dab)
        c.wait("sync", dab, dab.n)
        c.barrier()
        rows_sc.close()
        sel_sc = c.scope()
        pr_all = c.sb_in(sel_sc, c.fresh("prall"), [128, NTT * 8], F32)
        rw = c.sb_in(sel_sc, c.fresh("rw0f"), [128, 8 * ND], BF16)
        drw = c.sem("drw")
        load_w_tiles(c, sel_sc, W["L0_ffn_Wr"], D, 0, 8, "rw0", inc=drw, handle=rw)
        rw_done = drw.n
        with c.scope() as pA:
            ab_bc = c.sb_in(pA, c.fresh("abbc"), [128, 2 * TCH], F32)
            xsl = c.sb_in(pA, c.fresh("xsl"), [128, ND * TCH], F32)
            xn = c.sb_in(pA, c.fresh("xn"), [128, ND * TCH], BF16)
            nmx = c.sb_in(pA, c.fresh("nmx"), [128, 2], F32)
            psr = c.ps_in(pA, c.fresh("psr0"), [128, 8], F32)
            dwb = c.sem("dwb")
            for ch in range(NCH):
                t0 = ch * TCH
                c.dma("sync", ab_bc[:, 0:TCH],
                      AP(abd.tensor, t0, [[0, 128], [1, TCH]]),
                      waits=[(sv, sv.n)], inc=dwb)
                c.dma("sync", ab_bc[:, TCH:2 * TCH],
                      AP(abd.tensor, T + t0, [[0, 128], [1, TCH]]), inc=dwb)
                c.load_x(xsl, t0, t0 + TCH, inc=dxs)
                emit_xn_slice2(c, xn[:], xsl[:], ab_bc, TCH,
                               waits=[(dxs, dxs.n), (dwb, dwb.n), (sp, sp.n)])
                xn_done = sv.n
                for tt in range(TCH // 128):
                    ta = t0 // 128 + tt
                    for k in range(ND):
                        c.op("tensor", "matmul", psr[:],
                             xn[:, k * TCH + tt * 128: k * TCH + tt * 128 + 128],
                             rw[:, k * 8:(k + 1) * 8],
                             start=(k == 0), stop=(k == ND - 1),
                             waits=([(sv, xn_done), (drw, rw_done),
                                     (sv, sv.n)] if k == 0 else ()), inc=sp)
                    pv = pr_all[:, ta * 8:(ta + 1) * 8]
                    c.op("vector", "reduce_max", nmx[:, 0:1], psr[:],
                         axis=mybir.AxisListType.X, waits=[(sp, sp.n)], inc=sv)
                    c.op("vector", "tensor_scalar_mul", nmx[:, 0:1],
                         nmx[:, 0:1], -1.0, waits=[(sv, sv.n)], inc=sv)
                    c.op("scalar", "activation", pv, psr[:], AF.Exp,
                         bias=nmx[:, 0:1], accum_out=nmx[:, 1:2],
                         waits=[(sv, sv.n)], inc=sa)
                    c.op("vector", "reciprocal", nmx[:, 1:2], nmx[:, 1:2],
                         waits=[(sa, sa.n)], inc=sv)
                    c.op("vector", "tensor_scalar", pv, pv, nmx[:, 1:2], None,
                         ALU.mult, waits=[(sv, sv.n)], inc=sv)
            c.dma("sync", ag_in.rearrange("(tt p) e -> p tt e", p=128),
                  pr_all[:].rearrange("p (tt e) -> p tt e", tt=NTT),
                  waits=[(sv, sv.n)], inc=dab)
            c.op("gpsimd", "collective_compute",
                 "AllGather", ALU.bypass, [[0, 1, 2, 3, 4, 5, 6, 7]],
                 [ag_in.opt()], [ag_out.opt()],
                 waits=[(dab, dab.n)], inc=cc, inc_k=1)
            c.barrier()
        tau = c.sb_in(sel_sc, c.fresh("tau"), [128, 8], F32)
        with c.scope() as pB:
            sca = c.sb_in(pB, c.fresh("sca"), [128, 2048], F32)
            cmp_ = c.sb_in(pB, c.fresh("cmp"), [128, 2048], BF16)
            cnt = c.sb_in(pB, c.fresh("cnt"), [128, 8], F32)
            cntb = c.sb_in(pB, c.fresh("cntb"), [128, 8], BF16)
            lh = c.sb_in(pB, c.fresh("lh"), [1, 48], F32)
            psc = c.ps_in(pB, c.fresh("psc0"), [1, 8], F32)
            dtau = c.sem("dtau")
            c.wait("sync", cc, cc.n)
            c.dma("sync", sca[:],
                  ag_out.rearrange("(p k) e -> p (k e)", p=128), inc=dxs)
            sc_ld = dxs.n
            c.memset("vector", lh[:], 0.0)
            c.op("vector", "tensor_scalar", lh[0:1, 8:16], lh[0:1, 8:16],
                 0.0, 1.0, ALU.mult, ALU.add, inc=sv)
            for it in range(22):
                c.op("vector", "tensor_tensor", lh[0:1, 16:24], lh[0:1, 0:8],
                     lh[0:1, 8:16], ALU.add, waits=[(sv, sv.n)], inc=sv)
                c.op("vector", "tensor_scalar_mul", lh[0:1, 16:24],
                     lh[0:1, 16:24], 0.5, waits=[(sv, sv.n)], inc=sv)
                c.bounce_bcast(lh[0:1, 16:24], 8, tau[:], 128,
                               waits=[(sv, sv.n)], inc=dtau)
                t3 = tau[:].unsqueeze(1).broadcast_to([128, 256, 8])
                c.op("vector", "tensor_tensor",
                     cmp_[:].rearrange("p (k e) -> p k e", k=256),
                     sca[:].rearrange("p (k e) -> p k e", k=256), t3,
                     ALU.is_gt, waits=[(dtau, dtau.n), (dxs, sc_ld)], inc=sv)
                for e in range(8):
                    cv = cmp_[:].rearrange("p (k e) -> p k e", k=256)[:, :, e]
                    c.op("vector", "reduce_sum", cnt[:, e:e + 1], cv,
                         axis=mybir.AxisListType.X,
                         waits=[(sv, sv.n)] if e == 0 else (), inc=sv)
                c.op("vector", "tensor_copy", cntb[:], cnt[:],
                     waits=[(sv, sv.n)], inc=sv)
                c.op("tensor", "matmul", psc[:], c.consts["ones_col"][:, 0:1],
                     cntb[:], start=True, stop=True, waits=[(sv, sv.n)], inc=sp)
                c.op("vector", "tensor_single_scalar", lh[0:1, 24:32], psc[:],
                     float(KSEL), ALU.is_gt, waits=[(sp, sp.n)], inc=sv)
                c.op("vector", "tensor_sub", lh[0:1, 32:40], lh[0:1, 16:24],
                     lh[0:1, 0:8], waits=[(sv, sv.n)], inc=sv)
                c.op("vector", "tensor_mul", lh[0:1, 32:40], lh[0:1, 32:40],
                     lh[0:1, 24:32], waits=[(sv, sv.n)], inc=sv)
                c.op("vector", "tensor_add", lh[0:1, 0:8], lh[0:1, 0:8],
                     lh[0:1, 32:40], waits=[(sv, sv.n)], inc=sv)
                c.op("vector", "tensor_sub", lh[0:1, 40:48], lh[0:1, 8:16],
                     lh[0:1, 16:24], waits=[(sv, sv.n)], inc=sv)
                c.op("vector", "tensor_mul", lh[0:1, 40:48], lh[0:1, 40:48],
                     lh[0:1, 24:32], waits=[(sv, sv.n)], inc=sv)
                c.op("vector", "tensor_add", lh[0:1, 8:16], lh[0:1, 16:24],
                     lh[0:1, 40:48], waits=[(sv, sv.n)], inc=sv)
            c.bounce_bcast(lh[0:1, 8:16], 8, tau[:], 128,
                           waits=[(sv, sv.n)], inc=dtau)
            c.wait("sync", dtau, dtau.n)
            c.barrier()
        with c.scope() as pC:
            msk = c.sb_in(pC, c.fresh("msk"), [128, NTT * 8], F32)
            cwb = c.sb_in(pC, c.fresh("cwb0"), [128, 8], BF16)
            cwT = c.sb_in(pC, c.fresh("cwT0"), [8, T], BF16)
            pst = c.ps_in(pC, c.fresh("pst0"), [8, 128], F32)
            set_ = c.sem("set0f")
            t3 = tau[:].unsqueeze(1).broadcast_to([128, NTT, 8])
            c.op("vector", "tensor_tensor",
                 msk[:].rearrange("p (k e) -> p k e", k=NTT),
                 pr_all[:].rearrange("p (k e) -> p k e", k=NTT), t3,
                 ALU.is_gt, waits=[(sv, sv.n)], inc=sv)
            c.op("vector", "tensor_tensor", msk[:], msk[:], pr_all[:],
                 ALU.mult, waits=[(sv, sv.n)], inc=sv)
            for tt in range(NTT):
                c.op("vector", "tensor_copy", cwb[:],
                     msk[:, tt * 8:(tt + 1) * 8],
                     waits=[(sv, sv.n), (sp, sp.n)], inc=sv)
                c.op("tensor", "matmul", pst[:], cwb[:],
                     c.consts["identb"][:], start=True, stop=True,
                     waits=[(sv, sv.n), (set_, set_.n)], inc=sp)
                c.op("vector", "tensor_copy", cwT[:, tt * 128:(tt + 1) * 128],
                     pst[:], waits=[(sp, sp.n)], inc=set_)
            c.dma("sync", cwd[:, :], cwT[:], waits=[(set_, set_.n)], inc=dcw)
            c.wait("sync", dcw, dcw.n)
            c.barrier()
        sel_sc.close()
        with c.scope() as pD:
            ab_bc = c.sb_in(pD, c.fresh("abbc2"), [128, 2 * TCH], F32)
            xsl = c.sb_in(pD, c.fresh("xsl2"), [128, ND * TCH], F32)
            xn = c.sb_in(pD, c.fresh("xn2"), [128, ND * TCH], BF16)
            h = c.sb_in(pD, c.fresh("h0f"), [128, 32 * TCH], BF16)
            acc = c.sb_in(pD, c.fresh("acc0f"), [128, ND * TCH], F32)
            web = c.sb_in(pD, c.fresh("web0f"), [128, TCH], BF16)
            rtmp = c.sb_in(pD, c.fresh("rtmp0f"), [128, TB], F32)
            b1c = load_mat(c, pD, W["L0_ffn_b1c"], 128, 8 * 32, "b1c0", inc=dxs)
            b2r = load_row(c, pD, W["L0_ffn_b2r"], 8 * D, "b2r0", dtype=BF16,
                           inc=dxs)
            bias_done = dxs.n
            pp = c.ps_in(pD, c.fresh("pp0f"), [128, 4 * TB], F32)
            wbufs = [c.sb_in(pD, c.fresh("wb0f"), [128, 32 * 128], BF16)
                     for _ in range(2)]
            wl_sems = [c.sem("wl_l0f0"), c.sem("wl_l0f1")]
            seh = c.sem("seh0f")
            seo = c.sem("seo0f")
            dwb = c.sem("dwb")
            ones_row = c.consts["ones_row"]
            for ch in range(NCH):
                t0 = ch * TCH
                wl = [(dst_, dst_.n)] if ch else []
                c.dma("sync", ab_bc[:, 0:TCH],
                      AP(abd.tensor, t0, [[0, 128], [1, TCH]]),
                      waits=[(sv, sv.n)] + wl, inc=dwb)
                c.dma("sync", ab_bc[:, TCH:2 * TCH],
                      AP(abd.tensor, T + t0, [[0, 128], [1, TCH]]), inc=dwb)
                c.load_x(xsl, t0, t0 + TCH, inc=dxs)
                emit_xn_slice2(c, xn[:], xsl[:], ab_bc, TCH,
                               waits=[(dxs, dxs.n), (dwb, dwb.n), (sp, sp.n)])
                xn_done = sv.n
                for e in range(8):
                    c.dma("gpsimd", web[:],
                          AP(cwd.tensor, e * T + t0, [[0, 128], [1, TCH]]),
                          waits=[(dcw, dcw.n), (seo, seo.n)], inc=dcw)
                    web_done = dcw.n

                    def evac_h(mi, tb, ps, mmc, e=e):
                        m = mi // 128
                        c.op("scalar", "activation",
                             h[:, m * TCH + tb * TB: m * TCH + (tb + 1) * TB],
                             ps, AF.Gelu, bias=b1c[:, e * 32 + m: e * 32 + m + 1],
                             waits=[(sp, mmc), (dxs, bias_done)], inc=seh)

                    proj(c, pD, lambda k, tb: xn[:, k * TCH + tb * TB:
                                                 k * TCH + (tb + 1) * TB],
                         W["L0_ffn_W1f"], D, 0, 4096, NB, evac_h,
                         "w10", evac_sem=seh, w_row_lo=e * D, mblk=512,
                         first_waits=[(sv, xn_done), (seo, seo.n)],
                         pp=pp, wbufs=wbufs, wl_sems=wl_sems)

                    def evac_o(mi, tb, ps, mmc, e=e, web_done=web_done):
                        d = mi // 128
                        sl = slice(d * TCH + tb * TB, d * TCH + (tb + 1) * TB)
                        wsl = web[:, tb * TB:(tb + 1) * TB]
                        if e == 0:
                            c.op("vector", "tensor_tensor", acc[:, sl], ps,
                                 wsl, ALU.mult,
                                 waits=[(sp, mmc), (dcw, web_done)], inc=seo)
                        else:
                            c.op("vector", "tensor_tensor", rtmp[:], ps, wsl,
                                 ALU.mult, waits=[(sp, mmc), (dcw, web_done)],
                                 inc=sv)
                            c.op("vector", "tensor_tensor", acc[:, sl],
                                 acc[:, sl], rtmp[:], ALU.add, inc=seo)

                    proj(c, pD, lambda k, tb: h[:, k * TCH + tb * TB:
                                                k * TCH + (tb + 1) * TB],
                         W["L0_ffn_W2f"], 4096, 0, D, NB, evac_o, "w20",
                         evac_sem=seo, w_row_lo=e * 4096, mblk=128,
                         first_waits=[(seh, seh.n)],
                         pp=pp, wbufs=wbufs, wl_sems=wl_sems,
                         extra_k=(lambda mi, e=e: b2r[0:1, e * D + mi: e * D + mi + 128],
                                  lambda tb: ones_row[0:1, 0:TB]))
                for d in range(ND):
                    sl = slice(d * TCH, (d + 1) * TCH)
                    c.op("vector", "scalar_tensor_tensor", xsl[:, sl],
                         acc[:, sl], 0.5, xsl[:, sl], ALU.mult, ALU.add,
                         waits=[(seo, seo.n)], inc=sv)
                c.store_x(xsl, t0, t0 + TCH, waits=[(sv, sv.n)], inc=dst_)
            c.wait("sync", dst_, dst_.n)
            c.barrier()



def st_l3f(c, W, inst=0):
    TCH = 512
    NCH = T // TCH
    H = 16
    sv, sa, sp = c.sem("sv"), c.sem("sa"), c.sem("sp")
    dxs, dst_ = c.sem("dxs2"), c.sem("dst")
    dcw = c.sem("dcw")
    xnd = c.dram(c.fresh("xnd3f"), [D, T], BF16)
    kvd = c.dram(c.fresh("kvd3f"), [T, 512], BF16)
    yda = c.dram(c.fresh("yda3f"), [512, T], BF16)
    rd = c.dram(c.fresh("rd3f"), [D, T], F32)
    cwd = c.dram(c.fresh("cwd3f"), [8, T], BF16)
    with c.scope() as sc:
        rows_sc = c.scope()
        rows, rows_done = ln_rows(c, rows_sc, "l3f")
        abd = c.dram(c.fresh("abd3f"), [2 * T], F32)
        dab = c.sem("dab")
        c.dma("sync", AP(abd.tensor, 0, [[2 * T, 1], [1, 2 * T]]), rows[0:1, :],
              waits=[(sv, rows_done)], inc=dab)
        c.wait("sync", dab, dab.n)
        c.barrier()
        rows_sc.close()
        wln1 = load_mat(c, sc, W["L3_ffn_w_ln1c"], 128, ND, "wln1", inc=dxs)
        wln2 = load_mat(c, sc, W["L3_ffn_w_ln2c"], 128, ND, "wln2", inc=dxs)
        wln_done = dxs.n
        x1 = c.sb_in(sc, c.fresh("x13f"), [128, ND * T], BF16)
        with c.scope() as p1:
            ab_bc = c.sb_in(p1, c.fresh("abbc"), [128, 2 * TCH], F32)
            rrb = c.sb_in(p1, c.fresh("rrb"), [128, TCH], F32)
            rrow = c.sb_in(p1, c.fresh("rrow"), [1, TCH], F32)
            xsl = c.sb_in(p1, c.fresh("xsl"), [128, ND * TCH], F32)
            xn = c.sb_in(p1, c.fresh("xn"), [128, ND * TCH], BF16)
            psq = c.ps_in(p1, c.fresh("psq3f"), [1, TB], F32)
            dwb = c.sem("dwb")
            dxn = c.sem("dxn")
            ones_colf = c.consts["ones_colf"]
            for ch in range(NCH):
                t0 = ch * TCH
                c.dma("sync", ab_bc[:, 0:TCH],
                      AP(abd.tensor, t0, [[0, 128], [1, TCH]]),
                      waits=[(sv, sv.n), (dxn, dxn.n)], inc=dwb)
                c.dma("sync", ab_bc[:, TCH:2 * TCH],
                      AP(abd.tensor, T + t0, [[0, 128], [1, TCH]]), inc=dwb)
                c.load_x(xsl, t0, t0 + TCH, inc=dxs)
                emit_xn_slice2(c, xn[:], xsl[:], ab_bc, TCH,
                               waits=[(dxs, dxs.n), (dwb, dwb.n), (sp, sp.n)])
                xn_done = sv.n
                c.dma("sync", xnd.rearrange("(dc p) t -> p dc t", p=128)[:, :, t0:t0 + TCH],
                      xn[:].rearrange("p (dc t) -> p dc t", dc=ND),
                      waits=[(sv, xn_done)], inc=dxn)
                c.op("scalar", "activation", xsl[:], xn[:], AF.Square,
                     waits=[(sv, xn_done)], inc=sa)
                sq_done = sa.n
                for d in range(ND):
                    c.op("tensor", "matmul", psq[0:1, :], ones_colf[:, 0:1],
                         xsl[:, d * TCH:(d + 1) * TCH],
                         start=(d == 0), stop=(d == ND - 1),
                         waits=([(sa, sq_done), (sv, sv.n)] if d == 0 else ()),
                         inc=sp)
                c.op("vector", "tensor_scalar_mul", rrow[:], psq[0:1, :],
                     1.0 / D, waits=[(sp, sp.n)], inc=sv)
                c.op("scalar", "activation", rrow[:], rrow[:], AF.Sqrt,
                     bias=1e-5, waits=[(sv, sv.n)], inc=sa)
                c.op("vector", "reciprocal", rrow[:], rrow[:],
                     waits=[(sa, sa.n)], inc=sv)
                c.bounce_bcast(rrow[0:1, :], TCH, rrb[:], 128,
                               waits=[(sv, sv.n)], inc=dwb)
                rr_done = dwb.n
                for d in range(ND):
                    c.op("vector", "scalar_tensor_tensor",
                         x1[:, d * T + t0: d * T + t0 + TCH],
                         xn[:, d * TCH:(d + 1) * TCH], wln1[:, d:d + 1],
                         rrb[:, 0:TCH], ALU.mult, ALU.mult,
                         waits=[(dwb, rr_done), (sv, sv.n), (dxs, wln_done)],
                         inc=sv)
            c.wait("sync", dxn, dxn.n)
            c.barrier()
        kv_lat = c.sb_in(sc, c.fresh("kvl3f"), [128, T], BF16)
        krope = c.sb_in(sc, c.fresh("krp3f"), [32, T], BF16)
        with c.scope() as p2:
            pp = c.ps_in(p2, c.fresh("pp3f"), [128, 4 * TB], F32)
            wbufs = [c.sb_in(p2, c.fresh("wb3f"), [128, 8 * 512], BF16)
                     for _ in range(2)]
            wl_sems = [c.sem("wl_3f0"), c.sem("wl_3f1")]
            sek = c.sem("sek3f")
            dkv = c.sem("dkv3f")
            kvt = c.sb_in(p2, c.fresh("kvt3f"), [128, 4 * 512], BF16)

            def evac_ckv(mi, tb, ps, mmc):
                if mi == 0:
                    c.op("vector", "tensor_copy",
                         kv_lat[:, tb * TB:(tb + 1) * TB], ps,
                         waits=[(sp, mmc)], inc=sek)
                else:
                    c.op("vector", "tensor_copy",
                         krope[:, tb * TB:(tb + 1) * TB], ps[0:32, :],
                         waits=[(sp, mmc)], inc=sek)

            proj(c, p2, lambda k, tb: x1[:, k * T + tb * TB:k * T + (tb + 1) * TB],
                 W["L3_ffn_Wkva"], D, 0, 160, NT, evac_ckv, "kva",
                 evac_sem=sek, mblk=160,
                 first_waits=[(sv, sv.n)],
                 pp=pp, wbufs=wbufs, wl_sems=wl_sems)
            wkvb = c.sb_in(p2, c.fresh("wkvb"), [128, 512], BF16)
            c.dma("gpsimd", wkvb[:], W["L3_ffn_Wkvb"], inc=dkv)
            wkvb_done = dkv.n
            sv2 = c.sem("sev3f")
            for tb in range(NT):
                for tt in range(4):
                    par = tt % 2
                    ps = pp[:, par * TB:(par + 1) * TB]
                    c.op("tensor", "matmul", ps,
                         kv_lat[:, tb * TB + tt * 128: tb * TB + tt * 128 + 128],
                         wkvb[:], start=True, stop=True,
                         waits=[(sek, sek.n), (dkv, wkvb_done),
                                (sv2, max(0, sv2.n - 1))], inc=sp)
                    c.op("vector", "tensor_copy", kvt[:, tt * 512:(tt + 1) * 512],
                         ps, waits=[(sp, sp.n), (dkv, dkv.n)], inc=sv2)
                c.dma("sync", kvd[tb * TB:(tb + 1) * TB, :]
                      .rearrange("(tt p) dd -> p tt dd", p=128),
                      kvt[:].rearrange("p (tt dd) -> p tt dd", tt=4),
                      waits=[(sv2, sv2.n)], inc=dkv)
            c.wait("sync", dkv, dkv.n)
            c.barrier()
        with c.scope() as p3:
            cosx = c.sb_in(p3, c.fresh("cosx"), [128, T], BF16)
            sinx = c.sb_in(p3, c.fresh("sinx"), [128, T], BF16)
            qhp = c.sb_in(p3, c.fresh("qhp"), [128, T], BF16)
            khp = c.sb_in(p3, c.fresh("khp"), [128, T], BF16)
            qtmp = c.sb_in(p3, c.fresh("qtmp3"), [128, TB], BF16)
            qsh = c.sb_in(p3, c.fresh("qsh3"), [128, TB], BF16)
            prob = c.sb_in(p3, c.fresh("prob3"), [128, 3 * TB], BF16)
            vbuf = c.sb_in(p3, c.fresh("vbuf3"), [128, 4 * 4 * 64], BF16)
            rden = c.sb_in(p3, c.fresh("rden3"), [33, 2 * TB], F32)
            rdbc = c.sb_in(p3, c.fresh("rdbc3"), [32, 2 * TB], F32)
            ytmp = c.sb_in(p3, c.fresh("ytmp3"), [32, 2 * TB], BF16)
            trim = c.sb_in(p3, c.fresh("trim3"), [128, 4 * TB], BF16)
            pp = c.ps_in(p3, c.fresh("pp3f3"), [128, 2 * TB], F32)
            ps_sc = c.ps_in(p3, c.fresh("pssc3"), [128, 3 * TB], F32)
            ps_y = c.ps_in(p3, c.fresh("psy3"), [64, 2 * TB], F32)
            wbufs = [c.sb_in(p3, c.fresh("wb3f3"), [128, 8 * 128], BF16)
                     for _ in range(2)]
            wl_sems = [c.sem("wl_3f30"), c.sem("wl_3f31")]
            dq3 = c.sem("dq3f")
            dvv3 = [c.sem(f"dvv3f{i}") for i in range(4)]
            drd3 = c.sem("drd3f")
            it3 = 0
            it3h = {}
            vfree3 = {}
            exp3, pv3, msk3, qk3v = {}, {}, {}, {}
            vld_map, gvb_of = {}, {}
            vb_ctr = [0]
            g3b = [0]
            sdy = c.sem("sdy3f")
            seq = c.sem("seq3f")
            c.dma("gpsimd", trim[:], W["c_tri512"], inc=dq3)
            c.dma("gpsimd", cosx[:].rearrange("p t -> p t"), W["c_cos_l3f"],
                  inc=dq3)
            c.dma("gpsimd", sinx[:], W["c_ssin_l3f"], inc=dq3)
            tab_done = dq3.n
            c.memset("vector", vbuf[:], 0.0)
            vones3 = vbuf[:].rearrange("p (s dd) -> p s dd", s=16)[:, :, 32:33]
            c.op("vector", "memset", vones3, 1.0, inc=sv)
            vones_done = sv.n
            swap_mask = []
            for i in range(0, 32, 2):
                swap_mask += [i + 1, i]
            ones_col = c.consts["ones_col"]
            for hp in range(8):
                def evac_q(mi, tb, ps, mmc, dst=qhp):
                    c.op("vector", "tensor_copy", qtmp[:], ps,
                         waits=[(sp, mmc), (dq3, tab_done), (sp, sp.n)], inc=sv)
                    c.op("vector", "stream_shuffle", qsh[:], qtmp[:],
                         swap_mask, waits=[(sv, sv.n)], inc=sv)
                    sl = slice(tb * TB, (tb + 1) * TB)
                    c.op("vector", "tensor_tensor", dst[:, sl], qtmp[:],
                         cosx[:, sl], ALU.mult, waits=[(sv, sv.n)], inc=sv)
                    c.op("vector", "tensor_tensor", qsh[:], qsh[:],
                         sinx[:, sl], ALU.mult, waits=[(sv, sv.n)], inc=sv)
                    c.op("vector", "tensor_tensor", dst[:, sl], dst[:, sl],
                         qsh[:], ALU.add, waits=[(sv, sv.n)], inc=seq)

                proj(c, p3, lambda k, tb: x1[:, k * T + tb * TB:k * T + (tb + 1) * TB],
                     W["L3_ffn_Wq"], D, hp * 128, (hp + 1) * 128, NT, evac_q,
                     "q3f", evac_sem=seq, mblk=128,
                     first_waits=[(seq, seq.n), (sv, sv.n)],
                     pp=pp, wbufs=wbufs, wl_sems=wl_sems)
                if hp == 0:
                    wkp = c.sb_in(p3, c.fresh("wkp"), [128, 128], BF16)
                c.dma("gpsimd", wkp[:],
                      W["L3_ffn_Wkvb_pad"][:, hp * 128:(hp + 1) * 128],
                      waits=[(sp, sp.n)], inc=dq3)
                wkp_done = dq3.n
                sv3 = c.sem("sv")
                for tb in range(NT):
                    par = tb % 2
                    ps = pp[:, par * TB:(par + 1) * TB]
                    c.op("tensor", "matmul", ps, wkp[:],
                         kv_lat[:, tb * TB:(tb + 1) * TB],
                         start=True, stop=True,
                         waits=[(dq3, wkp_done), (seq, seq.n),
                                (sv, max(0, sv.n - 2))], inc=sp)
                    sl = slice(tb * TB, (tb + 1) * TB)
                    c.op("vector", "tensor_copy", qtmp[:], ps,
                         waits=[(sp, sp.n)], inc=sv)
                    c.op("vector", "tensor_copy", qtmp[32:64, :],
                         krope[:, sl], waits=[(sv, sv.n)], inc=sv)
                    c.op("vector", "tensor_copy", qtmp[96:128, :],
                         krope[:, sl], waits=[(sv, sv.n)], inc=sv)
                    c.op("vector", "stream_shuffle", qsh[:], qtmp[:],
                         swap_mask, waits=[(sv, sv.n)], inc=sv)
                    c.op("vector", "tensor_tensor", khp[:, sl], qtmp[:],
                         cosx[:, sl], ALU.mult, waits=[(sv, sv.n)], inc=sv)
                    c.op("vector", "tensor_tensor", qsh[:], qsh[:],
                         sinx[:, sl], ALU.mult, waits=[(sv, sv.n)], inc=sv)
                    c.op("vector", "tensor_tensor", khp[:, sl], khp[:, sl],
                         qsh[:], ALU.add, waits=[(sv, sv.n)], inc=sv)
                kq_done = sv.n
                its_list = [(hh2, tqb2) for hh2 in range(2)
                            for tqb2 in range(NT)]
                gsteps = []
                for li, (hh2, tqb2) in enumerate(its_list):
                    for tkb2 in range(4 * tqb2 + 4):
                        gsteps.append((li, hh2, tqb2, tkb2))
                it_base = it3

                def emit_vload(li, kB):
                    hh2, tqb2 = its_list[li]
                    h = hp * 2 + hh2
                    gvb = vb_ctr[0]
                    vb_ctr[0] += 1
                    gvb_of[(it_base + li, kB)] = gvb
                    kslot = gvb % 4
                    vw = []
                    if gvb - 4 in vfree3:
                        vw.append((sp, vfree3[gvb - 4]))
                    if gvb == 0:
                        vw.append((sv, vones_done))
                    c.dma("sync",
                          vbuf[:, kslot * 256:(kslot + 1) * 256]
                          .rearrange("p (tt dd) -> p tt dd", tt=4)[:, :, 0:32],
                          kvd[kB * 512:(kB + 1) * 512, h * 32:(h + 1) * 32]
                          .rearrange("(tt p) dd -> p tt dd", p=128),
                          waits=vw, inc=dvv3[kslot])
                    vld_map[gvb] = dvv3[kslot].n

                def emit_qk3(gi):
                    li, hh2, tqb2, tkb = gsteps[gi]
                    g3 = g3b[0] + gi
                    base = hh2 * 64
                    q0 = tqb2 * TB
                    kB, sub = divmod(tkb, 4)
                    if sub == 0:
                        emit_vload(li, kB)
                    sl2 = g3 % 3
                    scs = ps_sc[:, sl2 * TB:(sl2 + 1) * TB]
                    w = [(sv, kq_done)]
                    if g3 - 3 in exp3:
                        w.append((sa, exp3[g3 - 3]))
                    t0k = tkb * 128
                    c.op("tensor", "matmul", scs,
                         khp[base:base + 64, t0k:t0k + 128],
                         qhp[base:base + 64, q0:q0 + TB],
                         start=True, stop=True, waits=w, inc=sp)
                    qk3v[g3] = sp.n
                    pbv = prob[:, sl2 * TB:(sl2 + 1) * TB]
                    ew = [(sp, qk3v[g3]), (dq3, tab_done)]
                    if g3 - 3 in pv3:
                        ew.append((sp, pv3[g3 - 3]))
                    c.op("scalar", "activation", pbv, scs, AF.Exp,
                         scale=1.0 / 8.0, waits=ew, inc=sa)
                    exp3[g3] = sa.n
                    if t0k + 128 > q0:
                        moff = (t0k - q0) // 128 * TB
                        c.op("vector", "tensor_tensor", pbv, pbv,
                             trim[:, moff:moff + TB], ALU.mult,
                             waits=[(sa, exp3[g3]), (dq3, tab_done)],
                             inc=sv)
                        msk3[g3] = sv.n

                def emit_tail3(itid, hh2, tqb2):
                    h = hp * 2 + hh2
                    q0 = tqb2 * TB
                    ipp = itid % 2
                    av_done = sp.n
                    rdv = rden[32:33, ipp * TB:(ipp + 1) * TB]
                    rbv = rdbc[0:32, ipp * TB:(ipp + 1) * TB]
                    ytv = ytmp[0:32, ipp * TB:(ipp + 1) * TB]
                    h2 = it3h.get(itid - 2, {})
                    c.op("vector", "reciprocal", rdv,
                         ps_y[32:33, ipp * TB:(ipp + 1) * TB],
                         waits=[(sp, av_done), (drd3, h2.get("rddone", 0))],
                         inc=sv)
                    c.bounce_bcast(rdv, TB, rbv, 32, waits=[(sv, sv.n)],
                                   waits2=[(sv, h2.get("ytdone", 0))],
                                   inc=drd3)
                    rd_done = drd3.n
                    c.op("vector", "tensor_tensor", ytv,
                         ps_y[0:32, ipp * TB:(ipp + 1) * TB], rbv, ALU.mult,
                         waits=[(drd3, rd_done), (sdy, h2.get("stdone", 0))],
                         inc=sv)
                    yt_done = sv.n
                    c.dma("sync", yda[h * 32:(h + 1) * 32, q0:q0 + TB],
                          ytv, waits=[(sv, yt_done)], inc=sdy)
                    it3h[itid] = {"ytdone": yt_done, "stdone": sdy.n,
                                  "rddone": rd_done}

                def emit_pv3(gi):
                    li, hh2, tqb2, tkb = gsteps[gi]
                    g3 = g3b[0] + gi
                    itid = it_base + li
                    ntk = 4 * tqb2 + 4
                    kB, sub = divmod(tkb, 4)
                    gvb = gvb_of[(itid, kB)]
                    kslot = gvb % 4
                    sl2 = g3 % 3
                    ipp = itid % 2
                    pbv = prob[:, sl2 * TB:(sl2 + 1) * TB]
                    pw = (sv, msk3[g3]) if g3 in msk3 else (sa, exp3[g3])
                    w_av = [pw, (dvv3[kslot], vld_map[gvb])]
                    if tkb == 0:
                        h2 = it3h.get(itid - 2, {})
                        w_av.append((sv, h2.get("ytdone", 0)))
                    c.op("tensor", "matmul",
                         ps_y[:, ipp * TB:(ipp + 1) * TB],
                         vbuf[:, kslot * 256 + sub * 64:
                              kslot * 256 + sub * 64 + 64],
                         pbv, start=(tkb == 0), stop=(tkb == ntk - 1),
                         waits=w_av, inc=sp)
                    pv3[g3] = sp.n
                    if sub == 3:
                        vfree3[gvb] = sp.n
                    if tkb == ntk - 1:
                        emit_tail3(itid, hh2, tqb2)

                Gh = len(gsteps)
                for gi in range(Gh + 2):
                    if gi < Gh:
                        emit_qk3(gi)
                    if gi >= 2:
                        emit_pv3(gi - 2)
                g3b[0] += Gh
                it3 = it_base + len(its_list)
            c.wait("sync", sdy, sdy.n)
            c.barrier()
        with c.scope() as p4:
            ybuf = c.sb_in(p4, c.fresh("ybuf3"), [128, 4 * T], BF16)
            xnb = c.sb_in(p4, c.fresh("xnb3"), [128, ND * TB], BF16)
            rsl = c.sb_in(p4, c.fresh("rsl3"), [128, ND * TB], F32)
            pp4 = c.ps_in(p4, c.fresh("pp3f4"), [128, 4 * TB], F32)
            wb4 = [c.sb_in(p4, c.fresh("wb3f4"), [128, 4 * 512], BF16)
                   for _ in range(2)]
            wl4 = [c.sem("wl_3f40"), c.sem("wl_3f41")]
            seo = c.sem("seo3f")
            c.dma("sync", ybuf[:].rearrange("p (mc t) -> p mc t", mc=4),
                  yda.rearrange("(mc p) t -> p mc t", p=128), inc=dxs)
            yld = dxs.n
            for tb in range(NT):
                t0 = tb * TB
                c.dma("sync", xnb[:].rearrange("p (dc t) -> p dc t", dc=ND),
                      xnd.rearrange("(dc p) t -> p dc t", p=128)[:, :, t0:t0 + TB],
                      waits=[(seo, seo.n)], inc=dxs)
                xld = dxs.n

                def evac_ro(mi, tbb, ps, mmc):
                    d = mi // 128
                    sl = slice(d * TB, (d + 1) * TB)
                    c.op("vector", "tensor_tensor", rsl[:, sl],
                         xnb[:].rearrange("p (dc t) -> p dc t", dc=ND)[:, d],
                         ps, ALU.add, waits=[(sp, mmc), (dxs, xld)], inc=seo)

                proj(c, p4, lambda k, tbb: ybuf[:, k * T + t0:k * T + t0 + TB],
                     W["L3_ffn_Wo_nope"], 512, 0, D, 1, evac_ro, "o3f",
                     evac_sem=seo,
                     first_waits=[(dxs, yld), (seo, seo.n)],
                     pp=pp4, wbufs=wb4, wl_sems=wl4)
                c.dma("sync",
                      rd.rearrange("(dc p) t -> p dc t", p=128)[:, :, t0:t0 + TB],
                      rsl[:].rearrange("p (dc t) -> p dc t", dc=ND),
                      waits=[(seo, seo.n)], inc=dxs)
            c.wait("sync", dxs, dxs.n)
            c.barrier()
    with c.scope() as p5:
        sv, sa, sp = c.sem("sv"), c.sem("sa"), c.sem("sp")
        dxs = c.sem("dxs2")
        r2row = c.sb_in(p5, c.fresh("r2row"), [1, T], F32)
        rsl = c.sb_in(p5, c.fresh("rsl5"), [128, ND * TB], F32)
        rq = c.sb_in(p5, c.fresh("rq5"), [128, ND * TB], F32)
        psq = c.ps_in(p5, c.fresh("psq5"), [1, TB], F32)
        ones_colf = c.consts["ones_colf"]
        r2d = c.dram(c.fresh("r2d3f"), [T], F32)
        dr2 = c.sem("dr2")
        for tb in range(NT):
            t0 = tb * TB
            c.dma("sync", rsl[:].rearrange("p (dc t) -> p dc t", dc=ND),
                  rd.rearrange("(dc p) t -> p dc t", p=128)[:, :, t0:t0 + TB],
                  waits=[(sa, sa.n)], inc=dxs)
            c.op("scalar", "activation", rq[:], rsl[:], AF.Square,
                 waits=[(dxs, dxs.n), (sp, sp.n)], inc=sa)
            sq_done = sa.n
            for d in range(ND):
                c.op("tensor", "matmul", psq[0:1, :], ones_colf[:, 0:1],
                     rq[:, d * TB:(d + 1) * TB],
                     start=(d == 0), stop=(d == ND - 1),
                     waits=([(sa, sq_done), (sv, sv.n)] if d == 0 else ()),
                     inc=sp)
            seg = r2row[0:1, t0:t0 + TB]
            c.op("vector", "tensor_scalar_mul", seg, psq[0:1, :], 1.0 / D,
                 waits=[(sp, sp.n)], inc=sv)
            c.op("scalar", "activation", seg, seg, AF.Sqrt, bias=1e-5,
                 waits=[(sv, sv.n)], inc=sa)
            c.op("vector", "reciprocal", seg, seg, waits=[(sa, sa.n)], inc=sv)
        c.dma("sync", AP(r2d.tensor, 0, [[T, 1], [1, T]]), r2row[0:1, :],
              waits=[(sv, sv.n)], inc=dr2)
        c.wait("sync", dr2, dr2.n)
        c.barrier()
    with c.scope() as p6:
        sv, sa, sp = c.sem("sv"), c.sem("sa"), c.sem("sp")
        dxs, dst_ = c.sem("dxs2"), c.sem("dst")
        dcw = c.sem("dcw")
        TC6 = 1024
        NC6 = T // TC6
        NB6 = TC6 // TB
        NTT = TC6 // 128
        wln2 = load_mat(c, p6, W["L3_ffn_w_ln2c"], 128, ND, "wln2b", inc=dxs)
        gw = c.sb_in(p6, c.fresh("gw3f"), [128, 8 * ND], BF16)
        ebr = load_row(c, p6, W["L3_ffn_expert_bias"], 8, "ebr", dtype=BF16,
                       queue="gpsimd", inc=dxs)
        load_w_tiles(c, p6, W["L3_ffn_Wgate"], D, 0, 8, "gw3", inc=dxs,
                     handle=gw)
        gate_done = dxs.n
        rr2b = c.sb_in(p6, c.fresh("rr2b"), [128, TC6], F32)
        rsl = c.sb_in(p6, c.fresh("rsl6"), [128, ND * TC6], F32)
        x2 = c.sb_in(p6, c.fresh("x26"), [128, ND * TC6], BF16)
        xsl = c.sb_in(p6, c.fresh("xsl6"), [128, ND * TC6], F32)
        gsb = c.sb_in(p6, c.fresh("gsb"), [128, 4 * TC6], BF16)
        usb = c.sb_in(p6, c.fresh("usb"), [128, 4 * TC6], BF16)
        acc = c.sb_in(p6, c.fresh("acc6"), [128, ND * TC6], F32)
        web = c.sb_in(p6, c.fresh("web6"), [128, TC6], BF16)
        rtmp = c.sb_in(p6, c.fresh("rtmp6"), [128, TB], F32)
        probs = c.sb_in(p6, c.fresh("probs6"), [128, NTT * 8], F32)
        cwb = c.sb_in(p6, c.fresh("cwb6"), [128, NTT * 8], BF16)
        cwT = c.sb_in(p6, c.fresh("cwT6"), [8, TC6], BF16)
        g1 = c.sb_in(p6, c.fresh("g16"), [128, 8], F32)
        g2 = c.sb_in(p6, c.fresh("g26"), [128, 8], F32)
        gm = c.sb_in(p6, c.fresh("gm6"), [128, 4], F32)
        nmx = c.sb_in(p6, c.fresh("nmx6"), [128, 2], F32)
        pp = c.ps_in(p6, c.fresh("pp6"), [128, 4 * TB], F32)
        psr = c.ps_in(p6, c.fresh("psr6"), [128, 8], F32)
        pst = c.ps_in(p6, c.fresh("pst6"), [8, 128], F32)
        wbufs = [c.sb_in(p6, c.fresh("wb6"), [128, 8 * 512], BF16)
                 for _ in range(2)]
        wl_sems = [c.sem("wl_3f60"), c.sem("wl_3f61")]
        seh = c.sem("seh3f")
        seu = c.sem("seu3f")
        seo = c.sem("seo3f6")
        set_ = c.sem("set3f")
        identb = c.consts["identb"]
        ones_row = c.consts["ones_row"]
        for ch in range(NC6):
            t0 = ch * TC6
            wl = [(dst_, dst_.n)] if ch else []
            c.dma("sync", rr2b[:], AP(r2d.tensor, t0, [[0, 128], [1, TC6]]),
                  waits=[(sv, sv.n)] + wl, inc=dxs)
            c.dma("sync", rsl[:].rearrange("p (dc t) -> p dc t", dc=ND),
                  rd.rearrange("(dc p) t -> p dc t", p=128)[:, :, t0:t0 + TC6],
                  inc=dxs)
            c.load_x(xsl, t0, t0 + TC6, inc=dxs)
            rld = dxs.n
            for d in range(ND):
                c.op("vector", "scalar_tensor_tensor",
                     x2[:, d * TC6:(d + 1) * TC6],
                     rsl[:, d * TC6:(d + 1) * TC6], wln2[:, d:d + 1],
                     rr2b[:], ALU.mult, ALU.mult,
                     waits=[(dxs, rld), (dxs, gate_done), (sp, sp.n)],
                     inc=sv)
            x2_done = sv.n
            for tt in range(NTT):
                for k in range(ND):
                    c.op("tensor", "matmul", psr[:],
                         x2[:, k * TC6 + tt * 128: k * TC6 + tt * 128 + 128],
                         gw[:, k * 8:(k + 1) * 8],
                         start=(k == 0), stop=False,
                         waits=([(sv, x2_done), (sv, sv.n)] if k == 0 else ()),
                         inc=sp)
                c.op("tensor", "matmul", psr[:],
                     ones_row[0:1, 0:128], ebr[0:1, 0:8],
                     start=False, stop=True, inc=sp)
                pv = probs[:, tt * 8:(tt + 1) * 8]
                c.op("vector", "reduce_max", nmx[:, 0:1], psr[:],
                     axis=mybir.AxisListType.X, waits=[(sp, sp.n)], inc=sv)
                c.op("vector", "tensor_scalar_mul", nmx[:, 0:1], nmx[:, 0:1],
                     -1.0, waits=[(sv, sv.n)], inc=sv)
                c.op("scalar", "activation", pv, psr[:], AF.Exp,
                     bias=nmx[:, 0:1], accum_out=nmx[:, 1:2],
                     waits=[(sv, sv.n)], inc=sa)
                c.op("vector", "reciprocal", nmx[:, 1:2], nmx[:, 1:2],
                     waits=[(sa, sa.n)], inc=sv)
                c.op("vector", "tensor_scalar", pv, pv, nmx[:, 1:2], None,
                     ALU.mult, waits=[(sv, sv.n)], inc=sv)
                c.op("vector", "reduce_max", gm[:, 0:1], pv,
                     axis=mybir.AxisListType.X, waits=[(sv, sv.n)], inc=sv)
                c.op("vector", "tensor_scalar", g1[:, 0:8], pv, gm[:, 0:1],
                     None, ALU.is_equal, waits=[(sv, sv.n)], inc=sv)
                c.op("vector", "scalar_tensor_tensor", g2[:, 0:8], g1[:, 0:8],
                     -1e30, pv, ALU.mult, ALU.add, waits=[(sv, sv.n)], inc=sv)
                c.op("vector", "reduce_max", gm[:, 1:2], g2[:, 0:8],
                     axis=mybir.AxisListType.X, waits=[(sv, sv.n)], inc=sv)
                c.op("vector", "tensor_scalar", g2[:, 0:8], g2[:, 0:8],
                     gm[:, 1:2], None, ALU.is_equal, waits=[(sv, sv.n)], inc=sv)
                c.op("vector", "tensor_tensor", gm[:, 2:3], gm[:, 0:1],
                     gm[:, 1:2], ALU.add, waits=[(sv, sv.n)], inc=sv)
                c.op("vector", "reciprocal", gm[:, 3:4], gm[:, 2:3],
                     waits=[(sv, sv.n)], inc=sv)
                c.op("vector", "tensor_tensor", gm[:, 0:1], gm[:, 0:1],
                     gm[:, 3:4], ALU.mult, waits=[(sv, sv.n)], inc=sv)
                c.op("vector", "tensor_tensor", gm[:, 1:2], gm[:, 1:2],
                     gm[:, 3:4], ALU.mult, waits=[(sv, sv.n)], inc=sv)
                c.op("vector", "tensor_scalar", g1[:, 0:8], g1[:, 0:8],
                     gm[:, 0:1], None, ALU.mult, waits=[(sv, sv.n)], inc=sv)
                c.op("vector", "tensor_scalar", g2[:, 0:8], g2[:, 0:8],
                     gm[:, 1:2], None, ALU.mult, waits=[(sv, sv.n)], inc=sv)
                c.op("vector", "tensor_tensor", cwb[:, tt * 8:(tt + 1) * 8],
                     g1[:, 0:8], g2[:, 0:8], ALU.add, waits=[(sv, sv.n)],
                     inc=sv)
                c.op("tensor", "matmul", pst[:], cwb[:, tt * 8:(tt + 1) * 8],
                     identb[:], start=True, stop=True,
                     waits=[(sv, sv.n), (set_, set_.n)], inc=sp)
                c.op("vector", "tensor_copy", cwT[:, tt * 128:(tt + 1) * 128],
                     pst[:], waits=[(sp, sp.n), (dcw, dcw.n)], inc=set_)
            c.dma("sync", cwd[:, t0:t0 + TC6], cwT[:],
                  waits=[(set_, set_.n), (dcw, dcw.n)], inc=dcw)
            cw_st = dcw.n
            for ei in [8] + list(range(8)):
                shared = (ei == 8)
                e = 0 if shared else ei
                if not shared:
                    c.dma("gpsimd", web[:],
                          AP(cwd.tensor, e * T + t0, [[0, 128], [1, TC6]]),
                          waits=[(dcw, cw_st), (seo, seo.n)], inc=dcw)
                web_done = dcw.n
                wg = "L3_ffn_sWgf" if shared else "L3_ffn_Wgf"
                wu = "L3_ffn_sWuf" if shared else "L3_ffn_Wuf"
                wd = "L3_ffn_sWdf" if shared else "L3_ffn_Wdf"
                rlo = 0 if shared else e * D
                rlo2 = 0 if shared else e * 512

                def evac_g(mi, tb, ps, mmc):
                    m = mi // 128
                    c.op("scalar", "activation",
                         gsb[:, m * TC6 + tb * TB: m * TC6 + (tb + 1) * TB],
                         ps, AF.Silu, waits=[(sp, mmc)], inc=seh)

                def evac_u(mi, tb, ps, mmc):
                    m = mi // 128
                    c.op("scalar", "activation",
                         usb[:, m * TC6 + tb * TB: m * TC6 + (tb + 1) * TB],
                         ps, AF.Identity, waits=[(sp, mmc)], inc=seu)

                proj(c, p6, lambda k, tb: x2[:, k * TC6 + tb * TB:
                                             k * TC6 + (tb + 1) * TB],
                     W[wg], D, 0, 512, NB6, evac_g, "wg6", evac_sem=seh,
                     w_row_lo=rlo, first_waits=[(sv, x2_done), (seo, seo.n)],
                     pp=pp, wbufs=wbufs, wl_sems=wl_sems)
                proj(c, p6, lambda k, tb: x2[:, k * TC6 + tb * TB:
                                             k * TC6 + (tb + 1) * TB],
                     W[wu], D, 0, 512, NB6, evac_u, "wu6", evac_sem=seu,
                     w_row_lo=rlo, first_waits=[(seh, seh.n)],
                     pp=pp, wbufs=wbufs, wl_sems=wl_sems)
                c.op("vector", "tensor_tensor", gsb[:], gsb[:], usb[:],
                     ALU.mult, waits=[(seh, seh.n), (seu, seu.n)], inc=sv)
                gu_done = sv.n

                def evac_y(mi, tb, ps, mmc, shared=shared, web_done=web_done,
                           ei=ei):
                    d = mi // 128
                    sl = slice(d * TC6 + tb * TB, d * TC6 + (tb + 1) * TB)
                    if shared:
                        c.op("vector", "tensor_copy", acc[:, sl], ps,
                             waits=[(sp, mmc)], inc=seo)
                    else:
                        wsl = web[:, tb * TB:(tb + 1) * TB]
                        c.op("vector", "tensor_tensor", rtmp[:], ps, wsl,
                             ALU.mult, waits=[(sp, mmc), (dcw, web_done)],
                             inc=sv)
                        c.op("vector", "tensor_tensor", acc[:, sl],
                             acc[:, sl], rtmp[:], ALU.add, inc=seo)

                proj(c, p6, lambda k, tb: gsb[:, k * TC6 + tb * TB:
                                              k * TC6 + (tb + 1) * TB],
                     W[wd], 512, 0, D, NB6, evac_y, "wd6", evac_sem=seo,
                     w_row_lo=rlo2, mblk=512,
                     first_waits=[(sv, gu_done)],
                     pp=pp, wbufs=wbufs, wl_sems=wl_sems)
            for d in range(ND):
                sl = slice(d * TC6, (d + 1) * TC6)
                c.op("vector", "scalar_tensor_tensor", xsl[:, sl], acc[:, sl],
                     0.5, xsl[:, sl], ALU.mult, ALU.add,
                     waits=[(seo, seo.n)], inc=sv)
                c.op("vector", "scalar_tensor_tensor", xsl[:, sl],
                     rsl[:, sl], 0.5, xsl[:, sl], ALU.mult, ALU.add,
                     waits=[(sv, sv.n)], inc=sv)
            c.store_x(xsl, t0, t0 + TC6, waits=[(sv, sv.n)], inc=dst_)
        c.wait("sync", dst_, dst_.n)
        c.barrier()



def _np(x):
    return np.ascontiguousarray(np.asarray(x, dtype=np.float32))


BLOCK_SPECS = {
    "L1_ffn_w1": (1024, 512), "L1_ffn_w2": (4096, 128),
    "L3_attn_wq": (1024, 512), "L3_attn_wk": (1024, 512),
    "L3_attn_wv": (1024, 512),
    "L2_attn_w_qkv": (1024, 512), "L2_attn_w_out": (1024, 512),
    "L2_ffn_w1": (1024, 512), "L2_ffn_w2": (1024, 512),
    "L0_attn_in_proj_w": (1024, 512), "L0_attn_dt_w": (2048, 128),
    "L0_attn_b_w": (2048, 128), "L0_attn_c_w": (2048, 128),
    "L0_attn_out_w": (2048, 128),
    "L0_ffn_W1f": (1024, 512), "L0_ffn_W2f": (4096, 128),
    "L1_attn_W_in": (1024, 256), "L1_attn_W_xp": (2048, 64),
    "L1_attn_W_out": (2048, 128),
    "L3_ffn_Wq": (1024, 128), "L3_ffn_Wkva": (1024, 160),
    "L3_ffn_Wo_nope": (512, 512),
    "L3_ffn_Wgf": (1024, 512), "L3_ffn_Wuf": (1024, 512),
    "L3_ffn_Wdf": (512, 512), "L3_ffn_sWgf": (1024, 512),
    "L3_ffn_sWuf": (1024, 512), "L3_ffn_sWdf": (512, 512),
}


def _blockify(arr, kin, mblk):
    rows, ncols = arr.shape
    nrg = rows // kin
    kc = kin // 128
    nmb = ncols // mblk
    out = np.empty((nrg * nmb, 128, kc * mblk), np.float32)
    for rg in range(nrg):
        for mb in range(nmb):
            blk = arr[rg * kin:(rg + 1) * kin, mb * mblk:(mb + 1) * mblk]
            out[rg * nmb + mb] = blk.reshape(kc, 128, mblk) \
                .transpose(1, 0, 2).reshape(128, kc * mblk)
    return np.ascontiguousarray(out)


def host_prep(inputs):
    W = {k: _np(v) for k, v in inputs.items()}
    for k in ["L0_attn_in_proj_w", "L0_attn_dt_w", "L0_attn_b_w", "L0_attn_c_w",
              "L0_attn_out_w", "L2_ffn_router_w", "L3_attn_wq", "L3_attn_wk",
              "L3_attn_wv"]:
        W[k] = np.ascontiguousarray(W[k].T)
    for k in ["L0_attn_conv_w", "L1_attn_conv_w"]:
        W[k] = np.ascontiguousarray(W[k].reshape(W[k].shape[0], -1))
    for k in ["L1_ffn_b1"]:
        W[k + "c"] = np.ascontiguousarray(W[k].reshape(-1, 128).T)
    for k in ["L0_attn_in_proj_b", "L0_attn_conv_b", "L0_attn_dt_b",
              "L0_attn_A_log", "L0_attn_D_param"]:
        W[k + "c"] = np.ascontiguousarray(W[k].reshape(-1, 128).T)
    for pre in ["L0_attn", "L1_attn"]:
        cw = W[pre + "_conv_w"].reshape(16, 128, 4).transpose(1, 0, 2)
        W[pre + "_conv_wc"] = np.ascontiguousarray(cw.reshape(128, 64))
    for k in ["L1_attn_conv_b", "L1_attn_w_dt", "L1_attn_b_dt",
              "L1_attn_D_param"]:
        W[k + "c"] = np.ascontiguousarray(W[k].reshape(-1, 128).T)
    W["L1_attn_norm_wc"] = np.ascontiguousarray(
        W["L1_attn_norm_w"].reshape(-1, 128).T)
    al = W["L1_attn_A_log"].reshape(16, 128, 16).transpose(1, 0, 2)
    W["L1_attn_A_logc"] = np.ascontiguousarray(al.reshape(128, 256))
    wxp = np.zeros((2048, 64), np.float32)
    wxp[:, 0:33] = W["L1_attn_W_x"]
    W["L1_attn_W_xp"] = wxp
    W["L0_ffn_W1f"] = np.ascontiguousarray(W["L0_ffn_W1"].reshape(8 * 1024, 4096))
    W["L0_ffn_W2f"] = np.ascontiguousarray(W["L0_ffn_W2"].reshape(8 * 4096, 1024))
    b1c = np.stack([W["L0_ffn_b1"][e].reshape(32, 128).T for e in range(8)], 1)
    W["L0_ffn_b1c"] = np.ascontiguousarray(b1c.reshape(128, 8 * 32))
    W["L0_ffn_b2r"] = np.ascontiguousarray(W["L0_ffn_b2"].reshape(-1))
    for k in ["L3_ffn_w_ln1", "L3_ffn_w_ln2"]:
        W[k + "c"] = np.ascontiguousarray(W[k].reshape(-1, 128).T)
    kvb = W["L3_ffn_Wkvb"]
    pad = np.zeros((128, 8 * 128), np.float32)
    for hp in range(8):
        pad[:, hp * 128 + 0:hp * 128 + 32] = kvb[:, (2 * hp) * 32:(2 * hp + 1) * 32]
        pad[:, hp * 128 + 64:hp * 128 + 96] = kvb[:, (2 * hp + 1) * 32:(2 * hp + 2) * 32]
    W["L3_ffn_Wkvb_pad"] = pad
    wo = W["L3_ffn_Wo"]
    W["L3_ffn_Wo_nope"] = np.ascontiguousarray(
        wo.reshape(16, 64, 1024)[:, 0:32, :].reshape(512, 1024))
    W["L3_ffn_Wgf"] = np.ascontiguousarray(W["L3_ffn_Wg"].reshape(8 * 1024, 512))
    W["L3_ffn_Wuf"] = np.ascontiguousarray(W["L3_ffn_Wu"].reshape(8 * 1024, 512))
    W["L3_ffn_Wdf"] = np.ascontiguousarray(W["L3_ffn_Wd"].reshape(8 * 512, 1024))
    W["L3_ffn_sWgf"] = np.ascontiguousarray(W["L3_ffn_sWg"].reshape(1024, 512))
    W["L3_ffn_sWuf"] = np.ascontiguousarray(W["L3_ffn_sWu"].reshape(1024, 512))
    W["L3_ffn_sWdf"] = np.ascontiguousarray(W["L3_ffn_sWd"].reshape(512, 1024))
    BLOCK_META.clear()
    for name, (kin, mblk) in BLOCK_SPECS.items():
        ncols = W[name].shape[1]
        BLOCK_META[name] = (kin, mblk, ncols)
        W[name] = _blockify(W[name], kin, mblk)
    return W


SCALAR_CONSTS = [1e-5, 1e-6, 1e-4, 2.0, -2.0, 0.5]


def host_consts():
    C = {}
    C["c_ones_col"] = np.ones((128, 1), np.float32)
    C["c_ones_row"] = np.ones((1, T), np.float32)
    C["c_scal"] = np.tile(np.asarray(SCALAR_CONSTS, np.float32)[None, :], (128, 1))
    C["c_ident"] = np.eye(128, dtype=np.float32)
    pos = np.arange(T, dtype=np.float64)
    invf = 1.0 / (10000.0 ** (np.arange(0, D, 2, dtype=np.float64) / D))
    ang = pos[None, :] * invf[:, None]
    cos = np.cos(ang)
    sin = np.sin(ang)
    cos_fm = np.repeat(cos, 2, axis=0).astype(np.float32)
    ssin_fm = np.empty((D, T), np.float32)
    ssin_fm[0::2] = -sin
    ssin_fm[1::2] = sin
    C["c_cos_l3a"] = np.ascontiguousarray(cos_fm)
    C["c_ssin_l3a"] = np.ascontiguousarray(ssin_fm)
    p = np.arange(128)[:, None]
    f = np.arange(512)[None, :]
    tri = np.zeros((128, 4 * 512), np.float32)
    for j in range(4):
        tri[:, j * 512:(j + 1) * 512] = ((p + j * 128) <= f).astype(np.float32)
    C["c_tri512"] = tri
    fidx = np.arange(33, dtype=np.float64)
    n64 = np.arange(64, dtype=np.float64)
    ang2 = 2.0 * np.pi * fidx[:, None] * n64[None, :] / 64.0
    C["c_fftc"] = np.cos(ang2).astype(np.float32)
    C["c_ffts"] = np.sin(ang2).astype(np.float32)
    wf = np.full(33, 2.0 / 64.0)
    wf[0] = 1.0 / 64.0
    wf[32] = 1.0 / 64.0
    C["c_fftw"] = wf.astype(np.float32)
    th = 100000.0
    fr = 1.0 / th ** (np.arange(0, 32, 2, dtype=np.float64) / 32)
    ang3 = np.arange(T, dtype=np.float64)[None, :] * fr[:, None]
    c3 = np.cos(ang3)
    s3 = np.sin(ang3)
    cos_ext = np.ones((128, T), np.float64)
    ssin_ext = np.zeros((128, T), np.float64)
    for half in range(2):
        r0 = 32 + 64 * half
        cos_ext[r0 + 0:r0 + 32:2] = c3
        cos_ext[r0 + 1:r0 + 32:2] = c3
        ssin_ext[r0 + 0:r0 + 32:2] = -s3
        ssin_ext[r0 + 1:r0 + 32:2] = s3
    C["c_cos_l3f"] = cos_ext.astype(np.float32)
    C["c_ssin_l3f"] = ssin_ext.astype(np.float32)
    return C


STAGE_FNS = {
    "l1f_a": st_l1f,
    "l1f_b": st_l1f,
    "l3a": st_l3a,
    "l2a": st_l2a,
    "l2f_a": st_l2f,
    "l2f_b": st_l2f,
    "l0a": st_l0a,
    "l1a": st_l1a,
    "l0f_a": lambda c, W: st_l0f(c, W, 0),
    "l0f_b": lambda c, W: st_l0f(c, W, 1),
    "l3f_a": lambda c, W: st_l3f(c, W, 0),
    "l3f_b": lambda c, W: st_l3f(c, W, 1),
}

STAGE_WEIGHTS = {
    "l1f_a": ["L1_ffn_w1", "L1_ffn_b1c", "L1_ffn_w2", "L1_ffn_b2"],
    "l1f_b": ["L1_ffn_w1", "L1_ffn_b1c", "L1_ffn_w2", "L1_ffn_b2"],
    "l3a": ["L3_attn_wq", "L3_attn_wk", "L3_attn_wv"],
    "l2a": ["L2_attn_w_qkv", "L2_attn_w_out", "L2_attn_b_out",
            "L2_attn_mask_logits"],
    "l2f_a": ["L2_ffn_router_w", "L2_ffn_w1", "L2_ffn_w2"],
    "l2f_b": ["L2_ffn_router_w", "L2_ffn_w1", "L2_ffn_w2"],
    "l0a": ["L0_attn_in_proj_w", "L0_attn_in_proj_bc", "L0_attn_conv_wc",
            "L0_attn_conv_bc", "L0_attn_dt_w", "L0_attn_dt_bc", "L0_attn_b_w",
            "L0_attn_c_w", "L0_attn_A_logc", "L0_attn_D_paramc",
            "L0_attn_out_w", "L0_attn_out_b"],
    "l1a": ["L1_attn_W_in", "L1_attn_conv_wc", "L1_attn_conv_bc",
            "L1_attn_W_xp", "L1_attn_w_dtc", "L1_attn_b_dtc", "L1_attn_A_logc",
            "L1_attn_D_paramc", "L1_attn_W_out", "L1_attn_norm_wc"],
    "l0f_a": ["L0_ffn_Wr", "L0_ffn_W1f", "L0_ffn_b1c", "L0_ffn_W2f",
              "L0_ffn_b2r"],
    "l0f_b": ["L0_ffn_Wr", "L0_ffn_W1f", "L0_ffn_b1c", "L0_ffn_W2f",
              "L0_ffn_b2r"],
    "l3f_a": ["L3_ffn_w_ln1c", "L3_ffn_w_ln2c", "L3_ffn_Wq", "L3_ffn_Wkva", "L3_ffn_Wkvb", "L3_ffn_Wkvb_pad", "L3_ffn_Wo_nope", "L3_ffn_Wgate", "L3_ffn_expert_bias", "L3_ffn_Wgf", "L3_ffn_Wuf", "L3_ffn_Wdf", "L3_ffn_sWgf", "L3_ffn_sWuf", "L3_ffn_sWdf"],
    "l3f_b": ["L3_ffn_w_ln1c", "L3_ffn_w_ln2c", "L3_ffn_Wq", "L3_ffn_Wkva", "L3_ffn_Wkvb", "L3_ffn_Wkvb_pad", "L3_ffn_Wo_nope", "L3_ffn_Wgate", "L3_ffn_expert_bias", "L3_ffn_Wgf", "L3_ffn_Wuf", "L3_ffn_Wdf", "L3_ffn_sWgf", "L3_ffn_sWuf", "L3_ffn_sWdf"],
}

ALL_STAGES = ["l0f_a", "l0a", "l0f_b", "l1f_a", "l1a", "l1f_b",
              "l2f_a", "l2a", "l2f_b", "l3f_a", "l3a", "l3f_b"]

GROUPS = None


def emit_weight_cache(c, Waps, Wmeta):
    sv = c.sem("sv")
    dldw = [c.sem("dwcl0"), c.sem("dwcl1")]
    dstw = [c.sem("dwcs0"), c.sem("dwcs1")]
    with c.scope() as sc:
        bufs = [c.sb_in(sc, c.fresh("wcf"), [128, 4096], F32) for _ in range(2)]
        obufs = [c.sb_in(sc, c.fresh("wcb"), [128, 4096], BF16) for _ in range(2)]
        jobs = []
        for name, arr in Wmeta.items():
            if name in BLOCK_META and arr.ndim == 3:
                bf = c.dram(name + "_bf", list(arr.shape), BF16)
                c.wbf[name] = bf
                for n in range(arr.shape[0]):
                    jobs.append((Waps[name], bf, n, arr.shape[2]))
        hist = {}
        for j, (srcap, bf, n, kcm) in enumerate(jobs):
            slot = j % 2
            wl = [(sv, hist[j - 2])] if j - 2 in hist else []
            c.dma("sync", bufs[slot][:, 0:kcm], srcap[n], waits=wl,
                  inc=dldw[slot])
            c.op("vector", "tensor_copy", obufs[slot][:, 0:kcm],
                 bufs[slot][:, 0:kcm],
                 waits=[(dldw[slot], dldw[slot].n),
                        (dstw[slot], dstw[slot].n)], inc=sv)
            cv = sv.n
            c.dma("sync", bf[n], obufs[slot][:, 0:kcm],
                  waits=[(sv, cv)], inc=dstw[slot])
            hist[j] = cv
        for s_ in dstw:
            c.wait("sync", s_, s_.n)
        c.barrier()


def build_group_program(stage_names, Wmeta):
    nc = bass.Bass("TRN2", target_bir_lowering=False, debug=False)
    c = Ctx(nc)
    xio = nc.dram_tensor("xio", [D, T], F32, kind="ExternalInput").ap()
    xout = nc.dram_tensor("xout", [D, T], F32, kind="ExternalOutput").ap()
    c.xdr = xout
    Waps = {name: nc.dram_tensor(name, list(arr.shape), F32,
                                 kind="ExternalInput").ap()
            for name, arr in Wmeta.items()}
    with c.estack:
        ones_col = c.sb("ones_col", [128, 1], BF16)
        ones_colf = c.sb("ones_colf", [128, 1], F32)
        ones_row = c.sb("ones_row", [1, T], BF16)
        c.consts["ones_col"] = ones_col
        c.consts["ones_colf"] = ones_colf
        c.consts["ones_row"] = ones_row
        dx = c.sem("dx")
        c.dma("gpsimd", ones_col[:], Waps["c_ones_col"], inc=dx)
        c.dma("sync", ones_colf[:], Waps["c_ones_col"], inc=dx)
        c.dma("gpsimd", ones_row[:], Waps["c_ones_row"], inc=dx)
        scal = c.sb("scal", [128, len(SCALAR_CONSTS)], F32)
        c.dma("sync", scal[:], Waps["c_scal"], inc=dx)
        identb = c.sb("identb", [128, 128], BF16)
        c.dma("gpsimd", identb[:], Waps["c_ident"], inc=dx)
        c.consts["identb"] = identb
        for j, v in enumerate(SCALAR_CONSTS):
            nc.const_aps.register(F32, float(v), scal[:, j:j + 1])
        c.dma("sync", xout, xio, inc=dx)
        c.barrier()
        blocked = [n for n, arr in Wmeta.items()
                   if n in BLOCK_META and getattr(arr, "ndim", 0) == 3]
        first_use = {}
        for sname in stage_names:
            for n in sorted(STAGE_WEIGHTS[sname]):
                if n in blocked and n not in first_use:
                    first_use[n] = sname
        gate = {}
        gi = [0]

        def emit_cache(snames):
            for sname in snames:
                names = [n for n in blocked if first_use.get(n) == sname]
                if not names:
                    continue
                ds = c.sem(f"dwb{gi[0]}")
                gi[0] += 1
                for name in names:
                    arr = Wmeta[name]
                    bf = c.dram(name + "_bf", list(arr.shape), BF16)
                    c.wbf[name] = bf
                    for nb in range(arr.shape[0]):
                        c.dma("gpsimd", bf[nb], Waps[name][nb], inc=ds)
                for name in names:
                    c.wbf_gate[name] = (ds, ds.n)

        emit_cache(stage_names)
        for sname in stage_names:
            STAGE_FNS[sname](c, Waps)
        build_block_program(c)
    return nc


def kernel(**inputs):
    W = host_prep(inputs)
    consts = host_consts()
    x = W.pop("L0_attn_x")
    xs = [np.ascontiguousarray(x[b].T) for b in range(NCORES)]

    groups = GROUPS or [[s for s in ALL_STAGES if s in STAGE_FNS]]
    for g in groups:
        wnames = sorted({w for s in g for w in STAGE_WEIGHTS[s]})
        Wmeta = {n: W[n] for n in wnames}
        Wmeta.update(consts)
        nc = build_group_program(g, Wmeta)
        in_maps = []
        for b in range(NCORES):
            m = {"xio": xs[b]}
            m.update(Wmeta)
            in_maps.append(m)
        res = run_bass_kernel_spmd(nc, in_maps, core_ids=list(range(NCORES)))
        xs = [np.asarray(res.results[b]["xout"]) for b in range(NCORES)]

    out = np.stack([xs[b].T for b in range(NCORES)], axis=0)
    return np.ascontiguousarray(out.astype(np.float32))


# version 16
# speedup vs baseline: 1.0462x; 1.0462x over previous
import contextlib
import math
import numpy as np

import concourse.bass as bass
import concourse.mybir as mybir
from concourse.bass_types import AP
from concourse.bass_utils import run_bass_kernel_spmd

F32 = mybir.dt.float32
BF16 = mybir.dt.bfloat16
AF = mybir.ActivationFunctionType
ALU = mybir.AluOpType

NCORES = 8
T = 4096
D = 1024
TB = 512
NT = T // TB
ND = D // 128

ENGINES = ("sync", "vector", "scalar", "tensor", "gpsimd")


class Sem:
    __slots__ = ("h", "n", "name")

    def __init__(self, h, name):
        self.h = h
        self.n = 0
        self.name = name


class Ctx:
    def __init__(self, nc):
        self.nc = nc
        self.ops = []
        self.sems = {}
        self.estack = contextlib.ExitStack()
        self.consts = {}
        self.uid = 0
        self.dram_cache = {}
        self.wbf = {}
        self.wbf_gate = {}
        self.wfree = {}
        self.wrot = {}
        self.xdr = None

    SEM_ALIAS = {}
    for _n in ["wl0a40", "wl1a40", "wl_3f0", "wl_3f30", "wl_3f40", "wl_3f60",
               "wl_l0a0", "wl_l0f0", "wl_l1a0", "wl_l1f0", "wl_l2a0",
               "wl_l2f0", "wl_l2o0", "wl_l3a0", "wl0a_dt"]:
        SEM_ALIAS[_n] = "wlA"
    for _n in ["wl0a41", "wl1a41", "wl_3f1", "wl_3f31", "wl_3f41", "wl_3f61",
               "wl_l0a1", "wl_l0f1", "wl_l1a1", "wl_l1f1", "wl_l2a1",
               "wl_l2f1", "wl_l2o1", "wl_l3a1", "wl0a_b"]:
        SEM_ALIAS[_n] = "wlB"
    SEM_ALIAS["wl0a_c"] = "wlC"
    for _n in ["sez0a", "sez1a", "seh0f", "seh2f", "seh3f"]:
        SEM_ALIAS[_n] = "seh"
    for _n in ["seo0a", "seo0f", "seo1a", "seo2a", "seo2f", "seo3f",
               "seo3f6"]:
        SEM_ALIAS[_n] = "seo"
    for _n in ["seq2a", "seq3a", "seq3f"]:
        SEM_ALIAS[_n] = "seq"
    for _n in ["sev2a", "sev3a", "sev3f", "sek3f"]:
        SEM_ALIAS[_n] = "sev"
    for _n in ["set0f", "set2f", "set3f"]:
        SEM_ALIAS[_n] = "setx"
    for _n in ["sdy3f"]:
        SEM_ALIAS[_n] = "sdy"
    for _n in ["se0a2"]:
        SEM_ALIAS[_n] = "se2x"
    for _n in ["sef2a"]:
        SEM_ALIAS[_n] = "sefx"
    for _n in ["sex1a"]:
        SEM_ALIAS[_n] = "sexx"
    for _n in ["seu3f"]:
        SEM_ALIAS[_n] = "seu"
    for _n in ["dqq2a", "dqq3a"]:
        SEM_ALIAS[_n] = "dqq"
    for _n in ["drd2a", "drd3a", "drd3f"]:
        SEM_ALIAS[_n] = "drd"
    for _n in ["dxl3a"]:
        SEM_ALIAS[_n] = "dxl"
    for _n in ["dkk2a0", "dkk3a0"]:
        SEM_ALIAS[_n] = "dkk0"
    for _n in ["dkk2a1", "dkk3a1"]:
        SEM_ALIAS[_n] = "dkk1"
    for _n in ["dkk3a2"]:
        SEM_ALIAS[_n] = "dkk2"
    for _n in ["dvv2a0", "dvv3a0", "dvv3f0"]:
        SEM_ALIAS[_n] = "dvv0"
    for _n in ["dvv2a1", "dvv3a1", "dvv3f1"]:
        SEM_ALIAS[_n] = "dvv1"
    for _n in ["dvv3a2"]:
        SEM_ALIAS[_n] = "dvv2"
    for _n in ["dkv3f"]:
        SEM_ALIAS[_n] = "dkv"
    for _n in ["dq3f"]:
        SEM_ALIAS[_n] = "dq3x"

    def sem(self, name):
        name = self.SEM_ALIAS.get(name, name)
        if name not in self.sems:
            h = self.estack.enter_context(self.nc.semaphore(name))
            self.sems[name] = Sem(h, name)
        return self.sems[name]

    def sb(self, name, shape, dtype):
        return self.estack.enter_context(self.nc.sbuf_tensor(name, list(shape), dtype))

    def scope(self):
        return contextlib.ExitStack()

    def sb_in(self, scope, name, shape, dtype):
        return scope.enter_context(self.nc.sbuf_tensor(name, list(shape), dtype))

    def ps_in(self, scope, name, shape, dtype=F32):
        return scope.enter_context(self.nc.psum_tensor(name, list(shape), dtype))

    def dram(self, name, shape, dtype, kind="Internal"):
        if name not in self.dram_cache:
            self.dram_cache[name] = self.nc.dram_tensor(
                name, list(shape), dtype, kind=kind).ap()
        return self.dram_cache[name]

    def fresh(self, prefix):
        self.uid += 1
        return f"{prefix}_{self.uid}"

    def emit(self, engine, fn):
        self.ops.append((engine, [], fn))

    def op(self, engine, name, *args, waits=(), inc=None, inc_k=1, **kwargs):
        if inc is not None:
            inc.n += inc_k
        wl = [(s, (s.n if v is None else v)) for (s, v) in waits]

        def fn(e):
            ins = getattr(e, name)(*args, **kwargs)
            if inc is not None:
                ins.then_inc(inc.h, inc_k)
        self.emit2(engine, wl, fn)
        return inc.n if inc is not None else None

    def emit2(self, engine, wl, fn):
        self.ops.append((engine, wl, fn))

    def wait(self, engine, sem, val=None):
        v = sem.n if val is None else val
        self.emit2(engine, [(sem, v)], None)

    def dma(self, queue, out, in_, waits=(), inc=None, **kw):
        return self.op(queue, "dma_start", waits=waits, inc=inc, inc_k=16,
                       out=out, in_=in_, **kw)

    def barrier(self):
        targets = []
        for eng in ENGINES:
            s = self.sem("bar_" + eng)
            s.n += 1
            targets.append((eng, s, s.n))

            def fn(e, s=s):
                e.drain().then_inc(s.h, 1)
            self.emit(eng, fn)
        for eng in ENGINES:
            wl = [(s, v) for oeng, s, v in targets if oeng != eng]
            self.emit2(eng, wl, None)

    def memset(self, engine, ap, val, waits=()):
        s = self.sem("ms_" + engine)
        s.n += 1
        myv = s.n
        wl = [(ss, (ss.n if v is None else v)) for (ss, v) in waits]

        def fn(e):
            e.memset(ap, val).then_inc(s.h, 1)
            e.wait_ge(s.h, myv)
        self.emit2(engine, wl, fn)

    def bounce_bcast(self, row_src_ap, ncols, out_tile_ap, nparts,
                     waits=(), inc=None, dtype=F32, waits2=()):
        nm = self.fresh("bnc")
        dr = self.dram(nm, [ncols], dtype)
        s = self.sem("bncw")
        self.dma("sync", AP(dr.tensor, 0, [[ncols, 1], [1, ncols]]), row_src_ap, waits=waits, inc=s)
        src = AP(dr.tensor, 0, [[0, nparts], [1, ncols]])
        self.dma("sync", out_tile_ap, src,
                 waits=[(s, s.n)] + list(waits2), inc=inc)

    def x_slice_src(self, t0, t1):
        return self.xdr.rearrange("(dc p) t -> p dc t", p=128)[:, :, t0:t1]

    def load_x(self, h, t0, t1, waits=(), inc=None, queue="sync"):
        dst = h[:].rearrange("p (dc t) -> p dc t", dc=ND)
        self.dma(queue, dst, self.x_slice_src(t0, t1), waits=waits, inc=inc)

    def store_x(self, h, t0, t1, waits=(), inc=None, queue="sync"):
        src = h[:].rearrange("p (dc t) -> p dc t", dc=ND)
        self.dma(queue, self.x_slice_src(t0, t1), src, waits=waits, inc=inc)


def _replay(ctx, e, engine):
    seen = {}
    for eng, wl, fn in ctx.ops:
        if eng != engine:
            continue
        for s, v in wl:
            if v > 0 and v > seen.get(s.name, 0):
                e.wait_ge(s.h, v)
                seen[s.name] = v
        if fn is not None:
            fn(e)


def build_block_program(ctx):
    nc = ctx.nc
    with nc.Block() as block:
        @block.sync
        def _(e):
            _replay(ctx, e, "sync")

        @block.vector
        def _(e):
            _replay(ctx, e, "vector")

        @block.scalar
        def _(e):
            _replay(ctx, e, "scalar")

        @block.tensor
        def _(e):
            _replay(ctx, e, "tensor")

        @block.gpsimd
        def _(e):
            _replay(ctx, e, "gpsimd")



BLOCK_META = {}


def load_w_tiles(c, scope, w_ap, kin, cols_lo, cols_hi, tag, queue="gpsimd",
                 waits=(), inc=None, dtype=BF16, handle=None, row_lo=0):
    w = cols_hi - cols_lo
    kc = kin // 128
    h = handle
    if h is None:
        h = c.sb_in(scope, c.fresh(tag), [128, kc * w], dtype)
    name = w_ap.tensor.name
    if name in BLOCK_META:
        bkin, mblk, ncols = BLOCK_META[name]
        assert bkin == kin and w <= mblk and cols_lo % mblk == 0, (name, kin, cols_lo, w)
        nmb = ncols // mblk
        idx = (row_lo // kin) * nmb + cols_lo // mblk
        if name in c.wbf:
            w_ap = c.wbf[name]
            queue = "sync"
            g = c.wbf_gate.get(name)
            if g is not None:
                waits = list(waits) + [g]
        src = w_ap[idx]
        if w < mblk:
            src = w_ap[idx].rearrange("p (kc m) -> p kc m", kc=kc)[:, :, 0:w]
            dst = h[:, 0:kc * w].rearrange("p (kc m) -> p kc m", kc=kc)
        else:
            dst = h[:, 0:kc * w]
        c.dma(queue, dst, src, waits=waits, inc=inc)
    else:
        src = w_ap[row_lo:row_lo + kin, cols_lo:cols_hi] \
            .rearrange("(kc kp) m -> kp kc m", kp=128)
        dst = h[:, 0:kc * w].rearrange("p (kc m) -> p kc m", kc=kc)
        c.dma(queue, dst, src, waits=waits, inc=inc)
    return h, kc, w


def load_row(c, scope, vec_ap, n, tag, dtype=F32, queue="sync", waits=(), inc=None):
    h = c.sb_in(scope, c.fresh(tag), [1, n], dtype)
    if queue == "sync" and dtype != F32:
        queue = "gpsimd"
    c.dma(queue, h[:], AP(vec_ap.tensor, vec_ap.offset, [[n, 1], [1, n]]), waits=waits, inc=inc)
    return h


def load_mat(c, scope, mat_ap, rows, cols, tag, dtype=F32, queue="sync",
             waits=(), inc=None):
    h = c.sb_in(scope, c.fresh(tag), [rows, cols], dtype)
    if dtype != F32:
        queue = "gpsimd"
    c.dma(queue, h[:], mat_ap, waits=waits, inc=inc)
    return h



def ln_rows(c, scope, tag, src_override=None, eps=1e-5, nd=ND, scale_dim=None):
    sv, sa, sp = c.sem("sv"), c.sem("sa"), c.sem("sp")
    dxs = c.sem("dxs")
    dim = scale_dim or (nd * 128)
    rows = c.sb_in(scope, c.fresh(f"lnrow{tag}"), [1, 2 * T], F32)
    inner = scope.enter_context(c.scope()) if False else c.scope()
    scr = c.sb_in(inner, c.fresh(f"lnscr{tag}"), [1, 2 * T], F32)
    xsl = c.sb_in(inner, c.fresh(f"lnx{tag}"), [128, 2 * nd * TB], F32)
    xb = c.sb_in(inner, c.fresh(f"lnxb{tag}"), [128, 2 * nd * TB], BF16)
    xq = c.sb_in(inner, c.fresh(f"lnxq{tag}"), [128, 2 * nd * TB], BF16)
    pstat = c.ps_in(inner, c.fresh(f"lnps{tag}"), [1, 4 * TB], F32)
    ones_col = c.consts["ones_col"]

    cast_done = [None, None]
    cast_guard = [0, 0]
    psum_guard = [0, 0]
    for tb in range(NT):
        par = tb % 2
        xsl_v = xsl[:, par * nd * TB:(par + 1) * nd * TB]
        xb_v = xb[:, par * nd * TB:(par + 1) * nd * TB]
        xq_v = xq[:, par * nd * TB:(par + 1) * nd * TB]
        ps_s = pstat[0:1, par * 2 * TB: par * 2 * TB + TB]
        ps_q = pstat[0:1, par * 2 * TB + TB: par * 2 * TB + 2 * TB]
        wl = []
        if cast_done[par]:
            wl = [(sv, cast_done[par][0]), (sa, cast_done[par][1])]
        if src_override is None:
            dst = xsl_v.rearrange("p (dc t) -> p dc t", dc=nd)
            c.dma("sync", dst, c.x_slice_src(tb * TB, (tb + 1) * TB),
                  waits=wl, inc=dxs)
        else:
            src_override(xsl_v, tb, wl, dxs)
        lded = dxs.n
        wcast = [(dxs, lded)]
        if cast_guard[par]:
            wcast.append((sp, cast_guard[par]))
        c.op("vector", "tensor_copy", xb_v, xsl_v, waits=wcast, inc=sv)
        c.op("scalar", "activation", xq_v, xsl_v, AF.Square, waits=wcast, inc=sa)
        cast_done[par] = (sv.n, sa.n)
        v_done, a_done = sv.n, sa.n
        wmm = [(sv, v_done), (sa, a_done)]
        if psum_guard[par]:
            wmm.append((sv, psum_guard[par]))
        for d in range(nd):
            c.op("tensor", "matmul", ps_s, ones_col[:, 0:1],
                 xb_v.rearrange("p (dc t) -> p dc t", dc=nd)[:, d],
                 start=(d == 0), stop=(d == nd - 1),
                 waits=(wmm if d == 0 else ()), inc=sp)
        for d in range(nd):
            c.op("tensor", "matmul", ps_q, ones_col[:, 0:1],
                 xq_v.rearrange("p (dc t) -> p dc t", dc=nd)[:, d],
                 start=(d == 0), stop=(d == nd - 1), inc=sp)
        cast_guard[par] = sp.n
        pe_mm = sp.n
        a_seg = rows[0:1, tb * TB: tb * TB + TB]
        b_seg = rows[0:1, T + tb * TB: T + tb * TB + TB]
        m_seg = scr[0:1, tb * TB: tb * TB + TB]
        v_seg = scr[0:1, T + tb * TB: T + tb * TB + TB]
        c.op("vector", "tensor_scalar_mul", m_seg, ps_s, 1.0 / dim,
             waits=[(sp, pe_mm)], inc=sv)
        c.op("vector", "tensor_scalar_mul", v_seg, ps_q, 1.0 / dim, inc=sv)
        psum_guard[par] = sv.n
        c.op("vector", "tensor_tensor", b_seg, m_seg, m_seg, ALU.mult, inc=sv)
        c.op("vector", "tensor_sub", v_seg, v_seg, b_seg, inc=sv)
        c.op("scalar", "activation", b_seg, v_seg, AF.Sqrt, bias=float(eps),
             waits=[(sv, sv.n)], inc=sa)
        c.op("vector", "reciprocal", a_seg, b_seg, waits=[(sa, sa.n)], inc=sv)
        c.op("vector", "tensor_tensor", b_seg, m_seg, a_seg, ALU.mult, inc=sv)
    c.barrier()
    inner.close()
    return rows, sv.n



def emit_xn_slice(c, xn_v, xsl_v, ab_bc, t0, width, waits=(), inc=None, nd=ND):
    sv = c.sem("sv")
    a3 = ab_bc[:, t0:t0 + width].unsqueeze(1).broadcast_to([128, nd, width])
    b3 = ab_bc[:, T + t0:T + t0 + width].unsqueeze(1).broadcast_to([128, nd, width])
    x3 = xsl_v.rearrange("p (dc t) -> p dc t", dc=nd)
    n3 = xn_v.rearrange("p (dc t) -> p dc t", dc=nd)
    c.op("vector", "tensor_tensor", n3, x3, a3, ALU.mult, waits=waits, inc=sv)
    c.op("vector", "tensor_tensor", n3, n3, b3, ALU.subtract, inc=inc or sv)



def proj(c, scope, rhs_at, w_ap, kin, m_lo, m_hi, nt_blocks, evac, tag,
         mblk=512, extra_k=None, evac_sem=None, w_row_lo=0, first_waits=(),
         pp=None, wbufs=None, wl_sems=None):
    sp = c.sem("sp")
    sp_entry = sp.n
    if evac_sem is None:
        evac_sem = c.sem("se_" + tag[:4])
    if pp is None:
        pp = c.ps_in(scope, c.fresh(f"pp{tag}"), [128, 4 * TB], F32)
    npar = pp.shape[1] // TB
    nmb = (m_hi - m_lo + mblk - 1) // mblk
    kc = kin // 128
    wl_sem = wl_sems or [c.sem("wl_" + tag[:4] + "0"), c.sem("wl_" + tag[:4] + "1")]
    wbuf = wbufs or [c.sb_in(scope, c.fresh(f"wb{tag}"), [128, kc * mblk], BF16)
                     for _ in range(2)]
    rkey = id(wbuf[0])
    rot = c.wrot.get(rkey, 0)
    free_at = [c.wfree.get(id(wbuf[0]), 0), c.wfree.get(id(wbuf[1]), 0)]
    loaded = [0, 0]
    ev_hist = {}
    gi = 0
    fw = list(first_waits)
    for mb in range(nmb):
        slot = (mb + rot) % 2
        lo = m_lo + mb * mblk
        hi = min(m_lo + (mb + 1) * mblk, m_hi)
        load_w_tiles(c, scope, w_ap, kin, lo, hi, tag, queue="gpsimd",
                     waits=([(sp, free_at[slot])] if free_at[slot] else ()),
                     inc=wl_sem[slot], handle=wbuf[slot], row_lo=w_row_lo)
        loaded[slot] = wl_sem[slot].n
        wcols = hi - lo
        nmi = (wcols + 127) // 128
        groups = [(mi, tb) for mi in range(nmi) for tb in range(nt_blocks)]
        batches = []
        for mi in range(nmi):
            for t0_ in range(0, nt_blocks, npar):
                batches.append([(mi, tb) for tb in
                                range(t0_, min(t0_ + npar, nt_blocks))])
        for batch in batches:
            gis = {}
            for gmi in batch:
                gis[gmi] = gi
                gi += 1
            for k in range(kc):
                for (mi, tb) in batch:
                    g = gis[(mi, tb)]
                    mw = min(128, wcols - mi * 128)
                    par = g % npar
                    ps = pp[0:mw, par * TB:(par + 1) * TB]
                    w0 = []
                    if k == 0:
                        w0 = [(wl_sem[slot], loaded[slot])]
                        if g < npar:
                            w0 += fw
                        if g - npar in ev_hist:
                            w0.append((evac_sem, ev_hist[g - npar]))
                    c.op("tensor", "matmul", ps,
                         wbuf[slot][:, k * wcols + mi * 128:
                                    k * wcols + mi * 128 + mw],
                         rhs_at(k, tb),
                         start=(k == 0),
                         stop=(k == kc - 1 and extra_k is None),
                         waits=w0, inc=sp)
            for (mi, tb) in batch:
                g = gis[(mi, tb)]
                mw = min(128, wcols - mi * 128)
                mi_abs = lo + mi * 128
                par = g % npar
                ps = pp[0:mw, par * TB:(par + 1) * TB]
                if extra_k is not None:
                    lr, rr = extra_k
                    c.op("tensor", "matmul", ps, lr(mi_abs)[:, 0:mw],
                         rr(tb), start=False, stop=True, inc=sp)
                mmc = sp.n
                evac(mi_abs, tb, ps, mmc)
                ev_hist[g] = evac_sem.n
        free_at[slot] = sp.n
        c.wfree[id(wbuf[slot])] = sp.n
    c.wrot[rkey] = (rot + nmb) % 2



def st_l1f(c, W):
    TCH = 512
    NCH = T // TCH
    NB = TCH // TB
    sv, sa, sp = c.sem("sv"), c.sem("sa"), c.sem("sp")
    dg, dxs, dst_ = c.sem("dg"), c.sem("dxs2"), c.sem("dst")
    with c.scope() as sc:
        rows, rows_done = ln_rows(c, sc, "l1f")
        ab_bc = c.sb_in(sc, c.fresh("abbc"), [128, 2 * T], F32)
        dwb = c.sem("dwb")
        c.bounce_bcast(rows[0:1, :], 2 * T, ab_bc[:], 128,
                       waits=[(sv, rows_done)], inc=dwb)
        bc_done = dwb.n
        b1c = load_mat(c, sc, W["L1_ffn_b1c"], 128, 4096 // 128, "b1c", inc=dg)
        b2r = load_row(c, sc, W["L1_ffn_b2"], D, "b2r", dtype=BF16, inc=dg)
        bias_done = dg.n
        xsl = c.sb_in(sc, c.fresh("xsl"), [128, ND * TCH], F32)
        xn = c.sb_in(sc, c.fresh("xn"), [128, ND * TCH], BF16)
        h = c.sb_in(sc, c.fresh("h"), [128, 32 * TCH], BF16)
        seh = c.sem("seh")
        seo = c.sem("seo")
        ones_row = c.consts["ones_row"]
        pp = c.ps_in(sc, c.fresh("ppl1f"), [128, 4 * TB], F32)
        wbufs = [c.sb_in(sc, c.fresh("wbl1f"), [128, 4096], BF16) for _ in range(2)]
        wl_sems = [c.sem("wl_l1f0"), c.sem("wl_l1f1")]

        for ch in range(NCH):
            t0 = ch * TCH
            wl = [(dst_, dst_.n)] if ch else []
            c.load_x(xsl, t0, t0 + TCH, waits=wl, inc=dxs)
            emit_xn_slice(c, xn[:], xsl[:], ab_bc, t0, TCH,
                          waits=[(dxs, dxs.n), (dwb, bc_done)])
            xn_done = sv.n

            def rhs_xn(k, tb):
                return xn[:, k * TCH + tb * TB: k * TCH + (tb + 1) * TB]

            def evac_h(mi, tb, ps, mmc):
                c.op("scalar", "activation",
                     h[:, (mi // 128) * TCH + tb * TB:(mi // 128) * TCH + (tb + 1) * TB],
                     ps, AF.Gelu, bias=b1c[:, mi // 128: mi // 128 + 1],
                     waits=[(sp, mmc), (dg, bias_done)], inc=seh)

            proj(c, sc, rhs_xn, W["L1_ffn_w1"], D, 0, 4096, NB, evac_h,
                 "w1", evac_sem=seh, first_waits=[(sv, xn_done), (seo, seo.n)],
                 pp=pp, wbufs=wbufs, wl_sems=wl_sems)

            def rhs_h(k, tb):
                return h[:, k * TCH + tb * TB: k * TCH + (tb + 1) * TB]

            def evac_o(mi, tb, ps, mmc):
                d = mi // 128
                sl = slice(d * TCH + tb * TB, d * TCH + (tb + 1) * TB)
                c.op("vector", "scalar_tensor_tensor", xsl[:, sl], ps, 0.5,
                     xsl[:, sl], ALU.mult, ALU.add, waits=[(sp, mmc)], inc=seo)

            proj(c, sc, rhs_h, W["L1_ffn_w2"], 4096, 0, D, NB, evac_o,
                 "w2", mblk=128, evac_sem=seo, first_waits=[(seh, seh.n)],
                 pp=pp, wbufs=wbufs, wl_sems=wl_sems,
                 extra_k=(lambda mi: b2r[0:1, mi: mi + 128],
                          lambda tb: ones_row[0:1, 0:TB]))
            c.store_x(xsl, t0, t0 + TCH, waits=[(seo, seo.n)], inc=dst_)
        c.wait("sync", dst_, dst_.n)
        c.barrier()



def st_l3a(c, W):
    TCH = 512
    NCH = T // TCH
    sv, sa, sp = c.sem("sv"), c.sem("sa"), c.sem("sp")
    dxs, dst_, dqs = c.sem("dxs2"), c.sem("dst"), c.sem("dqs")
    dkv = c.sem("dkv")
    qd = c.dram(c.fresh("qd"), [D, T], BF16)
    kd = c.dram(c.fresh("kd"), [D, T], BF16)
    vd = c.dram(c.fresh("vd"), [T, D], BF16)
    with c.scope() as sc:
        rows, rows_done = ln_rows(c, sc, "l3a")
        ab_bc = c.sb_in(sc, c.fresh("abbc"), [128, 2 * T], F32)
        dwb = c.sem("dwb")
        c.bounce_bcast(rows[0:1, :], 2 * T, ab_bc[:], 128,
                       waits=[(sv, rows_done)], inc=dwb)
        bc_done = dwb.n
        with c.scope() as p1:
            xsl = c.sb_in(p1, c.fresh("xsl"), [128, ND * TCH], F32)
            xn = c.sb_in(p1, c.fresh("xn"), [128, ND * TCH], BF16)
            cosb = c.sb_in(p1, c.fresh("cosb"), [128, ND * TCH], F32)
            sinb = c.sb_in(p1, c.fresh("sinb"), [128, ND * TCH], F32)
            qtmp = c.sb_in(p1, c.fresh("qtmp"), [128, TCH], F32)
            qsh = c.sb_in(p1, c.fresh("qsh"), [128, TCH], F32)
            qacc = c.sb_in(p1, c.fresh("qacc"), [128, ND * TCH], BF16)
            vtmp = c.sb_in(p1, c.fresh("vtmp"), [128, (TCH // 128) * D], BF16)
            pp = c.ps_in(p1, c.fresh("ppl3a"), [128, 2 * TB], F32)
            wbufs = [c.sb_in(p1, c.fresh("wbl3a"), [128, 8 * 512], BF16)
                     for _ in range(2)]
            wl_sems = [c.sem("wl_l3a0"), c.sem("wl_l3a1")]
            swap_mask = []
            for i in range(0, 32, 2):
                swap_mask += [i + 1, i]
            seq = c.sem("seq3a")
            for ch in range(NCH):
                t0 = ch * TCH
                c.load_x(xsl, t0, t0 + TCH,
                         waits=[(seq, seq.n), (dqs, dqs.n)], inc=dxs)
                emit_xn_slice(c, xn[:], xsl[:], ab_bc, t0, TCH,
                              waits=[(dxs, dxs.n), (dwb, bc_done), (sp, sp.n)])
                xn_done = sv.n
                csrc = W["c_cos_l3a"].rearrange("(dc p) t -> p dc t", p=128)
                ssrc = W["c_ssin_l3a"].rearrange("(dc p) t -> p dc t", p=128)
                c.dma("sync", cosb[:].rearrange("p (dc t) -> p dc t", dc=ND),
                      csrc[:, :, t0:t0 + TCH], waits=[(seq, seq.n)], inc=dxs)
                c.dma("sync", sinb[:].rearrange("p (dc t) -> p dc t", dc=ND),
                      ssrc[:, :, t0:t0 + TCH], inc=dxs)
                tab_done = dxs.n

                for which, wname, dst in (
                        (0, "L3_attn_wq", qd), (1, "L3_attn_wk", kd)):
                    def evac_rope(mi, tb, ps, mmc, which=which):
                        m = mi // 128
                        c.op("vector", "tensor_copy", qtmp[:], ps,
                             waits=[(sp, mmc), (dxs, tab_done),
                                    (dqs, dqs.n)], inc=sv)
                        c.op("vector", "stream_shuffle", qsh[:], qtmp[:],
                             swap_mask, inc=sv)
                        cs = cosb[:, m * TCH:(m + 1) * TCH]
                        sn = sinb[:, m * TCH:(m + 1) * TCH]
                        qa = qacc[:, m * TCH:(m + 1) * TCH]
                        c.op("vector", "tensor_tensor", qa, qtmp[:], cs,
                             ALU.mult, inc=sv)
                        c.op("vector", "tensor_tensor", qsh[:], qsh[:], sn,
                             ALU.mult, inc=sv)
                        c.op("vector", "tensor_tensor", qa, qa, qsh[:],
                             ALU.add, inc=seq)

                    sev = c.sem("sev3a")
                    proj(c, sc, lambda k, tb: xn[:, k * TCH:(k + 1) * TCH],
                         W[wname], D, 0, D, 1, evac_rope, "l3aqk",
                         evac_sem=seq,
                         first_waits=[(sv, xn_done), (seq, seq.n), (sev, sev.n)],
                         pp=pp, wbufs=wbufs, wl_sems=wl_sems, mblk=512)
                    c.dma("sync",
                          dst.rearrange("(dc p) t -> p dc t", p=128)[:, :, t0:t0 + TCH],
                          qacc[:].rearrange("p (dc t) -> p dc t", dc=ND),
                          waits=[(seq, seq.n)], inc=dqs)
                sev = c.sem("sev3a")
                ntt = TCH // 128
                load_w_tiles(c, sc, W["L3_attn_wv"], D, 0, 512, "wv",
                             waits=[(sp, sp.n)], inc=wl_sems[0], handle=wbufs[0])
                load_w_tiles(c, sc, W["L3_attn_wv"], D, 512, 1024, "wv",
                             waits=[(sp, sp.n)], inc=wl_sems[1], handle=wbufs[1])
                ev_hist = [seq.n and 0, 0]
                ev_hist = {}
                g = 0
                for nb in range(2):
                    for tt in range(ntt):
                        par = g % 2
                        ps = pp[:, par * TB: par * TB + TB]
                        w0 = [(wl_sems[nb], wl_sems[nb].n), (seq, seq.n)]
                        if g >= 2:
                            w0.append((sev, ev_hist[g - 2]))
                        for k in range(ND):
                            lhsT = xn[:, k * TCH + tt * 128: k * TCH + tt * 128 + 128]
                            c.op("tensor", "matmul", ps,
                                 lhsT, wbufs[nb][:, k * 512:(k + 1) * 512],
                                 start=(k == 0), stop=(k == ND - 1),
                                 waits=(w0 if k == 0 else ()), inc=sp)
                        c.op("vector", "tensor_copy",
                             vtmp[:, tt * D + nb * 512: tt * D + nb * 512 + 512],
                             ps, waits=[(sp, sp.n), (dkv, dkv.n)], inc=sev)
                        ev_hist[g] = sev.n
                        g += 1
                c.dma("sync",
                      vd[t0:t0 + TCH, :].rearrange("(tt p) dd -> p tt dd", p=128),
                      vtmp[:].rearrange("p (tt dd) -> p tt dd", tt=ntt),
                      waits=[(sev, sev.n)], inc=dkv)
                c.wfree[id(wbufs[0])] = sp.n
                c.wfree[id(wbufs[1])] = sp.n
            c.wait("sync", dqs, dqs.n)
            c.wait("sync", dkv, dkv.n)
            c.barrier()
        import os
        if os.environ.get("L3A_PH") == "1":
            c.barrier()
            return
        with c.scope() as p2:
            TQ = 512
            NKMAX = T // 128
            qbuf = c.sb_in(p2, c.fresh("qbuf"), [128, ND * TQ], BF16)
            kbuf = c.sb_in(p2, c.fresh("kbuf"), [128, 3 * ND * 128], BF16)
            vbuf = c.sb_in(p2, c.fresh("vbuf"), [128, 3 * D], BF16)
            probs = c.sb_in(p2, c.fresh("probs"), [128, (T // 128) * TQ], BF16)
            rden = c.sb_in(p2, c.fresh("rden"), [1, TQ], F32)
            rdbc = c.sb_in(p2, c.fresh("rdbc"), [128, TQ], F32)
            yv = c.sb_in(p2, c.fresh("yv"), [128, TQ], F32)
            xsl2 = c.sb_in(p2, c.fresh("xsl2"), [128, ND * TQ], F32)
            trim = c.sb_in(p2, c.fresh("trim"), [128, 4 * TQ], BF16)
            ps_y = c.ps_in(p2, c.fresh("psy"), [128, 4 * TQ], F32)
            ps_sc = c.ps_in(p2, c.fresh("pssc"), [128, 2 * TQ], F32)
            ps_dn = c.ps_in(p2, c.fresh("psdn"), [1, TQ], F32)
            c.dma("gpsimd", trim[:], W["c_tri512"], inc=dkv)
            tri_done = dkv.n
            dqq = c.sem("dqq3a")
            dkk = [c.sem(f"dkk3a{i}") for i in range(3)]
            dvv = [c.sem(f"dvv3a{i}") for i in range(3)]
            drd = c.sem("drd3a")
            dxl = c.sem("dxl3a")
            ones_col = c.consts["ones_col"]
            prev_yv3a = [0]
            for tqb in range(T // TQ):
                q0 = tqb * TQ
                ntk = 4 * tqb + 4
                c.dma("sync",
                      qbuf[:].rearrange("p (dc t) -> p dc t", dc=ND),
                      qd.rearrange("(dc p) t -> p dc t", p=128)[:, :, q0:q0 + TQ],
                      waits=[(sp, sp.n)], inc=dqq)
                qld = dqq.n
                rd_done = xld = 0
                for half in range(2):
                    mlo = half * 4
                    kslot_free = {}
                    vslot_free = {}
                    exp_d = {}
                    pw_d = {}
                    qk_d = {}
                    kvld_m = {}

                    def emit_ld3a(tkb):
                        t0 = tkb * 128
                        vslot = tkb % 3
                        kslot = tkb % 3
                        wk = [(dqq, qld)]
                        if tkb - 3 in vslot_free:
                            wk.append((sp, vslot_free[tkb - 3]))
                        elif half == 1 and tkb < 3:
                            wk.append((sp, sp.n))
                        c.dma("sync", vbuf[:, vslot * D:(vslot + 1) * D],
                              vd[t0:t0 + 128, :], waits=wk, inc=dvv[vslot])
                        if half == 0:
                            kw = []
                            if tkb - 3 in kslot_free:
                                kw.append((sp, kslot_free[tkb - 3]))
                            elif tkb < 3:
                                kw.append((sp, sp.n))
                            c.dma("sync",
                                  kbuf[:, kslot * ND * 128:
                                       (kslot + 1) * ND * 128]
                                  .rearrange("p (dc t) -> p dc t", dc=ND),
                                  kd.rearrange("(dc p) t -> p dc t",
                                               p=128)[:, :, t0:t0 + 128],
                                  waits=kw, inc=dkk[kslot])
                        kvld_m[tkb] = (dvv[vslot].n, dkk[kslot].n)

                    def emit_qk3a(tkb):
                        t0 = tkb * 128
                        kslot = tkb % 3
                        vslot = tkb % 3
                        spar = tkb % 2
                        scs = ps_sc[:, spar * TQ:(spar + 1) * TQ]
                        for k in range(ND):
                            c.op("tensor", "matmul", scs,
                                 kbuf[:, kslot * ND * 128 + k * 128:
                                      kslot * ND * 128 + (k + 1) * 128],
                                 qbuf[:, k * TQ:(k + 1) * TQ],
                                 start=(k == 0), stop=(k == ND - 1),
                                 waits=([(dkk[kslot], kvld_m[tkb][1]),
                                         (dvv[vslot], kvld_m[tkb][0]),
                                         (sa, exp_d.get(tkb - 2, 0))]
                                        if k == 0 else ()), inc=sp)
                        qk_d[tkb] = sp.n
                        pslice = probs[:, tkb * TQ:(tkb + 1) * TQ]
                        c.op("scalar", "activation", pslice, scs,
                             AF.Exp, scale=1.0 / 32.0,
                             waits=[(sp, qk_d[tkb]), (dkv, tri_done)], inc=sa)
                        exp_d[tkb] = sa.n
                        if t0 + 128 > q0:
                            moff = (t0 - q0) // 128 * TQ
                            c.op("vector", "tensor_tensor", pslice,
                                 pslice, trim[:, moff:moff + TQ], ALU.mult,
                                 waits=[(sa, exp_d[tkb])], inc=sv)
                            pw_d[tkb] = (sv, sv.n)
                        else:
                            pw_d[tkb] = (sa, exp_d[tkb])

                    def emit_pv3a(tkb):
                        vslot = tkb % 3
                        for mi, m in enumerate(range(mlo, mlo + 4)):
                            w_av = []
                            if mi == 0:
                                w_av.append((dvv[vslot], kvld_m[tkb][0]))
                                if half == 0:
                                    w_av.append(pw_d[tkb])
                                if tkb == 0:
                                    w_av.append((sv, sv.n))
                            c.op("tensor", "matmul",
                                 ps_y[:, mi * TQ:(mi + 1) * TQ],
                                 vbuf[:, vslot * D + m * 128:
                                      vslot * D + (m + 1) * 128],
                                 probs[:, tkb * TQ:(tkb + 1) * TQ],
                                 start=(tkb == 0), stop=(tkb == ntk - 1),
                                 waits=w_av, inc=sp)
                        if half == 0:
                            c.op("tensor", "matmul", ps_dn[:],
                                 ones_col[:, 0:1],
                                 probs[:, tkb * TQ:(tkb + 1) * TQ],
                                 start=(tkb == 0), stop=(tkb == ntk - 1),
                                 inc=sp)
                        kslot_free[tkb] = sp.n
                        vslot_free[tkb] = sp.n

                    if half == 0:
                        for t_ in range(ntk + 1):
                            if t_ < ntk:
                                emit_ld3a(t_)
                                emit_qk3a(t_)
                            if t_ >= 1:
                                emit_pv3a(t_ - 1)
                    else:
                        for t_ in range(ntk):
                            emit_ld3a(t_)
                            emit_pv3a(t_)
                    av_done = sp.n
                    if half == 0:
                        c.op("vector", "reciprocal", rden[:], ps_dn[:],
                             waits=[(sp, av_done), (drd, drd.n)], inc=sv)
                        c.bounce_bcast(rden[:], TQ, rdbc[:], 128,
                                       waits=[(sv, sv.n)],
                                       waits2=[(sv, prev_yv3a[0])], inc=drd)
                        rd_done = drd.n
                        c.load_x(xsl2, q0, q0 + TQ,
                                 waits=[(dst_, dst_.n)], inc=dxl)
                        xld = dxl.n
                    for mi, m in enumerate(range(mlo, mlo + 4)):
                        c.op("vector", "tensor_tensor", yv[:],
                             ps_y[:, mi * TQ:(mi + 1) * TQ], rdbc[:], ALU.mult,
                             waits=[(drd, rd_done), (sp, av_done),
                                    (dxl, xld)], inc=sv)
                        prev_yv3a[0] = sv.n
                        c.op("vector", "tensor_tensor",
                             xsl2[:, m * TQ:(m + 1) * TQ],
                             xsl2[:, m * TQ:(m + 1) * TQ], yv[:], ALU.add,
                             inc=sv)
                c.store_x(xsl2, q0, q0 + TQ, waits=[(sv, sv.n)], inc=dst_)
            c.wait("sync", dst_, dst_.n)
            c.barrier()



def st_l2a(c, W):
    import os
    dbg = os.environ.get("L2A_DBG") == "1"
    kind = "ExternalOutput" if dbg else "Internal"
    TCH = 512
    NCH = T // TCH
    sv, sa, sp = c.sem("sv"), c.sem("sa"), c.sem("sp")
    dxs, dst_, dqs = c.sem("dxs2"), c.sem("dst"), c.sem("dqs")
    qd = c.dram("dbg_qd2", [D, T], BF16, kind=kind)
    kd = c.dram("dbg_kd2", [D, T], BF16, kind=kind)
    vd = c.dram("dbg_vd2", [T, D], BF16, kind=kind)
    yd = c.dram("dbg_yd2", [D, T], BF16, kind=kind)
    c2d = c.dram("dbg_c2", [128, 128], BF16, kind=kind) if dbg else None
    dba = c.dram("dbg_a", [33, 1], F32, kind=kind) if dbg else None
    dbr = c.dram("dbg_ar", [1, 33], F32, kind=kind) if dbg else None
    dbf = c.dram("dbg_f", [64, 128], BF16, kind=kind) if dbg else None
    with c.scope() as sc:
        rows_sc = c.scope()
        rows, rows_done = ln_rows(c, rows_sc, "l2a")
        abd = c.dram(c.fresh("abd"), [2 * T], F32)
        dab = c.sem("dab")
        c.dma("sync", AP(abd.tensor, 0, [[2 * T, 1], [1, 2 * T]]), rows[0:1, :],
              waits=[(sv, rows_done)], inc=dab)
        c.wait("sync", dab, dab.n)
        c.barrier()
        rows_sc.close()
        C2 = c.sb_in(sc, c.fresh("C2"), [128, 128], BF16)
        with c.scope() as s0:
            mrow = load_row(c, s0, W["L2_attn_mask_logits"], 33, "ml", inc=dxs)
            wf = load_row(c, s0, W["c_fftw"], 33, "wf", inc=dxs)
            Fc = load_mat(c, s0, W["c_fftc"], 33, 64, "Fc", inc=dxs)
            Fs = load_mat(c, s0, W["c_ffts"], 33, 64, "Fs", inc=dxs)
            Fcb = c.sb_in(s0, c.fresh("Fcb"), [64, 64], BF16)
            Fsb = c.sb_in(s0, c.fresh("Fsb"), [64, 64], BF16)
            Fcs = c.sb_in(s0, c.fresh("Fcs"), [64, 64], BF16)
            Fss = c.sb_in(s0, c.fresh("Fss"), [64, 64], BF16)
            for t_ in (Fcb, Fsb, Fcs, Fss):
                c.memset("vector", t_[:], 0.0)
            arow = c.sb_in(s0, c.fresh("arow"), [1, 33], F32)
            arowb = c.sb_in(s0, c.fresh("arowb"), [1, 128], BF16)
            onect = c.sb_in(s0, c.fresh("one1"), [1, 1], BF16)
            acol = c.sb_in(s0, c.fresh("acol"), [33, 1], F32)
            c64 = c.sb_in(s0, c.fresh("c64"), [64, 64], BF16)
            psa = c.ps_in(s0, c.fresh("psa"), [128, 1], F32)
            psc = c.ps_in(s0, c.fresh("psc"), [64, 64], F32)
            ld0 = dxs.n
            c.memset("vector", onect[:], 1.0)
            c.op("scalar", "activation", arow[:], mrow[:], AF.Sigmoid,
                 waits=[(dxs, ld0)], inc=sa)
            c.op("vector", "tensor_tensor", arow[:], arow[:], wf[:], ALU.mult,
                 waits=[(sa, sa.n)], inc=sv)
            c.memset("vector", arowb[:], 0.0)
            c.op("vector", "tensor_copy", arowb[0:1, 0:33], arow[:], inc=sv)
            c.op("tensor", "matmul", psa[:], arowb[0:1, :], onect[0:1, :],
                 start=True, stop=True, waits=[(sv, sv.n)], inc=sp)
            c.op("vector", "tensor_copy", acol[:], psa[0:33, :],
                 waits=[(sp, sp.n)], inc=sv)
            c.op("vector", "tensor_copy", Fcb[0:33, :], Fc[:], waits=[(dxs, ld0)], inc=sv)
            c.op("vector", "tensor_copy", Fsb[0:33, :], Fs[:], inc=sv)
            acb = acol[:, 0:1].broadcast_to([33, 64])
            c.op("vector", "tensor_tensor", Fcs[0:33, :], Fc[:], acb,
                 ALU.mult, waits=[(sv, sv.n)], inc=sv)
            c.op("vector", "tensor_tensor", Fss[0:33, :], Fs[:], acb,
                 ALU.mult, inc=sv)
            c.op("tensor", "matmul", psc[:], Fcs[0:64, :], Fcb[0:64, :],
                 start=True, stop=False, waits=[(sv, sv.n)], inc=sp)
            c.op("tensor", "matmul", psc[:], Fss[0:64, :], Fsb[0:64, :],
                 start=False, stop=True, inc=sp)
            c.op("vector", "tensor_copy", c64[:], psc[:], waits=[(sp, sp.n)], inc=sv)
            c.memset("vector", C2[:], 0.0, waits=[(sv, sv.n)])
            c.op("vector", "tensor_copy", C2[0:64, 0:64], c64[:], inc=sv)
            c.op("vector", "tensor_copy", C2[64:128, 64:128], c64[:], inc=sv)
            c2_done = sv.n
            if dbg:
                c.dma("sync", c2d, C2[:], waits=[(sv, c2_done)], inc=dxs)
                c.dma("sync", dba, acol[:], waits=[(sv, c2_done)], inc=dxs)
                c.dma("sync", dbr, arow[:], waits=[(sv, c2_done)], inc=dxs)
                c.dma("sync", dbf[:, 0:64], Fcs[:], waits=[(sv, c2_done)], inc=dxs)
                c.dma("sync", dbf[:, 64:128], Fcb[:], waits=[(sv, c2_done)], inc=dxs)
            c.barrier()
        with c.scope() as p1:
            ab_bc = c.sb_in(p1, c.fresh("abbc"), [128, 2 * T], F32)
            dwb = c.sem("dwb")
            c.dma("sync", ab_bc[:], AP(abd.tensor, 0, [[0, 128], [1, 2 * T]]),
                  waits=[(dab, dab.n)], inc=dwb)
            bc_done = dwb.n
            xsl = c.sb_in(p1, c.fresh("xsl"), [128, ND * TCH], F32)
            xn = c.sb_in(p1, c.fresh("xn"), [128, ND * TCH], BF16)
            qraw = c.sb_in(p1, c.fresh("qraw"), [128, TCH], BF16)
            qacc = c.sb_in(p1, c.fresh("qacc"), [128, ND * TCH], BF16)
            vtmp = c.sb_in(p1, c.fresh("vtmp"), [128, (TCH // 128) * D], BF16)
            pp = c.ps_in(p1, c.fresh("ppl2a"), [128, 4 * TB], F32)
            ppf = c.ps_in(p1, c.fresh("ppf"), [128, 2 * TB], F32)
            wbufs = [c.sb_in(p1, c.fresh("wbl2a"), [128, 8 * 512], BF16)
                     for _ in range(2)]
            wl_sems = [c.sem("wl_l2a0"), c.sem("wl_l2a1")]
            seq = c.sem("seq2a")
            sef = c.sem("sef2a")
            sev = c.sem("sev2a")
            for ch in range(NCH):
                t0 = ch * TCH
                c.load_x(xsl, t0, t0 + TCH,
                         waits=[(seq, seq.n), (sev, sev.n)], inc=dxs)
                emit_xn_slice(c, xn[:], xsl[:], ab_bc, t0, TCH,
                              waits=[(dxs, dxs.n), (dwb, bc_done), (sp, sp.n)])
                xn_done = sv.n
                for which, lo, dst in ((0, 0, qd), (1, D, kd)):
                    def evac_filt(mi, tb, ps, mmc, which=which, lo=lo):
                        m = (mi - lo) // 128
                        c.op("vector", "tensor_copy", qraw[:], ps,
                             waits=[(sp, mmc), (sef, sef.n)], inc=sv)
                        cpd = sv.n
                        par = m % 2
                        fps = ppf[:, par * TB:(par + 1) * TB]
                        c.op("tensor", "matmul", fps, C2[:], qraw[:],
                             start=True, stop=True,
                             waits=[(sv, cpd), (sef, max(0, sef.n - 1))], inc=sp)
                        c.op("vector", "tensor_copy",
                             qacc[:, m * TCH:(m + 1) * TCH], fps,
                             waits=[(sp, sp.n)], inc=seq)
                        sef.n += 1
                        def _fn(e, s=sef):
                            e.sem_inc(s.h, 1)
                        c.emit("vector", _fn)

                    proj(c, sc, lambda k, tb: xn[:, k * TCH:(k + 1) * TCH],
                         W["L2_attn_w_qkv"], D, lo, lo + D, 1, evac_filt,
                         "l2aqk", evac_sem=seq,
                         first_waits=[(sv, xn_done), (seq, seq.n)],
                         pp=pp, wbufs=wbufs, wl_sems=wl_sems, mblk=512)
                    c.dma("sync",
                          dst.rearrange("(dc p) t -> p dc t", p=128)[:, :, t0:t0 + TCH],
                          qacc[:].rearrange("p (dc t) -> p dc t", dc=ND),
                          waits=[(seq, seq.n)], inc=dqs)
                ntt = TCH // 128
                load_w_tiles(c, sc, W["L2_attn_w_qkv"], D, 2 * D, 2 * D + 512,
                             "wv2", waits=[(sp, sp.n)], inc=wl_sems[0],
                             handle=wbufs[0])
                load_w_tiles(c, sc, W["L2_attn_w_qkv"], D, 2 * D + 512, 3 * D,
                             "wv2", waits=[(sp, sp.n)], inc=wl_sems[1],
                             handle=wbufs[1])
                ev_hist = {}
                g = 0
                for nb in range(2):
                    for tt in range(ntt):
                        par = g % 2
                        ps = pp[:, par * TB: par * TB + TB]
                        w0 = [(wl_sems[nb], wl_sems[nb].n), (seq, seq.n)]
                        if g >= 2:
                            w0.append((sev, ev_hist[g - 2]))
                        for k in range(ND):
                            lhsT = xn[:, k * TCH + tt * 128: k * TCH + tt * 128 + 128]
                            c.op("tensor", "matmul", ps,
                                 lhsT, wbufs[nb][:, k * 512:(k + 1) * 512],
                                 start=(k == 0), stop=(k == ND - 1),
                                 waits=(w0 if k == 0 else ()), inc=sp)
                        c.op("vector", "tensor_copy",
                             vtmp[:, tt * D + nb * 512: tt * D + nb * 512 + 512],
                             ps, waits=[(sp, sp.n), (dqs, dqs.n)], inc=sev)
                        ev_hist[g] = sev.n
                        g += 1
                c.dma("sync",
                      vd[t0:t0 + TCH, :].rearrange("(tt p) dd -> p tt dd", p=128),
                      vtmp[:].rearrange("p (tt dd) -> p tt dd", tt=ntt),
                      waits=[(sev, sev.n)], inc=dqs)
                c.wfree[id(wbufs[0])] = sp.n
                c.wfree[id(wbufs[1])] = sp.n
            c.wait("sync", dqs, dqs.n)
            c.barrier()
        with c.scope() as p2:
            TQ = 512
            NKB = T // 512
            NTK = T // 128
            qbuf = c.sb_in(p2, c.fresh("qbuf"), [64, 2 * TQ], BF16)
            kbuf = c.sb_in(p2, c.fresh("kbuf"), [64, 4 * 512], BF16)
            vbuf = c.sb_in(p2, c.fresh("vbuf"), [128, 4 * 4 * 96], BF16)
            prob = c.sb_in(p2, c.fresh("prob"), [128, 3 * TQ], BF16)
            rden = c.sb_in(p2, c.fresh("rden"), [65, 2 * TQ], F32)
            rdbc = c.sb_in(p2, c.fresh("rdbc"), [64, 2 * TQ], F32)
            ytmp = c.sb_in(p2, c.fresh("ytmp"), [64, 2 * TQ], BF16)
            ps_y = c.ps_in(p2, c.fresh("psy"), [96, 2 * TQ], F32)
            ps_sc = c.ps_in(p2, c.fresh("pssc"), [128, 3 * TQ], F32)
            dqq = c.sem("dqq2a")
            dkk = [c.sem(f"dkk2a{i}") for i in range(4)]
            dvv = [c.sem(f"dvv2a{i}") for i in range(4)]
            drd = c.sem("drd2a")
            sdy = c.sem("sdy")
            c.memset("vector", vbuf[:], 0.0)
            vones = vbuf[:].rearrange("p (s dd) -> p s dd", s=16)[:, :, 64:65]
            c.op("vector", "memset", vones, 1.0, inc=sv)
            ones_done = sv.n
            seq_list = [(h, tqb) for h in range(16) for tqb in range(T // TQ)]
            NIT = len(seq_list)
            it_hist = {}
            kfree = {}
            kvld_map = {}
            qld_map = {}
            exp_done = {}
            pv_done = {}
            qk_done = {}

            def emit_qload(j):
                if j >= NIT:
                    return
                h, tqb = seq_list[j]
                ipj = j % 2
                h2 = it_hist.get(j - 2, {})
                c.dma("sync", qbuf[0:64, ipj * TQ:(ipj + 1) * TQ],
                      qd[h * 64:h * 64 + 64, tqb * TQ:tqb * TQ + TQ],
                      waits=[(sp, h2.get("avdone", 0))], inc=dqq)
                qld_map[j] = dqq.n

            def emit_kvload(j, kB):
                if j >= NIT:
                    return
                h, _ = seq_list[j]
                hr = h * 64
                gkB = j * NKB + kB
                kslot = gkB % 4
                kw = []
                if gkB - 4 in kfree:
                    kw.append((sp, kfree[gkB - 4]))
                if gkB == 0:
                    kw.append((sv, ones_done))
                c.dma("sync", kbuf[:, kslot * 512:(kslot + 1) * 512],
                      kd[hr:hr + 64, kB * 512:(kB + 1) * 512],
                      waits=kw, inc=dkk[kslot])
                c.dma("sync",
                      vbuf[:, kslot * 384:(kslot + 1) * 384]
                      .rearrange("p (tt dd) -> p tt dd", tt=4)[:, :, 0:64],
                      vd[kB * 512:(kB + 1) * 512, hr:hr + 64]
                      .rearrange("(tt p) dd -> p tt dd", p=128),
                      inc=dvv[kslot])
                kvld_map[gkB] = (dkk[kslot].n, dvv[kslot].n)

            def emit_qk(g):
                j, tkb = divmod(g, NTK)
                kB, sub = divmod(tkb, 4)
                gkB = j * NKB + kB
                kslot = gkB % 4
                ipj = j % 2
                spar = g % 3
                scs = ps_sc[:, spar * TQ:(spar + 1) * TQ]
                w = [(dkk[kslot], kvld_map[gkB][0]), (dqq, qld_map[j])]
                if g - 3 in exp_done:
                    w.append((sa, exp_done[g - 3]))
                c.op("tensor", "matmul", scs,
                     kbuf[0:64, kslot * 512 + sub * 128:
                          kslot * 512 + sub * 128 + 128],
                     qbuf[0:64, ipj * TQ:(ipj + 1) * TQ],
                     start=True, stop=True, waits=w, inc=sp)
                qk_done[g] = sp.n
                if sub == 0:
                    if kB + 1 < NKB:
                        emit_kvload(j, kB + 1)
                    else:
                        emit_kvload(j + 1, 0)
                if tkb == 4:
                    emit_qload(j + 1)
                pslice = prob[:, spar * TQ:(spar + 1) * TQ]
                ew = [(sp, qk_done[g])]
                if g - 3 in pv_done:
                    ew.append((sp, pv_done[g - 3]))
                c.op("scalar", "activation", pslice, scs, AF.Exp,
                     scale=1.0 / 8.0, waits=ew, inc=sa)
                exp_done[g] = sa.n

            def emit_tail(j):
                h, tqb = seq_list[j]
                ipj = j % 2
                av_done = sp.n
                rdv = rden[64:65, ipj * TQ:(ipj + 1) * TQ]
                rbv = rdbc[0:64, ipj * TQ:(ipj + 1) * TQ]
                ytv = ytmp[0:64, ipj * TQ:(ipj + 1) * TQ]
                h2 = it_hist.get(j - 2, {})
                c.op("vector", "reciprocal", rdv,
                     ps_y[64:65, ipj * TQ:(ipj + 1) * TQ],
                     waits=[(sp, av_done), (drd, h2.get("rddone", 0))],
                     inc=sv)
                c.bounce_bcast(rdv, TQ, rbv, 64, waits=[(sv, sv.n)],
                               waits2=[(sv, h2.get("ytdone", 0))], inc=drd)
                rd_done = drd.n
                c.op("vector", "tensor_tensor", ytv,
                     ps_y[0:64, ipj * TQ:(ipj + 1) * TQ], rbv, ALU.mult,
                     waits=[(drd, rd_done), (sdy, h2.get("stdone", 0))],
                     inc=sv)
                yt_done = sv.n
                c.dma("sync", yd[h * 64:h * 64 + 64, tqb * TQ:tqb * TQ + TQ],
                      ytv, waits=[(sv, yt_done)], inc=sdy)
                it_hist[j] = {"avdone": av_done, "ytdone": yt_done,
                              "stdone": sdy.n, "rddone": rd_done}

            def emit_pv(g):
                j, tkb = divmod(g, NTK)
                kB, sub = divmod(tkb, 4)
                gkB = j * NKB + kB
                kslot = gkB % 4
                ipj = j % 2
                spar = g % 3
                w = [(sa, exp_done[g]), (dvv[kslot], kvld_map[gkB][1])]
                if tkb == 0:
                    h2 = it_hist.get(j - 2, {})
                    w.append((sv, h2.get("ytdone", 0)))
                c.op("tensor", "matmul", ps_y[:, ipj * TQ:(ipj + 1) * TQ],
                     vbuf[:, kslot * 384 + sub * 96:
                          kslot * 384 + sub * 96 + 96],
                     prob[:, spar * TQ:(spar + 1) * TQ],
                     start=(tkb == 0), stop=(tkb == NTK - 1),
                     waits=w, inc=sp)
                pv_done[g] = sp.n
                if sub == 3:
                    kfree[gkB] = sp.n
                if tkb == NTK - 1:
                    emit_tail(j)

            emit_qload(0)
            emit_kvload(0, 0)
            G = NIT * NTK
            for g in range(G + 3):
                if g < G:
                    emit_qk(g)
                if g >= 3:
                    emit_pv(g - 3)
            c.wait("sync", sdy, sdy.n)
            c.barrier()
        with c.scope() as p3:
            ybuf = c.sb_in(p3, c.fresh("ybuf"), [128, ND * T], BF16)
            xsl3 = c.sb_in(p3, c.fresh("xsl3"), [128, ND * TB], F32)
            pp3 = c.ps_in(p3, c.fresh("pp3"), [128, 4 * TB], F32)
            wbufs3 = [c.sb_in(p3, c.fresh("wb3"), [128, 8 * 512], BF16)
                      for _ in range(2)]
            wl3 = [c.sem("wl_l2o0"), c.sem("wl_l2o1")]
            bo = load_row(c, p3, W["L2_attn_b_out"], D, "bo", dtype=BF16, inc=dxs)
            c.dma("sync", ybuf[:].rearrange("p (dc t) -> p dc t", dc=ND),
                  yd.rearrange("(dc p) t -> p dc t", p=128), inc=dxs)
            yld = dxs.n
            seo = c.sem("seo2a")
            ones_row = c.consts["ones_row"]
            for tb in range(NT):
                t0 = tb * TB
                c.load_x(xsl3, t0, t0 + TB, waits=[(dst_, dst_.n)], inc=dxs)
                xld = dxs.n

                def evac_o(mi, tbb, ps, mmc):
                    d = mi // 128
                    c.op("vector", "scalar_tensor_tensor",
                         xsl3[:, d * TB:(d + 1) * TB], ps, 1.0,
                         xsl3[:, d * TB:(d + 1) * TB], ALU.mult, ALU.add,
                         waits=[(sp, mmc), (dxs, xld)], inc=seo)

                proj(c, p3, lambda k, tbb: ybuf[:, k * T + t0:k * T + t0 + TB],
                     W["L2_attn_w_out"], D, 0, D, 1, evac_o, "l2ao",
                     evac_sem=seo, first_waits=[(dxs, yld), (seo, seo.n)],
                     pp=pp3, wbufs=wbufs3, wl_sems=wl3,
                     extra_k=(lambda mi: bo[0:1, mi: mi + 128],
                              lambda tbb: ones_row[0:1, 0:TB]))
                c.store_x(xsl3, t0, t0 + TB, waits=[(seo, seo.n)], inc=dst_)
            c.wait("sync", dst_, dst_.n)
            c.barrier()



def st_l2f(c, W):
    TCH = 1024
    NCH = T // TCH
    NB = TCH // TB
    NTT = TCH // 128
    sv, sa, sp = c.sem("sv"), c.sem("sa"), c.sem("sp")
    dxs, dst_ = c.sem("dxs2"), c.sem("dst")
    with c.scope() as sc:
        rows, rows_done = ln_rows(c, sc, "l2f")
        ab_bc = c.sb_in(sc, c.fresh("abbc"), [128, 2 * T], F32)
        dwb = c.sem("dwb")
        c.bounce_bcast(rows[0:1, :], 2 * T, ab_bc[:], 128,
                       waits=[(sv, rows_done)], inc=dwb)
        bc_done = dwb.n
        rw = c.sb_in(sc, c.fresh("rw"), [128, 8 * ND], BF16)
        drw = c.sem("drw")
        load_w_tiles(c, sc, W["L2_ffn_router_w"], D, 0, 8, "rw", inc=drw,
                     handle=rw)
        rw_done = drw.n
        xsl = c.sb_in(sc, c.fresh("xsl"), [128, ND * TCH], F32)
        xn = c.sb_in(sc, c.fresh("xn"), [128, ND * TCH], BF16)
        h = c.sb_in(sc, c.fresh("h"), [128, 8 * TCH], BF16)
        rtmp = c.sb_in(sc, c.fresh("rtmp"), [128, TB], BF16)
        acc = c.sb_in(sc, c.fresh("acc"), [128, ND * TCH], F32)
        web = c.sb_in(sc, c.fresh("web"), [128, TCH], BF16)
        probs = c.sb_in(sc, c.fresh("probs"), [128, NTT * 8], F32)
        cwb = c.sb_in(sc, c.fresh("cwb"), [128, NTT * 8], BF16)
        cwT = c.sb_in(sc, c.fresh("cwT"), [8, TCH], BF16)
        g1 = c.sb_in(sc, c.fresh("g1"), [128, 8], F32)
        g2 = c.sb_in(sc, c.fresh("g2"), [128, 8], F32)
        gm = c.sb_in(sc, c.fresh("gm"), [128, 4], F32)
        pp = c.ps_in(sc, c.fresh("ppl2f"), [128, 4 * TB], F32)
        psr = c.ps_in(sc, c.fresh("psr"), [128, 8], F32)
        pst = c.ps_in(sc, c.fresh("pst"), [8, 128], F32)
        wbufs = [c.sb_in(sc, c.fresh("wbl2f"), [128, 8 * 512], BF16)
                 for _ in range(2)]
        wl_sems = [c.sem("wl_l2f0"), c.sem("wl_l2f1")]
        identb = c.consts["identb"]
        seh = c.sem("seh2f")
        seo = c.sem("seo2f")
        set_ = c.sem("set2f")
        dcw = c.sem("dcw")
        import os
        l2fdbg = os.environ.get("L2F_DBG") == "1"
        cwd = c.dram("dbg_cwd2f", [8, T], BF16,
                     kind="ExternalOutput" if l2fdbg else "Internal")
        cbd = c.dram("dbg_cbd2f", [NCH, 128, NTT * 8], BF16,
                     kind="ExternalOutput" if l2fdbg else "Internal")
        pbd = c.dram("dbg_pbd2f", [NCH, 128, NTT * 8], F32,
                     kind="ExternalOutput" if l2fdbg else "Internal")

        for ch in range(NCH):
            t0 = ch * TCH
            wl = [(dst_, dst_.n)] if ch else []
            c.load_x(xsl, t0, t0 + TCH, waits=wl, inc=dxs)
            emit_xn_slice(c, xn[:], xsl[:], ab_bc, t0, TCH,
                          waits=[(dxs, dxs.n), (dwb, bc_done), (sp, sp.n)])
            xn_done = sv.n
            for tt in range(NTT):
                for k in range(ND):
                    c.op("tensor", "matmul", psr[:],
                         xn[:, k * TCH + tt * 128: k * TCH + tt * 128 + 128],
                         rw[:, k * 8:(k + 1) * 8],
                         start=(k == 0), stop=(k == ND - 1),
                         waits=([(sv, xn_done), (drw, rw_done),
                                 (sv, sv.n)] if k == 0 else ()), inc=sp)
                pv = probs[:, tt * 8:(tt + 1) * 8]
                c.op("scalar", "activation", pv, psr[:], AF.Sigmoid,
                     waits=[(sp, sp.n)], inc=sa)
                c.op("vector", "reduce_max", gm[:, 0:1], pv,
                     axis=mybir.AxisListType.X, waits=[(sa, sa.n)], inc=sv)
                c.op("vector", "tensor_scalar", g1[:, 0:8], pv, gm[:, 0:1],
                     None, ALU.is_equal, waits=[(sv, sv.n)], inc=sv)
                c.op("vector", "scalar_tensor_tensor", g2[:, 0:8], g1[:, 0:8],
                     -1e30, pv, ALU.mult, ALU.add, waits=[(sv, sv.n)], inc=sv)
                c.op("vector", "reduce_max", gm[:, 1:2], g2[:, 0:8],
                     axis=mybir.AxisListType.X, waits=[(sv, sv.n)], inc=sv)
                c.op("vector", "tensor_scalar", g2[:, 0:8], g2[:, 0:8],
                     gm[:, 1:2], None, ALU.is_equal, waits=[(sv, sv.n)], inc=sv)
                c.op("vector", "tensor_tensor", gm[:, 2:3], gm[:, 0:1],
                     gm[:, 1:2], ALU.add, waits=[(sv, sv.n)], inc=sv)
                c.op("vector", "tensor_scalar_add", gm[:, 2:3], gm[:, 2:3],
                     1e-20, waits=[(sv, sv.n)], inc=sv)
                c.op("vector", "reciprocal", gm[:, 3:4], gm[:, 2:3],
                     waits=[(sv, sv.n)], inc=sv)
                c.op("vector", "tensor_tensor", gm[:, 0:1], gm[:, 0:1],
                     gm[:, 3:4], ALU.mult, waits=[(sv, sv.n)], inc=sv)
                c.op("vector", "tensor_tensor", gm[:, 1:2], gm[:, 1:2],
                     gm[:, 3:4], ALU.mult, waits=[(sv, sv.n)], inc=sv)
                c.op("vector", "tensor_scalar", g1[:, 0:8], g1[:, 0:8],
                     gm[:, 0:1], None, ALU.mult, waits=[(sv, sv.n)], inc=sv)
                c.op("vector", "tensor_scalar", g2[:, 0:8], g2[:, 0:8],
                     gm[:, 1:2], None, ALU.mult, waits=[(sv, sv.n)], inc=sv)
                c.op("vector", "tensor_tensor", cwb[:, tt * 8:(tt + 1) * 8],
                     g1[:, 0:8], g2[:, 0:8], ALU.add, waits=[(sv, sv.n)], inc=sv)
                c.op("tensor", "matmul", pst[:],
                     cwb[:, tt * 8:(tt + 1) * 8], identb[:],
                     start=True, stop=True,
                     waits=[(sv, sv.n), (set_, set_.n)], inc=sp)
                c.op("vector", "tensor_copy", cwT[:, tt * 128:(tt + 1) * 128],
                     pst[:], waits=[(sp, sp.n), (dcw, dcw.n)], inc=set_)
            c.dma("sync", cwd[:, t0:t0 + TCH], cwT[:],
                  waits=[(set_, set_.n), (dcw, dcw.n)], inc=dcw)
            if l2fdbg:
                c.dma("sync", cbd[ch], cwb[:], inc=dcw)
                c.dma("sync", pbd[ch], probs[:], inc=dcw)
            cw_st = dcw.n
            for e in range(8):
                c.dma("sync", web[:],
                      AP(cwd.tensor, e * T + t0, [[0, 128], [1, TCH]]),
                      waits=[(dcw, cw_st), (seo, seo.n)], inc=dcw)
                web_done = dcw.n

                def evac_h(mi, tb, ps, mmc, e=e):
                    m = (mi - e * 1024) // 128
                    c.op("vector", "tensor_relu", rtmp[:], ps,
                         waits=[(sp, mmc), (seh, seh.n)], inc=sv)
                    c.op("scalar", "activation",
                         h[:, m * TCH + tb * TB: m * TCH + (tb + 1) * TB],
                         rtmp[:], AF.Square, waits=[(sv, sv.n)], inc=seh)

                proj(c, sc, lambda k, tb: xn[:, k * TCH + tb * TB:
                                             k * TCH + (tb + 1) * TB],
                     W["L2_ffn_w1"], D, e * 1024, (e + 1) * 1024, NB, evac_h,
                     "w1f", evac_sem=seh,
                     first_waits=[(sv, xn_done), (seo, seo.n), (set_, set_.n)],
                     pp=pp, wbufs=wbufs, wl_sems=wl_sems)

                def evac_o(mi, tb, ps, mmc, e=e):
                    d = mi // 128
                    sl = slice(d * TCH + tb * TB, d * TCH + (tb + 1) * TB)
                    wsl = web[:, tb * TB:(tb + 1) * TB]
                    if e == 0:
                        c.op("vector", "tensor_tensor", acc[:, sl], ps, wsl,
                             ALU.mult, waits=[(sp, mmc), (dcw, web_done)],
                             inc=seo)
                    else:
                        c.op("vector", "tensor_tensor", rtmp[:], ps, wsl,
                             ALU.mult, waits=[(sp, mmc), (dcw, web_done)],
                             inc=sv)
                        c.op("vector", "tensor_tensor", acc[:, sl], acc[:, sl],
                             rtmp[:], ALU.add, inc=seo)

                proj(c, sc, lambda k, tb: h[:, k * TCH + tb * TB:
                                            k * TCH + (tb + 1) * TB],
                     W["L2_ffn_w2"], 1024, 0, D, NB, evac_o, "w2f",
                     evac_sem=seo, w_row_lo=e * 1024,
                     first_waits=[(seh, seh.n)],
                     pp=pp, wbufs=wbufs, wl_sems=wl_sems)
            for d in range(ND):
                sl = slice(d * TCH, (d + 1) * TCH)
                c.op("vector", "scalar_tensor_tensor", xsl[:, sl], acc[:, sl],
                     0.5, xsl[:, sl], ALU.mult, ALU.add,
                     waits=[(seo, seo.n)], inc=sv)
            c.store_x(xsl, t0, t0 + TCH, waits=[(sv, sv.n)], inc=dst_)
        c.wait("sync", dst_, dst_.n)
        c.barrier()



def st_l0a(c, W):
    TCH = 512
    NCH = T // TCH
    NM = 16
    sv, sa, sp = c.sem("sv"), c.sem("sa"), c.sem("sp")
    dxs, dst_, dzy = c.sem("dxs2"), c.sem("dst"), c.sem("dzy")
    zd = c.dram(c.fresh("zd0a"), [2048, T], BF16)
    yd = c.dram(c.fresh("yd0a"), [2048, T], BF16)
    with c.scope() as sc:
        rows_sc = c.scope()
        rows, rows_done = ln_rows(c, rows_sc, "l0a")
        abd = c.dram(c.fresh("abd0a"), [2 * T], F32)
        dab = c.sem("dab")
        c.dma("sync", AP(abd.tensor, 0, [[2 * T, 1], [1, 2 * T]]), rows[0:1, :],
              waits=[(sv, rows_done)], inc=dab)
        c.wait("sync", dab, dab.n)
        c.barrier()
        rows_sc.close()
        dgp = c.sem("dgp")
        ipb = load_mat(c, sc, W["L0_attn_in_proj_bc"], 128, 32, "ipb", inc=dgp)
        cwc = load_mat(c, sc, W["L0_attn_conv_wc"], 128, 64, "cwc", inc=dgp)
        cbc = load_mat(c, sc, W["L0_attn_conv_bc"], 128, 16, "cbc", inc=dgp)
        dtb = load_mat(c, sc, W["L0_attn_dt_bc"], 128, 16, "dtb", inc=dgp)
        alc = load_mat(c, sc, W["L0_attn_A_logc"], 128, 16, "alc", inc=dgp)
        dpc = load_mat(c, sc, W["L0_attn_D_paramc"], 128, 16, "dpc", inc=dgp)
        par_done = dgp.n
        negA = c.sb_in(sc, c.fresh("negA"), [128, 16], F32)
        biasA = c.sb_in(sc, c.fresh("biasA"), [128, 16], F32)
        c.op("scalar", "activation", negA[:], alc[:], AF.Exp, scale=1.0,
             waits=[(dgp, par_done)], inc=sa)
        c.op("vector", "tensor_scalar_mul", negA[:], negA[:], -1.0,
             waits=[(sa, sa.n)], inc=sv)
        c.op("vector", "tensor_scalar_mul", biasA[:], negA[:], 1e-4,
             waits=[(sv, sv.n)], inc=sv)
        prm_done = sv.n
        u = c.sb_in(sc, c.fresh("u0a"), [128, NM * T], BF16)
        with c.scope() as p1:
            ab_bc = c.sb_in(p1, c.fresh("abbc"), [128, 2 * TCH], F32)
            xsl = c.sb_in(p1, c.fresh("xsl"), [128, ND * TCH], F32)
            xn = c.sb_in(p1, c.fresh("xn"), [128, ND * TCH], BF16)
            zbuf = c.sb_in(p1, c.fresh("zbuf"), [128, TCH], BF16)
            pp = c.ps_in(p1, c.fresh("pp0a"), [128, 4 * TB], F32)
            wbufs = [c.sb_in(p1, c.fresh("wb0a"), [128, 8 * 512], BF16)
                     for _ in range(2)]
            wl_sems = [c.sem("wl_l0a0"), c.sem("wl_l0a1")]
            sez = c.sem("sez0a")
            dwb = c.sem("dwb")
            for ch in range(NCH):
                t0 = ch * TCH
                asrc = AP(abd.tensor, t0, [[0, 128], [1, TCH]])
                bsrc = AP(abd.tensor, T + t0, [[0, 128], [1, TCH]])
                c.dma("sync", ab_bc[:, 0:TCH], asrc,
                      waits=[(sv, sv.n), (sez, sez.n)], inc=dwb)
                c.dma("sync", ab_bc[:, TCH:2 * TCH], bsrc, inc=dwb)
                c.load_x(xsl, t0, t0 + TCH, inc=dxs)
                emit_xn_slice2(c, xn[:], xsl[:], ab_bc, TCH,
                               waits=[(dxs, dxs.n), (dwb, dwb.n), (sp, sp.n)])
                xn_done = sv.n

                def evac_xp(mi, tb, ps, mmc):
                    m = mi // 128
                    if m < NM:
                        c.op("scalar", "activation",
                             u[:, m * T + t0 + tb * TB: m * T + t0 + (tb + 1) * TB],
                             ps, AF.Identity, bias=ipb[:, m:m + 1],
                             waits=[(sp, mmc), (dgp, par_done)], inc=sez)
                    else:
                        mz = m - NM
                        c.op("scalar", "activation", zbuf[:, tb * TB:(tb + 1) * TB],
                             ps, AF.Sigmoid, bias=ipb[:, m:m + 1],
                             waits=[(sp, mmc), (dzy, dzy.n)], inc=sez)
                        c.dma("sync",
                              zd[mz * 128:(mz + 1) * 128, t0 + tb * TB:t0 + (tb + 1) * TB],
                              zbuf[:, tb * TB:(tb + 1) * TB],
                              waits=[(sez, sez.n)], inc=dzy)

                proj(c, p1, lambda k, tb: xn[:, k * TCH + tb * TB:
                                             k * TCH + (tb + 1) * TB],
                     W["L0_attn_in_proj_w"], D, 0, 4096, TCH // TB, evac_xp,
                     "ip0a", evac_sem=sez,
                     first_waits=[(sv, xn_done), (sez, sez.n)],
                     pp=pp, wbufs=wbufs, wl_sems=wl_sems)
            c.wait("sync", dzy, dzy.n)
            c.barrier()
        with c.scope() as p2:
            acc = c.sb_in(p2, c.fresh("acc0a"), [128, T], F32)
            for m in range(NM):
                uv = u[:, m * T:(m + 1) * T]
                c.op("vector", "tensor_scalar", acc[:, 0:T], uv,
                     cwc[:, m * 4 + 3: m * 4 + 4], None, ALU.mult,
                     waits=[(sa, sa.n), (dgp, par_done)], inc=sv)
                for kk in range(3):
                    sh = 3 - kk
                    c.op("vector", "scalar_tensor_tensor", acc[:, sh:T],
                         u[:, m * T: m * T + T - sh],
                         cwc[:, m * 4 + kk: m * 4 + kk + 1],
                         acc[:, sh:T], ALU.mult, ALU.add,
                         waits=[(sv, sv.n)], inc=sv)
                c.op("scalar", "activation", uv, acc[:, 0:T], AF.Silu,
                     bias=cbc[:, m:m + 1], waits=[(sv, sv.n)], inc=sa)
            c.barrier()
        with c.scope() as p3:
            dcy = c.sb_in(p3, c.fresh("dcy"), [128, T], BF16)
            Bt = c.sb_in(p3, c.fresh("Bt"), [128, T], BF16)
            Ct = c.sb_in(p3, c.fresh("Ct"), [128, T], BF16)
            om = c.sb_in(p3, c.fresh("om"), [128, T], BF16)
            stt_ = c.sb_in(p3, c.fresh("stt"), [128, T], BF16)
            yv = c.sb_in(p3, c.fresh("yv"), [128, T], BF16)
            zy = om
            spl = c.sb_in(p3, c.fresh("spl"), [128, T], BF16)
            pp2 = c.ps_in(p3, c.fresh("pp0a2"), [128, 2 * TB], F32)
            wb2 = [c.sb_in(p3, c.fresh("wb0a2"), [128, 16 * 128], BF16)
                   for _ in range(3)]
            wsem = [c.sem("wl0a_dt"), c.sem("wl0a_b"), c.sem("wl0a_c")]
            se2 = c.sem("se0a2")
            bt_free = ct_free = scan_done = None
            for m in range(NM):
                for wi, wname in enumerate(["L0_attn_dt_w", "L0_attn_b_w",
                                            "L0_attn_c_w"]):
                    load_w_tiles(c, p3, W[wname], 2048, m * 128, (m + 1) * 128,
                                 "w0a2", queue="gpsimd",
                                 waits=[(sp, sp.n)], inc=wsem[wi],
                                 handle=wb2[wi])
                for wi, dest, ldd in ((0, spl, None), (1, Bt, None), (2, Ct, None)):
                    for tb in range(NT):
                        par = tb % 2
                        ps = pp2[:, par * TB:(par + 1) * TB]
                        w0 = [(wsem[wi], wsem[wi].n), (se2, max(0, se2.n - 1))]
                        for k in range(16):
                            c.op("tensor", "matmul", ps,
                                 wb2[wi][:, k * 128:(k + 1) * 128],
                                 u[:, k * T + tb * TB: k * T + (tb + 1) * TB],
                                 start=(k == 0), stop=(k == 15),
                                 waits=(w0 if k == 0 else ()), inc=sp)
                        sl = slice(tb * TB, (tb + 1) * TB)
                        if wi == 0:
                            c.op("scalar", "activation", spl[:, sl], ps,
                                 AF.Exp, bias=dtb[:, m:m + 1],
                                 waits=[(sp, sp.n)], inc=se2)
                        elif wi == 1:
                            c.op("scalar", "activation", dest[:, sl], ps,
                                 AF.Tanh,
                                 waits=[(sp, sp.n),
                                        (sv, bt_free if bt_free is not None
                                         else sv.n)],
                                 inc=se2)
                        else:
                            c.op("scalar", "activation", dest[:, sl], ps,
                                 AF.Tanh,
                                 waits=[(sp, sp.n),
                                        (sv, ct_free if ct_free is not None
                                         else sv.n)],
                                 inc=se2)
                c.op("scalar", "activation", spl[:], spl[:], AF.Ln, bias=1.0,
                     waits=[(se2, se2.n)], inc=sa)
                c.op("scalar", "activation", dcy[:], spl[:], AF.Exp,
                     scale=negA[:, m:m + 1], bias=biasA[:, m:m + 1],
                     waits=[(sa, sa.n),
                            (sv, scan_done if scan_done is not None
                             else sv.n)], inc=sa)
                c.op("vector", "tensor_scalar", om[:], dcy[:], -1.0, 1.0,
                     ALU.mult, ALU.add, waits=[(sa, sa.n)], inc=sv)
                c.op("vector", "tensor_tensor", om[:], om[:], Bt[:], ALU.mult,
                     waits=[(sv, sv.n)], inc=sv)
                bt_free = sv.n
                c.op("vector", "tensor_tensor", om[:], om[:],
                     u[:, m * T:(m + 1) * T], ALU.mult, waits=[(sv, sv.n)],
                     inc=sv)
                c.op("vector", "tensor_tensor_scan", stt_[:], dcy[:], om[:],
                     0.0, ALU.mult, ALU.add, waits=[(sv, sv.n)], inc=sv)
                scan_done = sv.n
                c.op("vector", "tensor_tensor", yv[:], Ct[:], stt_[:],
                     ALU.mult, waits=[(sv, sv.n)], inc=sv)
                ct_free = sv.n
                c.op("vector", "scalar_tensor_tensor", yv[:],
                     u[:, m * T:(m + 1) * T], dpc[:, m:m + 1], yv[:],
                     ALU.mult, ALU.add, waits=[(sv, sv.n)], inc=sv)
                c.dma("sync", zy[:], zd[m * 128:(m + 1) * 128, :],
                      waits=[(sv, sv.n), (dzy, dzy.n)], inc=dzy)
                c.op("vector", "tensor_tensor", zy[:], yv[:], zy[:], ALU.mult,
                     waits=[(dzy, dzy.n), (sv, sv.n)], inc=sv)
                c.dma("sync", yd[m * 128:(m + 1) * 128, :], zy[:],
                      waits=[(sv, sv.n)], inc=dzy)
            c.wait("sync", dzy, dzy.n)
            c.barrier()
    with c.scope() as p4:
        sv, sa, sp = c.sem("sv"), c.sem("sa"), c.sem("sp")
        ybuf = c.sb_in(p4, c.fresh("ybuf0a"), [128, 16 * T], BF16)
        xsl4 = c.sb_in(p4, c.fresh("xsl4"), [128, ND * TB], F32)
        ob = load_row(c, p4, W["L0_attn_out_b"], D, "ob", dtype=BF16, inc=dxs)
        c.dma("sync", ybuf[:].rearrange("p (mc t) -> p mc t", mc=16),
              yd.rearrange("(mc p) t -> p mc t", p=128), inc=dxs)
        yld = dxs.n
        seo = c.sem("seo0a")
        ones_row = c.consts["ones_row"]
        pp4 = c.ps_in(p4, c.fresh("pp0a4"), [128, 4 * TB], F32)
        wb4 = [c.sb_in(p4, c.fresh("wb0a4"), [128, 16 * 128], BF16)
               for _ in range(2)]
        wl4 = [c.sem("wl0a40"), c.sem("wl0a41")]
        for tb in range(NT):
            t0 = tb * TB
            c.load_x(xsl4, t0, t0 + TB, waits=[(dst_, dst_.n)], inc=dxs)
            xld = dxs.n

            def evac_o(mi, tbb, ps, mmc):
                d = mi // 128
                c.op("vector", "scalar_tensor_tensor",
                     xsl4[:, d * TB:(d + 1) * TB], ps, 1.0,
                     xsl4[:, d * TB:(d + 1) * TB], ALU.mult, ALU.add,
                     waits=[(sp, mmc), (dxs, xld)], inc=seo)

            proj(c, p4, lambda k, tbb: ybuf[:, k * T + t0:k * T + t0 + TB],
                 W["L0_attn_out_w"], 2048, 0, D, 1, evac_o, "o0a",
                 evac_sem=seo, mblk=128,
                 first_waits=[(dxs, yld), (seo, seo.n)],
                 pp=pp4, wbufs=wb4, wl_sems=wl4,
                 extra_k=(lambda mi: ob[0:1, mi: mi + 128],
                          lambda tbb: ones_row[0:1, 0:TB]))
            c.store_x(xsl4, t0, t0 + TB, waits=[(seo, seo.n)], inc=dst_)
        c.wait("sync", dst_, dst_.n)
        c.barrier()


def emit_xn_slice2(c, xn_v, xsl_v, ab_bc, width, waits=(), inc=None, nd=ND):
    sv = c.sem("sv")
    a3 = ab_bc[:, 0:width].unsqueeze(1).broadcast_to([128, nd, width])
    b3 = ab_bc[:, width:2 * width].unsqueeze(1).broadcast_to([128, nd, width])
    x3 = xsl_v.rearrange("p (dc t) -> p dc t", dc=nd)
    n3 = xn_v.rearrange("p (dc t) -> p dc t", dc=nd)
    c.op("vector", "tensor_tensor", n3, x3, a3, ALU.mult, waits=waits, inc=sv)
    c.op("vector", "tensor_tensor", n3, n3, b3, ALU.subtract, inc=inc or sv)



def st_l1a(c, W):
    TCH = 512
    NCH = T // TCH
    NM = 16
    NS = 16
    TC2 = 256
    NC2 = T // TC2
    sv, sa, sp = c.sem("sv"), c.sem("sa"), c.sem("sp")
    dxs, dst_, dzy = c.sem("dxs2"), c.sem("dst"), c.sem("dzy")
    import os
    l1adbg = os.environ.get("L1A_DBG") == "1"
    kind = "ExternalOutput" if l1adbg else "Internal"
    zd = c.dram("dbg_zd1a", [2048, T], BF16, kind=kind)
    yd = c.dram("dbg_yd1a", [2048, T], BF16, kind=kind)
    ssmd = c.dram("dbg_ssm1a", [64, T], F32, kind=kind) if l1adbg else None
    xcd = c.dram("dbg_xc1a", [2048, T], BF16, kind=kind) if l1adbg else None
    bncd = c.dram(c.fresh("bnc1a"), [33, TC2], F32)
    with c.scope() as sc:
        rows_sc = c.scope()
        rows, rows_done = ln_rows(c, rows_sc, "l1a")
        abd = c.dram(c.fresh("abd1a"), [2 * T], F32)
        dab = c.sem("dab")
        c.dma("sync", AP(abd.tensor, 0, [[2 * T, 1], [1, 2 * T]]), rows[0:1, :],
              waits=[(sv, rows_done)], inc=dab)
        c.wait("sync", dab, dab.n)
        c.barrier()
        rows_sc.close()
        dgp = c.sem("dgp")
        cwc = load_mat(c, sc, W["L1_attn_conv_wc"], 128, 64, "cwc", inc=dgp)
        cbc = load_mat(c, sc, W["L1_attn_conv_bc"], 128, 16, "cbc", inc=dgp)
        wdt = load_mat(c, sc, W["L1_attn_w_dtc"], 128, 16, "wdt", inc=dgp)
        bdt = load_mat(c, sc, W["L1_attn_b_dtc"], 128, 16, "bdt", inc=dgp)
        alc = load_mat(c, sc, W["L1_attn_A_logc"], 128, 256, "alc", inc=dgp)
        dpc = load_mat(c, sc, W["L1_attn_D_paramc"], 128, 16, "dpc", inc=dgp)
        nwc = load_mat(c, sc, W["L1_attn_norm_wc"], 128, 8, "nwc", inc=dgp)
        par_done = dgp.n
        negA = c.sb_in(sc, c.fresh("negA1"), [128, 256], BF16)
        negAf = c.sb_in(sc, c.fresh("negA1f"), [128, 256], F32)
        c.op("scalar", "activation", negAf[:], alc[:], AF.Exp,
             waits=[(dgp, par_done)], inc=sa)
        c.op("vector", "tensor_scalar_mul", negAf[:], negAf[:], -1.0,
             waits=[(sa, sa.n)], inc=sv)
        prm_done = sv.n
        xc = c.sb_in(sc, c.fresh("xc1a"), [128, NM * T], BF16)
        ssm = c.sb_in(sc, c.fresh("ssm1a"), [64, T], F32)
        with c.scope() as p1:
            ab_bc = c.sb_in(p1, c.fresh("abbc"), [128, 2 * TCH], F32)
            rrb = c.sb_in(p1, c.fresh("rrb"), [128, TCH], F32)
            rrow = c.sb_in(p1, c.fresh("rrow"), [1, TCH], F32)
            xsl = c.sb_in(p1, c.fresh("xsl"), [128, ND * TCH], F32)
            xn = c.sb_in(p1, c.fresh("xn"), [128, ND * TCH], BF16)
            xq = xsl
            zbuf = c.sb_in(p1, c.fresh("zbuf"), [128, TCH], BF16)
            pp = c.ps_in(p1, c.fresh("pp1a"), [128, 4 * TB], F32)
            psq = c.ps_in(p1, c.fresh("psq1a"), [1, TB], F32)
            wbufs = [c.sb_in(p1, c.fresh("wb1a"), [128, 8 * 256], BF16)
                     for _ in range(2)]
            wl_sems = [c.sem("wl_l1a0"), c.sem("wl_l1a1")]
            sez = c.sem("sez1a")
            dwb = c.sem("dwb")
            ones_colf = c.consts["ones_colf"]
            for ch in range(NCH):
                t0 = ch * TCH
                asrc = AP(abd.tensor, t0, [[0, 128], [1, TCH]])
                bsrc = AP(abd.tensor, T + t0, [[0, 128], [1, TCH]])
                c.dma("sync", ab_bc[:, 0:TCH], asrc,
                      waits=[(sv, sv.n), (sez, sez.n)], inc=dwb)
                c.dma("sync", ab_bc[:, TCH:2 * TCH], bsrc, inc=dwb)
                c.load_x(xsl, t0, t0 + TCH, inc=dxs)
                emit_xn_slice2(c, xn[:], xsl[:], ab_bc, TCH,
                               waits=[(dxs, dxs.n), (dwb, dwb.n), (sp, sp.n)])
                xn_done = sv.n
                c.op("scalar", "activation", xq[:], xn[:], AF.Square,
                     waits=[(sv, xn_done)], inc=sa)
                sq_done = sa.n
                for d in range(ND):
                    c.op("tensor", "matmul", psq[0:1, :], ones_colf[:, 0:1],
                         xq[:, d * TCH:(d + 1) * TCH],
                         start=(d == 0), stop=(d == ND - 1),
                         waits=([(sa, sq_done), (sv, sv.n)] if d == 0 else ()),
                         inc=sp)
                c.op("vector", "tensor_scalar_mul", rrow[:], psq[0:1, :],
                     1.0 / D, waits=[(sp, sp.n)], inc=sv)
                c.op("scalar", "activation", rrow[:], rrow[:], AF.Sqrt,
                     bias=1e-6, waits=[(sv, sv.n)], inc=sa)
                c.op("vector", "reciprocal", rrow[:], rrow[:],
                     waits=[(sa, sa.n)], inc=sv)
                c.bounce_bcast(rrow[0:1, :], TCH, rrb[:], 128,
                               waits=[(sv, sv.n)], inc=dwb)
                rr_done = dwb.n
                for d in range(ND):
                    c.op("vector", "scalar_tensor_tensor",
                         xn[:, d * TCH:(d + 1) * TCH],
                         xn[:, d * TCH:(d + 1) * TCH], nwc[:, d:d + 1],
                         rrb[:, 0:TCH], ALU.mult, ALU.mult,
                         waits=[(dwb, rr_done), (sv, sv.n)], inc=sv)
                xn_done = sv.n

                def evac_xz(mi, tb, ps, mmc):
                    m = mi // 128
                    if m < NM:
                        c.op("scalar", "activation",
                             xc[:, m * T + t0 + tb * TB: m * T + t0 + (tb + 1) * TB],
                             ps, AF.Identity, waits=[(sp, mmc)], inc=sez)
                    else:
                        mz = m - NM
                        c.op("scalar", "activation", zbuf[:, tb * TB:(tb + 1) * TB],
                             ps, AF.Sigmoid if False else AF.Silu,
                             waits=[(sp, mmc), (dzy, dzy.n)], inc=sez)
                        c.dma("sync",
                              zd[mz * 128:(mz + 1) * 128, t0 + tb * TB:t0 + (tb + 1) * TB],
                              zbuf[:, tb * TB:(tb + 1) * TB],
                              waits=[(sez, sez.n)], inc=dzy)

                proj(c, p1, lambda k, tb: xn[:, k * TCH + tb * TB:
                                             k * TCH + (tb + 1) * TB],
                     W["L1_attn_W_in"], D, 0, 4096, TCH // TB, evac_xz,
                     "ip1a", evac_sem=sez, mblk=256,
                     first_waits=[(sv, xn_done), (sez, sez.n)],
                     pp=pp, wbufs=wbufs, wl_sems=wl_sems)
            c.wait("sync", dzy, dzy.n)
            c.barrier()
        with c.scope() as p2:
            acc = c.sb_in(p2, c.fresh("acc1a"), [128, T], F32)
            for m in range(NM):
                uv = xc[:, m * T:(m + 1) * T]
                c.op("vector", "tensor_scalar", acc[:, 0:T], uv,
                     cwc[:, m * 4 + 3: m * 4 + 4], None, ALU.mult,
                     waits=[(sa, sa.n), (dgp, par_done)], inc=sv)
                for kk in range(3):
                    sh = 3 - kk
                    c.op("vector", "scalar_tensor_tensor", acc[:, sh:T],
                         xc[:, m * T: m * T + T - sh],
                         cwc[:, m * 4 + kk: m * 4 + kk + 1],
                         acc[:, sh:T], ALU.mult, ALU.add,
                         waits=[(sv, sv.n)], inc=sv)
                c.op("scalar", "activation", uv, acc[:, 0:T], AF.Silu,
                     bias=cbc[:, m:m + 1], waits=[(sv, sv.n)], inc=sa)
            c.barrier()
        with c.scope() as p2b:
            wx = c.sb_in(p2b, c.fresh("wx1a"), [128, 16 * 64], BF16)
            dwx = c.sem("dwx")
            load_w_tiles(c, p2b, W["L1_attn_W_xp"], 2048, 0, 64, "wx",
                         inc=dwx, handle=wx)
            psx = c.ps_in(p2b, c.fresh("psx1a"), [64, TB], F32)
            sex = c.sem("sex1a")
            for tb in range(NT):
                for k in range(16):
                    c.op("tensor", "matmul", psx[:],
                         wx[:, k * 64:(k + 1) * 64],
                         xc[:, k * T + tb * TB: k * T + (tb + 1) * TB],
                         start=(k == 0), stop=(k == 15),
                         waits=([(dwx, dwx.n), (sa, sa.n),
                                 (sex, sex.n)] if k == 0 else ()), inc=sp)
                c.op("vector", "tensor_copy",
                     ssm[:, tb * TB:(tb + 1) * TB], psx[:],
                     waits=[(sp, sp.n)], inc=sex)
            if l1adbg:
                c.dma("sync", ssmd, ssm[:], waits=[(sv, sv.n), (sa, sa.n)], inc=dxs)
                c.dma("sync", xcd.rearrange("(mc p) t -> p mc t", p=128),
                      xc[:].rearrange("p (mc t) -> p mc t", mc=16), inc=dxs)
            c.barrier()
        with c.scope() as p3:
            Brep = c.sb_in(p3, c.fresh("Brep"), [128, NS * TC2], BF16)
            Crep = c.sb_in(p3, c.fresh("Crep"), [128, NS * TC2], BF16)
            dtr = c.sb_in(p3, c.fresh("dtr"), [128, TC2], F32)
            dtm = c.sb_in(p3, c.fresh("dtm"), [128, TC2], F32)
            dtxc = c.sb_in(p3, c.fresh("dtxc"), [128, TC2], BF16)
            abar = c.sb_in(p3, c.fresh("abar"), [128, NS * TC2], BF16)
            bx = c.sb_in(p3, c.fresh("bx"), [128, NS * TC2], BF16)
            hbuf = c.sb_in(p3, c.fresh("hbuf"), [128, NS * TC2], BF16)
            hC = c.sb_in(p3, c.fresh("hC"), [128, NS * TC2], BF16)
            yv = c.sb_in(p3, c.fresh("yv1"), [128, TC2], BF16)
            hc = c.sb_in(p3, c.fresh("hc1"), [128, NM * NS], F32)
            dbc = c.sem("dbc")
            c.memset("vector", hc[:], 0.0)
            sv0 = c.sem("sv")
            prev_dtxc = [0]
            prev_scan = {}
            for tc in range(NC2):
                t0 = tc * TC2
                c.dma("sync", bncd[:, :], ssm[0:33, t0:t0 + TC2],
                      waits=[(sv, sv.n), (sa, sa.n), (dbc, dbc.n)], inc=dbc)
                c.wait("sync", dbc, dbc.n)
                bsrc = AP(bncd.tensor, 1 * TC2, [[0, 128], [TC2, NS], [1, TC2]])
                csrc = AP(bncd.tensor, 17 * TC2, [[0, 128], [TC2, NS], [1, TC2]])
                dsrc = AP(bncd.tensor, 0, [[0, 128], [1, TC2]])
                c.dma("gpsimd", Brep[:].rearrange("p (s t) -> p s t", s=NS),
                      bsrc, waits=[(dbc, dbc.n)], inc=dbc)
                c.dma("gpsimd", Crep[:].rearrange("p (s t) -> p s t", s=NS),
                      csrc, inc=dbc)
                c.dma("sync", dtr[:], dsrc, inc=dbc)
                rep_done = dbc.n
                for m in range(NM):
                    xcs = xc[:, m * T + t0: m * T + t0 + TC2]
                    c.op("scalar", "activation", dtm[:], dtr[:], AF.Exp,
                         scale=wdt[:, m:m + 1], bias=bdt[:, m:m + 1],
                         waits=[(dbc, rep_done), (sv, prev_dtxc[0])], inc=sa)
                    c.op("scalar", "activation", dtm[:], dtm[:], AF.Ln,
                         bias=1.0, inc=sa)
                    c.op("vector", "tensor_tensor", dtxc[:], dtm[:], xcs,
                         ALU.mult, waits=[(sa, sa.n)], inc=sv)
                    prev_dtxc[0] = sv.n
                    abar_done = {}
                    for s in range(NS):
                        aw = ([(sv, prev_scan[s])] if s in prev_scan else [])
                        c.op("scalar", "activation",
                             abar[:, s * TC2:(s + 1) * TC2], dtm[:], AF.Exp,
                             scale=negAf[:, m * NS + s:m * NS + s + 1],
                             waits=aw, inc=sa)
                        abar_done[s] = sa.n
                    for s in range(NS):
                        c.op("vector", "tensor_tensor",
                             bx[:, s * TC2:(s + 1) * TC2], dtxc[:],
                             Brep[:, s * TC2:(s + 1) * TC2], ALU.mult,
                             inc=sv)
                    for s in range(NS):
                        c.op("vector", "tensor_tensor_scan",
                             hbuf[:, s * TC2:(s + 1) * TC2],
                             abar[:, s * TC2:(s + 1) * TC2],
                             bx[:, s * TC2:(s + 1) * TC2],
                             hc[:, m * NS + s: m * NS + s + 1],
                             ALU.mult, ALU.add,
                             waits=[(sa, abar_done[s])],
                             inc=sv)
                        prev_scan[s] = sv.n
                    c.op("vector", "tensor_copy", hc[:, m * NS:(m + 1) * NS],
                         hbuf[:].rearrange("p (s t) -> p s t", s=NS)[:, :, TC2 - 1],
                         inc=sv)
                    c.op("vector", "tensor_tensor", hC[:], hbuf[:], Crep[:],
                         ALU.mult, inc=sv)
                    c.op("vector", "tensor_tensor", hbuf[:, 0:8 * TC2],
                         hC[:, 0:8 * TC2], hC[:, 8 * TC2:16 * TC2],
                         ALU.add, inc=sv)
                    c.op("vector", "tensor_tensor",
                         hbuf[:, 8 * TC2:12 * TC2], hbuf[:, 0:4 * TC2],
                         hbuf[:, 4 * TC2:8 * TC2], ALU.add, inc=sv)
                    c.op("vector", "tensor_tensor",
                         hbuf[:, 12 * TC2:14 * TC2], hbuf[:, 8 * TC2:10 * TC2],
                         hbuf[:, 10 * TC2:12 * TC2], ALU.add, inc=sv)
                    c.op("vector", "tensor_tensor", yv[:],
                         hbuf[:, 12 * TC2:13 * TC2],
                         hbuf[:, 13 * TC2:14 * TC2], ALU.add, inc=sv)
                    c.op("vector", "scalar_tensor_tensor", yv[:], xcs,
                         dpc[:, m:m + 1], yv[:], ALU.mult, ALU.add,
                         waits=[(sv, sv.n)], inc=sv)
                    c.dma("sync", yd[m * 128:(m + 1) * 128, t0:t0 + TC2],
                          yv[:], waits=[(sv, sv.n), (dzy, dzy.n)], inc=dzy)
            c.wait("sync", dzy, dzy.n)
            c.barrier()
    zgate_hist = {}
    with c.scope() as p4:
        sv, sa, sp = c.sem("sv"), c.sem("sa"), c.sem("sp")
        ybuf = c.sb_in(p4, c.fresh("ybuf1a"), [128, 16 * T], BF16)
        xsl4 = c.sb_in(p4, c.fresh("xsl41"), [128, ND * TB], F32)
        ab4 = c.sb_in(p4, c.fresh("ab41"), [128, 2 * TB], F32)
        tmp4 = c.sb_in(p4, c.fresh("tmp41"), [128, ND * TB], F32)
        ztl = c.sb_in(p4, c.fresh("ztl1a"), [128, 2 * T], BF16)
        c.dma("sync", ybuf[:].rearrange("p (mc t) -> p mc t", mc=16),
              yd.rearrange("(mc p) t -> p mc t", p=128), inc=dxs)
        yld = dxs.n
        dzl = c.sem("dzy")
        for mt in range(16):
            zpar = mt % 2
            zv = ztl[:, zpar * T:(zpar + 1) * T]
            wl_ = [(sv, zgate_hist[mt - 2])] if mt >= 2 else []
            c.dma("sync", zv, zd[mt * 128:(mt + 1) * 128, :],
                  waits=wl_, inc=dzl)
            c.op("vector", "tensor_tensor", ybuf[:, mt * T:(mt + 1) * T],
                 ybuf[:, mt * T:(mt + 1) * T], zv, ALU.mult,
                 waits=[(dzl, dzl.n), (dxs, yld)], inc=sv)
            zgate_hist[mt] = sv.n
        gate_done = sv.n
        seo = c.sem("seo1a")
        pp4 = c.ps_in(p4, c.fresh("pp1a4"), [128, 4 * TB], F32)
        wb4 = [c.sb_in(p4, c.fresh("wb1a4"), [128, 16 * 128], BF16)
               for _ in range(2)]
        wl4 = [c.sem("wl1a40"), c.sem("wl1a41")]
        for tb in range(NT):
            t0 = tb * TB
            c.load_x(xsl4, t0, t0 + TB, waits=[(dst_, dst_.n)], inc=dxs)
            c.dma("sync", ab4[:, 0:TB], AP(abd.tensor, t0, [[0, 128], [1, TB]]),
                  waits=[(sv, sv.n)], inc=dxs)
            c.dma("sync", ab4[:, TB:2 * TB],
                  AP(abd.tensor, T + t0, [[0, 128], [1, TB]]), inc=dxs)
            xld = dxs.n
            a3 = ab4[:, 0:TB].unsqueeze(1).broadcast_to([128, ND, TB])
            b3 = ab4[:, TB:2 * TB].unsqueeze(1).broadcast_to([128, ND, TB])
            x3 = xsl4[:].rearrange("p (dc t) -> p dc t", dc=ND)
            t3 = tmp4[:].rearrange("p (dc t) -> p dc t", dc=ND)
            c.op("vector", "tensor_tensor", t3, x3, a3, ALU.mult,
                 waits=[(dxs, xld)], inc=sv)
            c.op("vector", "tensor_tensor", t3, t3, b3, ALU.subtract, inc=sv)
            c.op("vector", "tensor_tensor", xsl4[:], xsl4[:], tmp4[:],
                 ALU.add, inc=sv)
            xn_add = sv.n

            def evac_o(mi, tbb, ps, mmc):
                d = mi // 128
                c.op("vector", "scalar_tensor_tensor",
                     xsl4[:, d * TB:(d + 1) * TB], ps, 1.0,
                     xsl4[:, d * TB:(d + 1) * TB], ALU.mult, ALU.add,
                     waits=[(sp, mmc), (sv, xn_add)], inc=seo)

            proj(c, p4, lambda k, tbb: ybuf[:, k * T + t0:k * T + t0 + TB],
                 W["L1_attn_W_out"], 2048, 0, D, 1, evac_o, "o1a",
                 evac_sem=seo, mblk=128,
                 first_waits=[(dxs, yld), (sv, gate_done), (seo, seo.n)],
                 pp=pp4, wbufs=wb4, wl_sems=wl4)
            c.store_x(xsl4, t0, t0 + TB, waits=[(seo, seo.n)], inc=dst_)
        c.wait("sync", dst_, dst_.n)
        c.barrier()



def st_l0f(c, W, inst=0):
    TCH = 1024
    NCH = T // TCH
    NB = TCH // TB
    NTT = T // 128
    KSEL = 8192
    sv, sa, sp = c.sem("sv"), c.sem("sa"), c.sem("sp")
    dxs, dst_ = c.sem("dxs2"), c.sem("dst")
    dcw = c.sem("dcw")
    cc = c.sem("cc_" + str(inst))
    ag_in = c.dram(c.fresh("agin"), [T, 8], F32)
    ag_out = c.dram(c.fresh("agout"), [NCORES * T, 8], F32)
    cwd = c.dram(c.fresh("cwd0f"), [8, T], BF16)
    with c.scope() as sc:
        rows_sc = c.scope()
        rows, rows_done = ln_rows(c, rows_sc, "l0f")
        abd = c.dram(c.fresh("abd0f"), [2 * T], F32)
        dab = c.sem("dab")
        c.dma("sync", AP(abd.tensor, 0, [[2 * T, 1], [1, 2 * T]]), rows[0:1, :],
              waits=[(sv, rows_done)], inc=dab)
        c.wait("sync", dab, dab.n)
        c.barrier()
        rows_sc.close()
        sel_sc = c.scope()
        pr_all = c.sb_in(sel_sc, c.fresh("prall"), [128, NTT * 8], F32)
        rw = c.sb_in(sel_sc, c.fresh("rw0f"), [128, 8 * ND], BF16)
        drw = c.sem("drw")
        load_w_tiles(c, sel_sc, W["L0_ffn_Wr"], D, 0, 8, "rw0", inc=drw, handle=rw)
        rw_done = drw.n
        with c.scope() as pA:
            ab_bc = c.sb_in(pA, c.fresh("abbc"), [128, 2 * TCH], F32)
            xsl = c.sb_in(pA, c.fresh("xsl"), [128, ND * TCH], F32)
            xn = c.sb_in(pA, c.fresh("xn"), [128, ND * TCH], BF16)
            nmx = c.sb_in(pA, c.fresh("nmx"), [128, 2], F32)
            psr = c.ps_in(pA, c.fresh("psr0"), [128, 8], F32)
            dwb = c.sem("dwb")
            for ch in range(NCH):
                t0 = ch * TCH
                c.dma("sync", ab_bc[:, 0:TCH],
                      AP(abd.tensor, t0, [[0, 128], [1, TCH]]),
                      waits=[(sv, sv.n)], inc=dwb)
                c.dma("sync", ab_bc[:, TCH:2 * TCH],
                      AP(abd.tensor, T + t0, [[0, 128], [1, TCH]]), inc=dwb)
                c.load_x(xsl, t0, t0 + TCH, inc=dxs)
                emit_xn_slice2(c, xn[:], xsl[:], ab_bc, TCH,
                               waits=[(dxs, dxs.n), (dwb, dwb.n), (sp, sp.n)])
                xn_done = sv.n
                for tt in range(TCH // 128):
                    ta = t0 // 128 + tt
                    for k in range(ND):
                        c.op("tensor", "matmul", psr[:],
                             xn[:, k * TCH + tt * 128: k * TCH + tt * 128 + 128],
                             rw[:, k * 8:(k + 1) * 8],
                             start=(k == 0), stop=(k == ND - 1),
                             waits=([(sv, xn_done), (drw, rw_done),
                                     (sv, sv.n)] if k == 0 else ()), inc=sp)
                    pv = pr_all[:, ta * 8:(ta + 1) * 8]
                    c.op("vector", "reduce_max", nmx[:, 0:1], psr[:],
                         axis=mybir.AxisListType.X, waits=[(sp, sp.n)], inc=sv)
                    c.op("vector", "tensor_scalar_mul", nmx[:, 0:1],
                         nmx[:, 0:1], -1.0, waits=[(sv, sv.n)], inc=sv)
                    c.op("scalar", "activation", pv, psr[:], AF.Exp,
                         bias=nmx[:, 0:1], accum_out=nmx[:, 1:2],
                         waits=[(sv, sv.n)], inc=sa)
                    c.op("vector", "reciprocal", nmx[:, 1:2], nmx[:, 1:2],
                         waits=[(sa, sa.n)], inc=sv)
                    c.op("vector", "tensor_scalar", pv, pv, nmx[:, 1:2], None,
                         ALU.mult, waits=[(sv, sv.n)], inc=sv)
            c.dma("sync", ag_in.rearrange("(tt p) e -> p tt e", p=128),
                  pr_all[:].rearrange("p (tt e) -> p tt e", tt=NTT),
                  waits=[(sv, sv.n)], inc=dab)
            c.op("gpsimd", "collective_compute",
                 "AllGather", ALU.bypass, [[0, 1, 2, 3, 4, 5, 6, 7]],
                 [ag_in.opt()], [ag_out.opt()],
                 waits=[(dab, dab.n)], inc=cc, inc_k=1)
            c.barrier()
        tau = c.sb_in(sel_sc, c.fresh("tau"), [128, 8], F32)
        with c.scope() as pB:
            sca = c.sb_in(pB, c.fresh("sca"), [128, 2048], F32)
            cmp_ = c.sb_in(pB, c.fresh("cmp"), [128, 2048], BF16)
            cnt = c.sb_in(pB, c.fresh("cnt"), [128, 8], F32)
            cntb = c.sb_in(pB, c.fresh("cntb"), [128, 8], BF16)
            lh = c.sb_in(pB, c.fresh("lh"), [1, 48], F32)
            psc = c.ps_in(pB, c.fresh("psc0"), [1, 8], F32)
            dtau = c.sem("dtau")
            c.wait("sync", cc, cc.n)
            c.dma("sync", sca[:],
                  ag_out.rearrange("(p k) e -> p (k e)", p=128), inc=dxs)
            sc_ld = dxs.n
            c.memset("vector", lh[:], 0.0)
            c.op("vector", "tensor_scalar", lh[0:1, 8:16], lh[0:1, 8:16],
                 0.0, 1.0, ALU.mult, ALU.add, inc=sv)
            for it in range(22):
                c.op("vector", "tensor_tensor", lh[0:1, 16:24], lh[0:1, 0:8],
                     lh[0:1, 8:16], ALU.add, waits=[(sv, sv.n)], inc=sv)
                c.op("vector", "tensor_scalar_mul", lh[0:1, 16:24],
                     lh[0:1, 16:24], 0.5, waits=[(sv, sv.n)], inc=sv)
                c.bounce_bcast(lh[0:1, 16:24], 8, tau[:], 128,
                               waits=[(sv, sv.n)], inc=dtau)
                t3 = tau[:].unsqueeze(1).broadcast_to([128, 256, 8])
                c.op("vector", "tensor_tensor",
                     cmp_[:].rearrange("p (k e) -> p k e", k=256),
                     sca[:].rearrange("p (k e) -> p k e", k=256), t3,
                     ALU.is_gt, waits=[(dtau, dtau.n), (dxs, sc_ld)], inc=sv)
                for e in range(8):
                    cv = cmp_[:].rearrange("p (k e) -> p k e", k=256)[:, :, e]
                    c.op("vector", "reduce_sum", cnt[:, e:e + 1], cv,
                         axis=mybir.AxisListType.X,
                         waits=[(sv, sv.n)] if e == 0 else (), inc=sv)
                c.op("vector", "tensor_copy", cntb[:], cnt[:],
                     waits=[(sv, sv.n)], inc=sv)
                c.op("tensor", "matmul", psc[:], c.consts["ones_col"][:, 0:1],
                     cntb[:], start=True, stop=True, waits=[(sv, sv.n)], inc=sp)
                c.op("vector", "tensor_single_scalar", lh[0:1, 24:32], psc[:],
                     float(KSEL), ALU.is_gt, waits=[(sp, sp.n)], inc=sv)
                c.op("vector", "tensor_sub", lh[0:1, 32:40], lh[0:1, 16:24],
                     lh[0:1, 0:8], waits=[(sv, sv.n)], inc=sv)
                c.op("vector", "tensor_mul", lh[0:1, 32:40], lh[0:1, 32:40],
                     lh[0:1, 24:32], waits=[(sv, sv.n)], inc=sv)
                c.op("vector", "tensor_add", lh[0:1, 0:8], lh[0:1, 0:8],
                     lh[0:1, 32:40], waits=[(sv, sv.n)], inc=sv)
                c.op("vector", "tensor_sub", lh[0:1, 40:48], lh[0:1, 8:16],
                     lh[0:1, 16:24], waits=[(sv, sv.n)], inc=sv)
                c.op("vector", "tensor_mul", lh[0:1, 40:48], lh[0:1, 40:48],
                     lh[0:1, 24:32], waits=[(sv, sv.n)], inc=sv)
                c.op("vector", "tensor_add", lh[0:1, 8:16], lh[0:1, 16:24],
                     lh[0:1, 40:48], waits=[(sv, sv.n)], inc=sv)
            c.bounce_bcast(lh[0:1, 8:16], 8, tau[:], 128,
                           waits=[(sv, sv.n)], inc=dtau)
            c.wait("sync", dtau, dtau.n)
            c.barrier()
        with c.scope() as pC:
            msk = c.sb_in(pC, c.fresh("msk"), [128, NTT * 8], F32)
            cwb = c.sb_in(pC, c.fresh("cwb0"), [128, 8], BF16)
            cwT = c.sb_in(pC, c.fresh("cwT0"), [8, T], BF16)
            pst = c.ps_in(pC, c.fresh("pst0"), [8, 128], F32)
            set_ = c.sem("set0f")
            t3 = tau[:].unsqueeze(1).broadcast_to([128, NTT, 8])
            c.op("vector", "tensor_tensor",
                 msk[:].rearrange("p (k e) -> p k e", k=NTT),
                 pr_all[:].rearrange("p (k e) -> p k e", k=NTT), t3,
                 ALU.is_gt, waits=[(sv, sv.n)], inc=sv)
            c.op("vector", "tensor_tensor", msk[:], msk[:], pr_all[:],
                 ALU.mult, waits=[(sv, sv.n)], inc=sv)
            for tt in range(NTT):
                c.op("vector", "tensor_copy", cwb[:],
                     msk[:, tt * 8:(tt + 1) * 8],
                     waits=[(sv, sv.n), (sp, sp.n)], inc=sv)
                c.op("tensor", "matmul", pst[:], cwb[:],
                     c.consts["identb"][:], start=True, stop=True,
                     waits=[(sv, sv.n), (set_, set_.n)], inc=sp)
                c.op("vector", "tensor_copy", cwT[:, tt * 128:(tt + 1) * 128],
                     pst[:], waits=[(sp, sp.n)], inc=set_)
            c.dma("sync", cwd[:, :], cwT[:], waits=[(set_, set_.n)], inc=dcw)
            c.wait("sync", dcw, dcw.n)
            c.barrier()
        sel_sc.close()
        with c.scope() as pD:
            ab_bc = c.sb_in(pD, c.fresh("abbc2"), [128, 2 * TCH], F32)
            xsl = c.sb_in(pD, c.fresh("xsl2"), [128, ND * TCH], F32)
            xn = c.sb_in(pD, c.fresh("xn2"), [128, ND * TCH], BF16)
            h = c.sb_in(pD, c.fresh("h0f"), [128, 32 * TCH], BF16)
            acc = c.sb_in(pD, c.fresh("acc0f"), [128, ND * TCH], F32)
            web = c.sb_in(pD, c.fresh("web0f"), [128, TCH], BF16)
            rtmp = c.sb_in(pD, c.fresh("rtmp0f"), [128, TB], F32)
            b1c = load_mat(c, pD, W["L0_ffn_b1c"], 128, 8 * 32, "b1c0", inc=dxs)
            b2r = load_row(c, pD, W["L0_ffn_b2r"], 8 * D, "b2r0", dtype=BF16,
                           inc=dxs)
            bias_done = dxs.n
            pp = c.ps_in(pD, c.fresh("pp0f"), [128, 4 * TB], F32)
            wbufs = [c.sb_in(pD, c.fresh("wb0f"), [128, 32 * 128], BF16)
                     for _ in range(2)]
            wl_sems = [c.sem("wl_l0f0"), c.sem("wl_l0f1")]
            seh = c.sem("seh0f")
            seo = c.sem("seo0f")
            dwb = c.sem("dwb")
            ones_row = c.consts["ones_row"]
            for ch in range(NCH):
                t0 = ch * TCH
                wl = [(dst_, dst_.n)] if ch else []
                c.dma("sync", ab_bc[:, 0:TCH],
                      AP(abd.tensor, t0, [[0, 128], [1, TCH]]),
                      waits=[(sv, sv.n)] + wl, inc=dwb)
                c.dma("sync", ab_bc[:, TCH:2 * TCH],
                      AP(abd.tensor, T + t0, [[0, 128], [1, TCH]]), inc=dwb)
                c.load_x(xsl, t0, t0 + TCH, inc=dxs)
                emit_xn_slice2(c, xn[:], xsl[:], ab_bc, TCH,
                               waits=[(dxs, dxs.n), (dwb, dwb.n), (sp, sp.n)])
                xn_done = sv.n
                for e in range(8):
                    c.dma("gpsimd", web[:],
                          AP(cwd.tensor, e * T + t0, [[0, 128], [1, TCH]]),
                          waits=[(dcw, dcw.n), (seo, seo.n)], inc=dcw)
                    web_done = dcw.n

                    def evac_h(mi, tb, ps, mmc, e=e):
                        m = mi // 128
                        c.op("scalar", "activation",
                             h[:, m * TCH + tb * TB: m * TCH + (tb + 1) * TB],
                             ps, AF.Gelu, bias=b1c[:, e * 32 + m: e * 32 + m + 1],
                             waits=[(sp, mmc), (dxs, bias_done)], inc=seh)

                    proj(c, pD, lambda k, tb: xn[:, k * TCH + tb * TB:
                                                 k * TCH + (tb + 1) * TB],
                         W["L0_ffn_W1f"], D, 0, 4096, NB, evac_h,
                         "w10", evac_sem=seh, w_row_lo=e * D, mblk=512,
                         first_waits=[(sv, xn_done), (seo, seo.n)],
                         pp=pp, wbufs=wbufs, wl_sems=wl_sems)

                    def evac_o(mi, tb, ps, mmc, e=e, web_done=web_done):
                        d = mi // 128
                        sl = slice(d * TCH + tb * TB, d * TCH + (tb + 1) * TB)
                        wsl = web[:, tb * TB:(tb + 1) * TB]
                        if e == 0:
                            c.op("vector", "tensor_tensor", acc[:, sl], ps,
                                 wsl, ALU.mult,
                                 waits=[(sp, mmc), (dcw, web_done)], inc=seo)
                        else:
                            c.op("vector", "tensor_tensor", rtmp[:], ps, wsl,
                                 ALU.mult, waits=[(sp, mmc), (dcw, web_done)],
                                 inc=sv)
                            c.op("vector", "tensor_tensor", acc[:, sl],
                                 acc[:, sl], rtmp[:], ALU.add, inc=seo)

                    proj(c, pD, lambda k, tb: h[:, k * TCH + tb * TB:
                                                k * TCH + (tb + 1) * TB],
                         W["L0_ffn_W2f"], 4096, 0, D, NB, evac_o, "w20",
                         evac_sem=seo, w_row_lo=e * 4096, mblk=128,
                         first_waits=[(seh, seh.n)],
                         pp=pp, wbufs=wbufs, wl_sems=wl_sems,
                         extra_k=(lambda mi, e=e: b2r[0:1, e * D + mi: e * D + mi + 128],
                                  lambda tb: ones_row[0:1, 0:TB]))
                for d in range(ND):
                    sl = slice(d * TCH, (d + 1) * TCH)
                    c.op("vector", "scalar_tensor_tensor", xsl[:, sl],
                         acc[:, sl], 0.5, xsl[:, sl], ALU.mult, ALU.add,
                         waits=[(seo, seo.n)], inc=sv)
                c.store_x(xsl, t0, t0 + TCH, waits=[(sv, sv.n)], inc=dst_)
            c.wait("sync", dst_, dst_.n)
            c.barrier()



def st_l3f(c, W, inst=0):
    TCH = 512
    NCH = T // TCH
    H = 16
    sv, sa, sp = c.sem("sv"), c.sem("sa"), c.sem("sp")
    dxs, dst_ = c.sem("dxs2"), c.sem("dst")
    dcw = c.sem("dcw")
    xnd = c.dram(c.fresh("xnd3f"), [D, T], BF16)
    kvd = c.dram(c.fresh("kvd3f"), [T, 512], BF16)
    yda = c.dram(c.fresh("yda3f"), [512, T], BF16)
    rd = c.dram(c.fresh("rd3f"), [D, T], F32)
    cwd = c.dram(c.fresh("cwd3f"), [8, T], BF16)
    with c.scope() as sc:
        rows_sc = c.scope()
        rows, rows_done = ln_rows(c, rows_sc, "l3f")
        abd = c.dram(c.fresh("abd3f"), [2 * T], F32)
        dab = c.sem("dab")
        c.dma("sync", AP(abd.tensor, 0, [[2 * T, 1], [1, 2 * T]]), rows[0:1, :],
              waits=[(sv, rows_done)], inc=dab)
        c.wait("sync", dab, dab.n)
        c.barrier()
        rows_sc.close()
        wln1 = load_mat(c, sc, W["L3_ffn_w_ln1c"], 128, ND, "wln1", inc=dxs)
        wln2 = load_mat(c, sc, W["L3_ffn_w_ln2c"], 128, ND, "wln2", inc=dxs)
        wln_done = dxs.n
        x1 = c.sb_in(sc, c.fresh("x13f"), [128, ND * T], BF16)
        with c.scope() as p1:
            ab_bc = c.sb_in(p1, c.fresh("abbc"), [128, 2 * TCH], F32)
            rrb = c.sb_in(p1, c.fresh("rrb"), [128, TCH], F32)
            rrow = c.sb_in(p1, c.fresh("rrow"), [1, TCH], F32)
            xsl = c.sb_in(p1, c.fresh("xsl"), [128, ND * TCH], F32)
            xn = c.sb_in(p1, c.fresh("xn"), [128, ND * TCH], BF16)
            psq = c.ps_in(p1, c.fresh("psq3f"), [1, TB], F32)
            dwb = c.sem("dwb")
            dxn = c.sem("dxn")
            ones_colf = c.consts["ones_colf"]
            for ch in range(NCH):
                t0 = ch * TCH
                c.dma("sync", ab_bc[:, 0:TCH],
                      AP(abd.tensor, t0, [[0, 128], [1, TCH]]),
                      waits=[(sv, sv.n), (dxn, dxn.n)], inc=dwb)
                c.dma("sync", ab_bc[:, TCH:2 * TCH],
                      AP(abd.tensor, T + t0, [[0, 128], [1, TCH]]), inc=dwb)
                c.load_x(xsl, t0, t0 + TCH, inc=dxs)
                emit_xn_slice2(c, xn[:], xsl[:], ab_bc, TCH,
                               waits=[(dxs, dxs.n), (dwb, dwb.n), (sp, sp.n)])
                xn_done = sv.n
                c.dma("sync", xnd.rearrange("(dc p) t -> p dc t", p=128)[:, :, t0:t0 + TCH],
                      xn[:].rearrange("p (dc t) -> p dc t", dc=ND),
                      waits=[(sv, xn_done)], inc=dxn)
                c.op("scalar", "activation", xsl[:], xn[:], AF.Square,
                     waits=[(sv, xn_done)], inc=sa)
                sq_done = sa.n
                for d in range(ND):
                    c.op("tensor", "matmul", psq[0:1, :], ones_colf[:, 0:1],
                         xsl[:, d * TCH:(d + 1) * TCH],
                         start=(d == 0), stop=(d == ND - 1),
                         waits=([(sa, sq_done), (sv, sv.n)] if d == 0 else ()),
                         inc=sp)
                c.op("vector", "tensor_scalar_mul", rrow[:], psq[0:1, :],
                     1.0 / D, waits=[(sp, sp.n)], inc=sv)
                c.op("scalar", "activation", rrow[:], rrow[:], AF.Sqrt,
                     bias=1e-5, waits=[(sv, sv.n)], inc=sa)
                c.op("vector", "reciprocal", rrow[:], rrow[:],
                     waits=[(sa, sa.n)], inc=sv)
                c.bounce_bcast(rrow[0:1, :], TCH, rrb[:], 128,
                               waits=[(sv, sv.n)], inc=dwb)
                rr_done = dwb.n
                for d in range(ND):
                    c.op("vector", "scalar_tensor_tensor",
                         x1[:, d * T + t0: d * T + t0 + TCH],
                         xn[:, d * TCH:(d + 1) * TCH], wln1[:, d:d + 1],
                         rrb[:, 0:TCH], ALU.mult, ALU.mult,
                         waits=[(dwb, rr_done), (sv, sv.n), (dxs, wln_done)],
                         inc=sv)
            c.wait("sync", dxn, dxn.n)
            c.barrier()
        kv_lat = c.sb_in(sc, c.fresh("kvl3f"), [128, T], BF16)
        krope = c.sb_in(sc, c.fresh("krp3f"), [32, T], BF16)
        with c.scope() as p2:
            pp = c.ps_in(p2, c.fresh("pp3f"), [128, 4 * TB], F32)
            wbufs = [c.sb_in(p2, c.fresh("wb3f"), [128, 8 * 512], BF16)
                     for _ in range(2)]
            wl_sems = [c.sem("wl_3f0"), c.sem("wl_3f1")]
            sek = c.sem("sek3f")
            dkv = c.sem("dkv3f")
            kvt = c.sb_in(p2, c.fresh("kvt3f"), [128, 4 * 512], BF16)

            def evac_ckv(mi, tb, ps, mmc):
                if mi == 0:
                    c.op("vector", "tensor_copy",
                         kv_lat[:, tb * TB:(tb + 1) * TB], ps,
                         waits=[(sp, mmc)], inc=sek)
                else:
                    c.op("vector", "tensor_copy",
                         krope[:, tb * TB:(tb + 1) * TB], ps[0:32, :],
                         waits=[(sp, mmc)], inc=sek)

            proj(c, p2, lambda k, tb: x1[:, k * T + tb * TB:k * T + (tb + 1) * TB],
                 W["L3_ffn_Wkva"], D, 0, 160, NT, evac_ckv, "kva",
                 evac_sem=sek, mblk=160,
                 first_waits=[(sv, sv.n)],
                 pp=pp, wbufs=wbufs, wl_sems=wl_sems)
            wkvb = c.sb_in(p2, c.fresh("wkvb"), [128, 512], BF16)
            c.dma("gpsimd", wkvb[:], W["L3_ffn_Wkvb"], inc=dkv)
            wkvb_done = dkv.n
            sv2 = c.sem("sev3f")
            for tb in range(NT):
                for tt in range(4):
                    par = tt % 2
                    ps = pp[:, par * TB:(par + 1) * TB]
                    c.op("tensor", "matmul", ps,
                         kv_lat[:, tb * TB + tt * 128: tb * TB + tt * 128 + 128],
                         wkvb[:], start=True, stop=True,
                         waits=[(sek, sek.n), (dkv, wkvb_done),
                                (sv2, max(0, sv2.n - 1))], inc=sp)
                    c.op("vector", "tensor_copy", kvt[:, tt * 512:(tt + 1) * 512],
                         ps, waits=[(sp, sp.n), (dkv, dkv.n)], inc=sv2)
                c.dma("sync", kvd[tb * TB:(tb + 1) * TB, :]
                      .rearrange("(tt p) dd -> p tt dd", p=128),
                      kvt[:].rearrange("p (tt dd) -> p tt dd", tt=4),
                      waits=[(sv2, sv2.n)], inc=dkv)
            c.wait("sync", dkv, dkv.n)
            c.barrier()
        with c.scope() as p3:
            cosx = c.sb_in(p3, c.fresh("cosx"), [128, T], BF16)
            sinx = c.sb_in(p3, c.fresh("sinx"), [128, T], BF16)
            qhp = c.sb_in(p3, c.fresh("qhp"), [128, T], BF16)
            khp = c.sb_in(p3, c.fresh("khp"), [128, T], BF16)
            qtmp = c.sb_in(p3, c.fresh("qtmp3"), [128, TB], BF16)
            qsh = c.sb_in(p3, c.fresh("qsh3"), [128, TB], BF16)
            prob = c.sb_in(p3, c.fresh("prob3"), [128, 3 * TB], BF16)
            vbuf = c.sb_in(p3, c.fresh("vbuf3"), [128, 4 * 4 * 64], BF16)
            rden = c.sb_in(p3, c.fresh("rden3"), [33, 2 * TB], F32)
            rdbc = c.sb_in(p3, c.fresh("rdbc3"), [32, 2 * TB], F32)
            ytmp = c.sb_in(p3, c.fresh("ytmp3"), [32, 2 * TB], BF16)
            trim = c.sb_in(p3, c.fresh("trim3"), [128, 4 * TB], BF16)
            pp = c.ps_in(p3, c.fresh("pp3f3"), [128, 2 * TB], F32)
            ps_sc = c.ps_in(p3, c.fresh("pssc3"), [128, 3 * TB], F32)
            ps_y = c.ps_in(p3, c.fresh("psy3"), [64, 2 * TB], F32)
            wbufs = [c.sb_in(p3, c.fresh("wb3f3"), [128, 8 * 128], BF16)
                     for _ in range(2)]
            wl_sems = [c.sem("wl_3f30"), c.sem("wl_3f31")]
            dq3 = c.sem("dq3f")
            dvv3 = [c.sem(f"dvv3f{i}") for i in range(4)]
            drd3 = c.sem("drd3f")
            it3 = 0
            it3h = {}
            vfree3 = {}
            exp3, pv3, msk3, qk3v = {}, {}, {}, {}
            vld_map, gvb_of = {}, {}
            vb_ctr = [0]
            g3b = [0]
            sdy = c.sem("sdy3f")
            seq = c.sem("seq3f")
            c.dma("gpsimd", trim[:], W["c_tri512"], inc=dq3)
            c.dma("gpsimd", cosx[:].rearrange("p t -> p t"), W["c_cos_l3f"],
                  inc=dq3)
            c.dma("gpsimd", sinx[:], W["c_ssin_l3f"], inc=dq3)
            tab_done = dq3.n
            c.memset("vector", vbuf[:], 0.0)
            vones3 = vbuf[:].rearrange("p (s dd) -> p s dd", s=16)[:, :, 32:33]
            c.op("vector", "memset", vones3, 1.0, inc=sv)
            vones_done = sv.n
            swap_mask = []
            for i in range(0, 32, 2):
                swap_mask += [i + 1, i]
            ones_col = c.consts["ones_col"]
            for hp in range(8):
                def evac_q(mi, tb, ps, mmc, dst=qhp):
                    c.op("vector", "tensor_copy", qtmp[:], ps,
                         waits=[(sp, mmc), (dq3, tab_done), (sp, sp.n)], inc=sv)
                    c.op("vector", "stream_shuffle", qsh[:], qtmp[:],
                         swap_mask, waits=[(sv, sv.n)], inc=sv)
                    sl = slice(tb * TB, (tb + 1) * TB)
                    c.op("vector", "tensor_tensor", dst[:, sl], qtmp[:],
                         cosx[:, sl], ALU.mult, waits=[(sv, sv.n)], inc=sv)
                    c.op("vector", "tensor_tensor", qsh[:], qsh[:],
                         sinx[:, sl], ALU.mult, waits=[(sv, sv.n)], inc=sv)
                    c.op("vector", "tensor_tensor", dst[:, sl], dst[:, sl],
                         qsh[:], ALU.add, waits=[(sv, sv.n)], inc=seq)

                proj(c, p3, lambda k, tb: x1[:, k * T + tb * TB:k * T + (tb + 1) * TB],
                     W["L3_ffn_Wq"], D, hp * 128, (hp + 1) * 128, NT, evac_q,
                     "q3f", evac_sem=seq, mblk=128,
                     first_waits=[(seq, seq.n), (sv, sv.n)],
                     pp=pp, wbufs=wbufs, wl_sems=wl_sems)
                if hp == 0:
                    wkp = c.sb_in(p3, c.fresh("wkp"), [128, 128], BF16)
                c.dma("gpsimd", wkp[:],
                      W["L3_ffn_Wkvb_pad"][:, hp * 128:(hp + 1) * 128],
                      waits=[(sp, sp.n)], inc=dq3)
                wkp_done = dq3.n
                sv3 = c.sem("sv")
                for tb in range(NT):
                    par = tb % 2
                    ps = pp[:, par * TB:(par + 1) * TB]
                    c.op("tensor", "matmul", ps, wkp[:],
                         kv_lat[:, tb * TB:(tb + 1) * TB],
                         start=True, stop=True,
                         waits=[(dq3, wkp_done), (seq, seq.n),
                                (sv, max(0, sv.n - 2))], inc=sp)
                    sl = slice(tb * TB, (tb + 1) * TB)
                    c.op("vector", "tensor_copy", qtmp[:], ps,
                         waits=[(sp, sp.n)], inc=sv)
                    c.op("vector", "tensor_copy", qtmp[32:64, :],
                         krope[:, sl], waits=[(sv, sv.n)], inc=sv)
                    c.op("vector", "tensor_copy", qtmp[96:128, :],
                         krope[:, sl], waits=[(sv, sv.n)], inc=sv)
                    c.op("vector", "stream_shuffle", qsh[:], qtmp[:],
                         swap_mask, waits=[(sv, sv.n)], inc=sv)
                    c.op("vector", "tensor_tensor", khp[:, sl], qtmp[:],
                         cosx[:, sl], ALU.mult, waits=[(sv, sv.n)], inc=sv)
                    c.op("vector", "tensor_tensor", qsh[:], qsh[:],
                         sinx[:, sl], ALU.mult, waits=[(sv, sv.n)], inc=sv)
                    c.op("vector", "tensor_tensor", khp[:, sl], khp[:, sl],
                         qsh[:], ALU.add, waits=[(sv, sv.n)], inc=sv)
                kq_done = sv.n
                its_list = [(hh2, tqb2) for hh2 in range(2)
                            for tqb2 in range(NT)]
                gsteps = []
                for li, (hh2, tqb2) in enumerate(its_list):
                    for tkb2 in range(4 * tqb2 + 4):
                        gsteps.append((li, hh2, tqb2, tkb2))
                it_base = it3

                def emit_vload(li, kB):
                    hh2, tqb2 = its_list[li]
                    h = hp * 2 + hh2
                    gvb = vb_ctr[0]
                    vb_ctr[0] += 1
                    gvb_of[(it_base + li, kB)] = gvb
                    kslot = gvb % 4
                    vw = []
                    if gvb - 4 in vfree3:
                        vw.append((sp, vfree3[gvb - 4]))
                    if gvb == 0:
                        vw.append((sv, vones_done))
                    c.dma("sync",
                          vbuf[:, kslot * 256:(kslot + 1) * 256]
                          .rearrange("p (tt dd) -> p tt dd", tt=4)[:, :, 0:32],
                          kvd[kB * 512:(kB + 1) * 512, h * 32:(h + 1) * 32]
                          .rearrange("(tt p) dd -> p tt dd", p=128),
                          waits=vw, inc=dvv3[kslot])
                    vld_map[gvb] = dvv3[kslot].n

                def emit_qk3(gi):
                    li, hh2, tqb2, tkb = gsteps[gi]
                    g3 = g3b[0] + gi
                    base = hh2 * 64
                    q0 = tqb2 * TB
                    kB, sub = divmod(tkb, 4)
                    if sub == 0:
                        emit_vload(li, kB)
                    sl2 = g3 % 3
                    scs = ps_sc[:, sl2 * TB:(sl2 + 1) * TB]
                    w = [(sv, kq_done)]
                    if g3 - 3 in exp3:
                        w.append((sa, exp3[g3 - 3]))
                    t0k = tkb * 128
                    c.op("tensor", "matmul", scs,
                         khp[base:base + 64, t0k:t0k + 128],
                         qhp[base:base + 64, q0:q0 + TB],
                         start=True, stop=True, waits=w, inc=sp)
                    qk3v[g3] = sp.n
                    pbv = prob[:, sl2 * TB:(sl2 + 1) * TB]
                    ew = [(sp, qk3v[g3]), (dq3, tab_done)]
                    if g3 - 3 in pv3:
                        ew.append((sp, pv3[g3 - 3]))
                    c.op("scalar", "activation", pbv, scs, AF.Exp,
                         scale=1.0 / 8.0, waits=ew, inc=sa)
                    exp3[g3] = sa.n
                    if t0k + 128 > q0:
                        moff = (t0k - q0) // 128 * TB
                        c.op("vector", "tensor_tensor", pbv, pbv,
                             trim[:, moff:moff + TB], ALU.mult,
                             waits=[(sa, exp3[g3]), (dq3, tab_done)],
                             inc=sv)
                        msk3[g3] = sv.n

                def emit_tail3(itid, hh2, tqb2):
                    h = hp * 2 + hh2
                    q0 = tqb2 * TB
                    ipp = itid % 2
                    av_done = sp.n
                    rdv = rden[32:33, ipp * TB:(ipp + 1) * TB]
                    rbv = rdbc[0:32, ipp * TB:(ipp + 1) * TB]
                    ytv = ytmp[0:32, ipp * TB:(ipp + 1) * TB]
                    h2 = it3h.get(itid - 2, {})
                    c.op("vector", "reciprocal", rdv,
                         ps_y[32:33, ipp * TB:(ipp + 1) * TB],
                         waits=[(sp, av_done), (drd3, h2.get("rddone", 0))],
                         inc=sv)
                    c.bounce_bcast(rdv, TB, rbv, 32, waits=[(sv, sv.n)],
                                   waits2=[(sv, h2.get("ytdone", 0))],
                                   inc=drd3)
                    rd_done = drd3.n
                    c.op("vector", "tensor_tensor", ytv,
                         ps_y[0:32, ipp * TB:(ipp + 1) * TB], rbv, ALU.mult,
                         waits=[(drd3, rd_done), (sdy, h2.get("stdone", 0))],
                         inc=sv)
                    yt_done = sv.n
                    c.dma("sync", yda[h * 32:(h + 1) * 32, q0:q0 + TB],
                          ytv, waits=[(sv, yt_done)], inc=sdy)
                    it3h[itid] = {"ytdone": yt_done, "stdone": sdy.n,
                                  "rddone": rd_done}

                def emit_pv3(gi):
                    li, hh2, tqb2, tkb = gsteps[gi]
                    g3 = g3b[0] + gi
                    itid = it_base + li
                    ntk = 4 * tqb2 + 4
                    kB, sub = divmod(tkb, 4)
                    gvb = gvb_of[(itid, kB)]
                    kslot = gvb % 4
                    sl2 = g3 % 3
                    ipp = itid % 2
                    pbv = prob[:, sl2 * TB:(sl2 + 1) * TB]
                    pw = (sv, msk3[g3]) if g3 in msk3 else (sa, exp3[g3])
                    w_av = [pw, (dvv3[kslot], vld_map[gvb])]
                    if tkb == 0:
                        h2 = it3h.get(itid - 2, {})
                        w_av.append((sv, h2.get("ytdone", 0)))
                    c.op("tensor", "matmul",
                         ps_y[:, ipp * TB:(ipp + 1) * TB],
                         vbuf[:, kslot * 256 + sub * 64:
                              kslot * 256 + sub * 64 + 64],
                         pbv, start=(tkb == 0), stop=(tkb == ntk - 1),
                         waits=w_av, inc=sp)
                    pv3[g3] = sp.n
                    if sub == 3:
                        vfree3[gvb] = sp.n
                    if tkb == ntk - 1:
                        emit_tail3(itid, hh2, tqb2)

                Gh = len(gsteps)
                for gi in range(Gh + 2):
                    if gi < Gh:
                        emit_qk3(gi)
                    if gi >= 2:
                        emit_pv3(gi - 2)
                g3b[0] += Gh
                it3 = it_base + len(its_list)
            c.wait("sync", sdy, sdy.n)
            c.barrier()
        with c.scope() as p4:
            ybuf = c.sb_in(p4, c.fresh("ybuf3"), [128, 4 * T], BF16)
            xnb = c.sb_in(p4, c.fresh("xnb3"), [128, ND * TB], BF16)
            rsl = c.sb_in(p4, c.fresh("rsl3"), [128, ND * TB], F32)
            pp4 = c.ps_in(p4, c.fresh("pp3f4"), [128, 4 * TB], F32)
            wb4 = [c.sb_in(p4, c.fresh("wb3f4"), [128, 4 * 512], BF16)
                   for _ in range(2)]
            wl4 = [c.sem("wl_3f40"), c.sem("wl_3f41")]
            seo = c.sem("seo3f")
            c.dma("sync", ybuf[:].rearrange("p (mc t) -> p mc t", mc=4),
                  yda.rearrange("(mc p) t -> p mc t", p=128), inc=dxs)
            yld = dxs.n
            for tb in range(NT):
                t0 = tb * TB
                c.dma("sync", xnb[:].rearrange("p (dc t) -> p dc t", dc=ND),
                      xnd.rearrange("(dc p) t -> p dc t", p=128)[:, :, t0:t0 + TB],
                      waits=[(seo, seo.n)], inc=dxs)
                xld = dxs.n

                def evac_ro(mi, tbb, ps, mmc):
                    d = mi // 128
                    sl = slice(d * TB, (d + 1) * TB)
                    c.op("vector", "tensor_tensor", rsl[:, sl],
                         xnb[:].rearrange("p (dc t) -> p dc t", dc=ND)[:, d],
                         ps, ALU.add, waits=[(sp, mmc), (dxs, xld)], inc=seo)

                proj(c, p4, lambda k, tbb: ybuf[:, k * T + t0:k * T + t0 + TB],
                     W["L3_ffn_Wo_nope"], 512, 0, D, 1, evac_ro, "o3f",
                     evac_sem=seo,
                     first_waits=[(dxs, yld), (seo, seo.n)],
                     pp=pp4, wbufs=wb4, wl_sems=wl4)
                c.dma("sync",
                      rd.rearrange("(dc p) t -> p dc t", p=128)[:, :, t0:t0 + TB],
                      rsl[:].rearrange("p (dc t) -> p dc t", dc=ND),
                      waits=[(seo, seo.n)], inc=dxs)
            c.wait("sync", dxs, dxs.n)
            c.barrier()
    with c.scope() as p5:
        sv, sa, sp = c.sem("sv"), c.sem("sa"), c.sem("sp")
        dxs = c.sem("dxs2")
        r2row = c.sb_in(p5, c.fresh("r2row"), [1, T], F32)
        rsl = c.sb_in(p5, c.fresh("rsl5"), [128, ND * TB], F32)
        rq = c.sb_in(p5, c.fresh("rq5"), [128, ND * TB], F32)
        psq = c.ps_in(p5, c.fresh("psq5"), [1, TB], F32)
        ones_colf = c.consts["ones_colf"]
        r2d = c.dram(c.fresh("r2d3f"), [T], F32)
        dr2 = c.sem("dr2")
        for tb in range(NT):
            t0 = tb * TB
            c.dma("sync", rsl[:].rearrange("p (dc t) -> p dc t", dc=ND),
                  rd.rearrange("(dc p) t -> p dc t", p=128)[:, :, t0:t0 + TB],
                  waits=[(sa, sa.n)], inc=dxs)
            c.op("scalar", "activation", rq[:], rsl[:], AF.Square,
                 waits=[(dxs, dxs.n), (sp, sp.n)], inc=sa)
            sq_done = sa.n
            for d in range(ND):
                c.op("tensor", "matmul", psq[0:1, :], ones_colf[:, 0:1],
                     rq[:, d * TB:(d + 1) * TB],
                     start=(d == 0), stop=(d == ND - 1),
                     waits=([(sa, sq_done), (sv, sv.n)] if d == 0 else ()),
                     inc=sp)
            seg = r2row[0:1, t0:t0 + TB]
            c.op("vector", "tensor_scalar_mul", seg, psq[0:1, :], 1.0 / D,
                 waits=[(sp, sp.n)], inc=sv)
            c.op("scalar", "activation", seg, seg, AF.Sqrt, bias=1e-5,
                 waits=[(sv, sv.n)], inc=sa)
            c.op("vector", "reciprocal", seg, seg, waits=[(sa, sa.n)], inc=sv)
        c.dma("sync", AP(r2d.tensor, 0, [[T, 1], [1, T]]), r2row[0:1, :],
              waits=[(sv, sv.n)], inc=dr2)
        c.wait("sync", dr2, dr2.n)
        c.barrier()
    with c.scope() as p6:
        sv, sa, sp = c.sem("sv"), c.sem("sa"), c.sem("sp")
        dxs, dst_ = c.sem("dxs2"), c.sem("dst")
        dcw = c.sem("dcw")
        TC6 = 1024
        NC6 = T // TC6
        NB6 = TC6 // TB
        NTT = TC6 // 128
        wln2 = load_mat(c, p6, W["L3_ffn_w_ln2c"], 128, ND, "wln2b", inc=dxs)
        gw = c.sb_in(p6, c.fresh("gw3f"), [128, 8 * ND], BF16)
        ebr = load_row(c, p6, W["L3_ffn_expert_bias"], 8, "ebr", dtype=BF16,
                       queue="gpsimd", inc=dxs)
        load_w_tiles(c, p6, W["L3_ffn_Wgate"], D, 0, 8, "gw3", inc=dxs,
                     handle=gw)
        gate_done = dxs.n
        rr2b = c.sb_in(p6, c.fresh("rr2b"), [128, TC6], F32)
        rsl = c.sb_in(p6, c.fresh("rsl6"), [128, ND * TC6], F32)
        x2 = c.sb_in(p6, c.fresh("x26"), [128, ND * TC6], BF16)
        xsl = c.sb_in(p6, c.fresh("xsl6"), [128, ND * TC6], F32)
        gsb = c.sb_in(p6, c.fresh("gsb"), [128, 4 * TC6], BF16)
        usb = c.sb_in(p6, c.fresh("usb"), [128, 4 * TC6], BF16)
        acc = c.sb_in(p6, c.fresh("acc6"), [128, ND * TC6], F32)
        web = c.sb_in(p6, c.fresh("web6"), [128, TC6], BF16)
        rtmp = c.sb_in(p6, c.fresh("rtmp6"), [128, TB], F32)
        probs = c.sb_in(p6, c.fresh("probs6"), [128, NTT * 8], F32)
        cwb = c.sb_in(p6, c.fresh("cwb6"), [128, NTT * 8], BF16)
        cwT = c.sb_in(p6, c.fresh("cwT6"), [8, TC6], BF16)
        g1 = c.sb_in(p6, c.fresh("g16"), [128, 8], F32)
        g2 = c.sb_in(p6, c.fresh("g26"), [128, 8], F32)
        gm = c.sb_in(p6, c.fresh("gm6"), [128, 4], F32)
        nmx = c.sb_in(p6, c.fresh("nmx6"), [128, 2], F32)
        pp = c.ps_in(p6, c.fresh("pp6"), [128, 4 * TB], F32)
        psr = c.ps_in(p6, c.fresh("psr6"), [128, 8], F32)
        pst = c.ps_in(p6, c.fresh("pst6"), [8, 128], F32)
        wbufs = [c.sb_in(p6, c.fresh("wb6"), [128, 8 * 512], BF16)
                 for _ in range(2)]
        wl_sems = [c.sem("wl_3f60"), c.sem("wl_3f61")]
        seh = c.sem("seh3f")
        seu = c.sem("seu3f")
        seo = c.sem("seo3f6")
        set_ = c.sem("set3f")
        identb = c.consts["identb"]
        ones_row = c.consts["ones_row"]
        for ch in range(NC6):
            t0 = ch * TC6
            wl = [(dst_, dst_.n)] if ch else []
            c.dma("sync", rr2b[:], AP(r2d.tensor, t0, [[0, 128], [1, TC6]]),
                  waits=[(sv, sv.n)] + wl, inc=dxs)
            c.dma("sync", rsl[:].rearrange("p (dc t) -> p dc t", dc=ND),
                  rd.rearrange("(dc p) t -> p dc t", p=128)[:, :, t0:t0 + TC6],
                  inc=dxs)
            c.load_x(xsl, t0, t0 + TC6, inc=dxs)
            rld = dxs.n
            for d in range(ND):
                c.op("vector", "scalar_tensor_tensor",
                     x2[:, d * TC6:(d + 1) * TC6],
                     rsl[:, d * TC6:(d + 1) * TC6], wln2[:, d:d + 1],
                     rr2b[:], ALU.mult, ALU.mult,
                     waits=[(dxs, rld), (dxs, gate_done), (sp, sp.n)],
                     inc=sv)
            x2_done = sv.n
            for tt in range(NTT):
                for k in range(ND):
                    c.op("tensor", "matmul", psr[:],
                         x2[:, k * TC6 + tt * 128: k * TC6 + tt * 128 + 128],
                         gw[:, k * 8:(k + 1) * 8],
                         start=(k == 0), stop=False,
                         waits=([(sv, x2_done), (sv, sv.n)] if k == 0 else ()),
                         inc=sp)
                c.op("tensor", "matmul", psr[:],
                     ones_row[0:1, 0:128], ebr[0:1, 0:8],
                     start=False, stop=True, inc=sp)
                pv = probs[:, tt * 8:(tt + 1) * 8]
                c.op("vector", "reduce_max", nmx[:, 0:1], psr[:],
                     axis=mybir.AxisListType.X, waits=[(sp, sp.n)], inc=sv)
                c.op("vector", "tensor_scalar_mul", nmx[:, 0:1], nmx[:, 0:1],
                     -1.0, waits=[(sv, sv.n)], inc=sv)
                c.op("scalar", "activation", pv, psr[:], AF.Exp,
                     bias=nmx[:, 0:1], accum_out=nmx[:, 1:2],
                     waits=[(sv, sv.n)], inc=sa)
                c.op("vector", "reciprocal", nmx[:, 1:2], nmx[:, 1:2],
                     waits=[(sa, sa.n)], inc=sv)
                c.op("vector", "tensor_scalar", pv, pv, nmx[:, 1:2], None,
                     ALU.mult, waits=[(sv, sv.n)], inc=sv)
                c.op("vector", "reduce_max", gm[:, 0:1], pv,
                     axis=mybir.AxisListType.X, waits=[(sv, sv.n)], inc=sv)
                c.op("vector", "tensor_scalar", g1[:, 0:8], pv, gm[:, 0:1],
                     None, ALU.is_equal, waits=[(sv, sv.n)], inc=sv)
                c.op("vector", "scalar_tensor_tensor", g2[:, 0:8], g1[:, 0:8],
                     -1e30, pv, ALU.mult, ALU.add, waits=[(sv, sv.n)], inc=sv)
                c.op("vector", "reduce_max", gm[:, 1:2], g2[:, 0:8],
                     axis=mybir.AxisListType.X, waits=[(sv, sv.n)], inc=sv)
                c.op("vector", "tensor_scalar", g2[:, 0:8], g2[:, 0:8],
                     gm[:, 1:2], None, ALU.is_equal, waits=[(sv, sv.n)], inc=sv)
                c.op("vector", "tensor_tensor", gm[:, 2:3], gm[:, 0:1],
                     gm[:, 1:2], ALU.add, waits=[(sv, sv.n)], inc=sv)
                c.op("vector", "reciprocal", gm[:, 3:4], gm[:, 2:3],
                     waits=[(sv, sv.n)], inc=sv)
                c.op("vector", "tensor_tensor", gm[:, 0:1], gm[:, 0:1],
                     gm[:, 3:4], ALU.mult, waits=[(sv, sv.n)], inc=sv)
                c.op("vector", "tensor_tensor", gm[:, 1:2], gm[:, 1:2],
                     gm[:, 3:4], ALU.mult, waits=[(sv, sv.n)], inc=sv)
                c.op("vector", "tensor_scalar", g1[:, 0:8], g1[:, 0:8],
                     gm[:, 0:1], None, ALU.mult, waits=[(sv, sv.n)], inc=sv)
                c.op("vector", "tensor_scalar", g2[:, 0:8], g2[:, 0:8],
                     gm[:, 1:2], None, ALU.mult, waits=[(sv, sv.n)], inc=sv)
                c.op("vector", "tensor_tensor", cwb[:, tt * 8:(tt + 1) * 8],
                     g1[:, 0:8], g2[:, 0:8], ALU.add, waits=[(sv, sv.n)],
                     inc=sv)
                c.op("tensor", "matmul", pst[:], cwb[:, tt * 8:(tt + 1) * 8],
                     identb[:], start=True, stop=True,
                     waits=[(sv, sv.n), (set_, set_.n)], inc=sp)
                c.op("vector", "tensor_copy", cwT[:, tt * 128:(tt + 1) * 128],
                     pst[:], waits=[(sp, sp.n), (dcw, dcw.n)], inc=set_)
            c.dma("sync", cwd[:, t0:t0 + TC6], cwT[:],
                  waits=[(set_, set_.n), (dcw, dcw.n)], inc=dcw)
            cw_st = dcw.n
            for ei in [8] + list(range(8)):
                shared = (ei == 8)
                e = 0 if shared else ei
                if not shared:
                    c.dma("gpsimd", web[:],
                          AP(cwd.tensor, e * T + t0, [[0, 128], [1, TC6]]),
                          waits=[(dcw, cw_st), (seo, seo.n)], inc=dcw)
                web_done = dcw.n
                wg = "L3_ffn_sWgf" if shared else "L3_ffn_Wgf"
                wu = "L3_ffn_sWuf" if shared else "L3_ffn_Wuf"
                wd = "L3_ffn_sWdf" if shared else "L3_ffn_Wdf"
                rlo = 0 if shared else e * D
                rlo2 = 0 if shared else e * 512

                def evac_g(mi, tb, ps, mmc):
                    m = mi // 128
                    c.op("scalar", "activation",
                         gsb[:, m * TC6 + tb * TB: m * TC6 + (tb + 1) * TB],
                         ps, AF.Silu, waits=[(sp, mmc)], inc=seh)

                def evac_u(mi, tb, ps, mmc):
                    m = mi // 128
                    c.op("scalar", "activation",
                         usb[:, m * TC6 + tb * TB: m * TC6 + (tb + 1) * TB],
                         ps, AF.Identity, waits=[(sp, mmc)], inc=seu)

                proj(c, p6, lambda k, tb: x2[:, k * TC6 + tb * TB:
                                             k * TC6 + (tb + 1) * TB],
                     W[wg], D, 0, 512, NB6, evac_g, "wg6", evac_sem=seh,
                     w_row_lo=rlo, first_waits=[(sv, x2_done), (seo, seo.n)],
                     pp=pp, wbufs=wbufs, wl_sems=wl_sems)
                proj(c, p6, lambda k, tb: x2[:, k * TC6 + tb * TB:
                                             k * TC6 + (tb + 1) * TB],
                     W[wu], D, 0, 512, NB6, evac_u, "wu6", evac_sem=seu,
                     w_row_lo=rlo, first_waits=[(seh, seh.n)],
                     pp=pp, wbufs=wbufs, wl_sems=wl_sems)
                c.op("vector", "tensor_tensor", gsb[:], gsb[:], usb[:],
                     ALU.mult, waits=[(seh, seh.n), (seu, seu.n)], inc=sv)
                gu_done = sv.n

                def evac_y(mi, tb, ps, mmc, shared=shared, web_done=web_done,
                           ei=ei):
                    d = mi // 128
                    sl = slice(d * TC6 + tb * TB, d * TC6 + (tb + 1) * TB)
                    if shared:
                        c.op("vector", "tensor_copy", acc[:, sl], ps,
                             waits=[(sp, mmc)], inc=seo)
                    else:
                        wsl = web[:, tb * TB:(tb + 1) * TB]
                        c.op("vector", "tensor_tensor", rtmp[:], ps, wsl,
                             ALU.mult, waits=[(sp, mmc), (dcw, web_done)],
                             inc=sv)
                        c.op("vector", "tensor_tensor", acc[:, sl],
                             acc[:, sl], rtmp[:], ALU.add, inc=seo)

                proj(c, p6, lambda k, tb: gsb[:, k * TC6 + tb * TB:
                                              k * TC6 + (tb + 1) * TB],
                     W[wd], 512, 0, D, NB6, evac_y, "wd6", evac_sem=seo,
                     w_row_lo=rlo2, mblk=512,
                     first_waits=[(sv, gu_done)],
                     pp=pp, wbufs=wbufs, wl_sems=wl_sems)
            for d in range(ND):
                sl = slice(d * TC6, (d + 1) * TC6)
                c.op("vector", "scalar_tensor_tensor", xsl[:, sl], acc[:, sl],
                     0.5, xsl[:, sl], ALU.mult, ALU.add,
                     waits=[(seo, seo.n)], inc=sv)
                c.op("vector", "scalar_tensor_tensor", xsl[:, sl],
                     rsl[:, sl], 0.5, xsl[:, sl], ALU.mult, ALU.add,
                     waits=[(sv, sv.n)], inc=sv)
            c.store_x(xsl, t0, t0 + TC6, waits=[(sv, sv.n)], inc=dst_)
        c.wait("sync", dst_, dst_.n)
        c.barrier()



def _np(x):
    return np.ascontiguousarray(np.asarray(x, dtype=np.float32))


BLOCK_SPECS = {
    "L1_ffn_w1": (1024, 512), "L1_ffn_w2": (4096, 128),
    "L3_attn_wq": (1024, 512), "L3_attn_wk": (1024, 512),
    "L3_attn_wv": (1024, 512),
    "L2_attn_w_qkv": (1024, 512), "L2_attn_w_out": (1024, 512),
    "L2_ffn_w1": (1024, 512), "L2_ffn_w2": (1024, 512),
    "L0_attn_in_proj_w": (1024, 512), "L0_attn_dt_w": (2048, 128),
    "L0_attn_b_w": (2048, 128), "L0_attn_c_w": (2048, 128),
    "L0_attn_out_w": (2048, 128),
    "L0_ffn_W1f": (1024, 512), "L0_ffn_W2f": (4096, 128),
    "L1_attn_W_in": (1024, 256), "L1_attn_W_xp": (2048, 64),
    "L1_attn_W_out": (2048, 128),
    "L3_ffn_Wq": (1024, 128), "L3_ffn_Wkva": (1024, 160),
    "L3_ffn_Wo_nope": (512, 512),
    "L3_ffn_Wgf": (1024, 512), "L3_ffn_Wuf": (1024, 512),
    "L3_ffn_Wdf": (512, 512), "L3_ffn_sWgf": (1024, 512),
    "L3_ffn_sWuf": (1024, 512), "L3_ffn_sWdf": (512, 512),
}


def _blockify(arr, kin, mblk):
    rows, ncols = arr.shape
    nrg = rows // kin
    kc = kin // 128
    nmb = ncols // mblk
    out = np.empty((nrg * nmb, 128, kc * mblk), np.float32)
    for rg in range(nrg):
        for mb in range(nmb):
            blk = arr[rg * kin:(rg + 1) * kin, mb * mblk:(mb + 1) * mblk]
            out[rg * nmb + mb] = blk.reshape(kc, 128, mblk) \
                .transpose(1, 0, 2).reshape(128, kc * mblk)
    return np.ascontiguousarray(out)


def host_prep(inputs):
    W = {k: _np(v) for k, v in inputs.items()}
    for k in ["L0_attn_in_proj_w", "L0_attn_dt_w", "L0_attn_b_w", "L0_attn_c_w",
              "L0_attn_out_w", "L2_ffn_router_w", "L3_attn_wq", "L3_attn_wk",
              "L3_attn_wv"]:
        W[k] = np.ascontiguousarray(W[k].T)
    for k in ["L0_attn_conv_w", "L1_attn_conv_w"]:
        W[k] = np.ascontiguousarray(W[k].reshape(W[k].shape[0], -1))
    for k in ["L1_ffn_b1"]:
        W[k + "c"] = np.ascontiguousarray(W[k].reshape(-1, 128).T)
    for k in ["L0_attn_in_proj_b", "L0_attn_conv_b", "L0_attn_dt_b",
              "L0_attn_A_log", "L0_attn_D_param"]:
        W[k + "c"] = np.ascontiguousarray(W[k].reshape(-1, 128).T)
    for pre in ["L0_attn", "L1_attn"]:
        cw = W[pre + "_conv_w"].reshape(16, 128, 4).transpose(1, 0, 2)
        W[pre + "_conv_wc"] = np.ascontiguousarray(cw.reshape(128, 64))
    for k in ["L1_attn_conv_b", "L1_attn_w_dt", "L1_attn_b_dt",
              "L1_attn_D_param"]:
        W[k + "c"] = np.ascontiguousarray(W[k].reshape(-1, 128).T)
    W["L1_attn_norm_wc"] = np.ascontiguousarray(
        W["L1_attn_norm_w"].reshape(-1, 128).T)
    al = W["L1_attn_A_log"].reshape(16, 128, 16).transpose(1, 0, 2)
    W["L1_attn_A_logc"] = np.ascontiguousarray(al.reshape(128, 256))
    wxp = np.zeros((2048, 64), np.float32)
    wxp[:, 0:33] = W["L1_attn_W_x"]
    W["L1_attn_W_xp"] = wxp
    W["L0_ffn_W1f"] = np.ascontiguousarray(W["L0_ffn_W1"].reshape(8 * 1024, 4096))
    W["L0_ffn_W2f"] = np.ascontiguousarray(W["L0_ffn_W2"].reshape(8 * 4096, 1024))
    b1c = np.stack([W["L0_ffn_b1"][e].reshape(32, 128).T for e in range(8)], 1)
    W["L0_ffn_b1c"] = np.ascontiguousarray(b1c.reshape(128, 8 * 32))
    W["L0_ffn_b2r"] = np.ascontiguousarray(W["L0_ffn_b2"].reshape(-1))
    for k in ["L3_ffn_w_ln1", "L3_ffn_w_ln2"]:
        W[k + "c"] = np.ascontiguousarray(W[k].reshape(-1, 128).T)
    kvb = W["L3_ffn_Wkvb"]
    pad = np.zeros((128, 8 * 128), np.float32)
    for hp in range(8):
        pad[:, hp * 128 + 0:hp * 128 + 32] = kvb[:, (2 * hp) * 32:(2 * hp + 1) * 32]
        pad[:, hp * 128 + 64:hp * 128 + 96] = kvb[:, (2 * hp + 1) * 32:(2 * hp + 2) * 32]
    W["L3_ffn_Wkvb_pad"] = pad
    wo = W["L3_ffn_Wo"]
    W["L3_ffn_Wo_nope"] = np.ascontiguousarray(
        wo.reshape(16, 64, 1024)[:, 0:32, :].reshape(512, 1024))
    W["L3_ffn_Wgf"] = np.ascontiguousarray(W["L3_ffn_Wg"].reshape(8 * 1024, 512))
    W["L3_ffn_Wuf"] = np.ascontiguousarray(W["L3_ffn_Wu"].reshape(8 * 1024, 512))
    W["L3_ffn_Wdf"] = np.ascontiguousarray(W["L3_ffn_Wd"].reshape(8 * 512, 1024))
    W["L3_ffn_sWgf"] = np.ascontiguousarray(W["L3_ffn_sWg"].reshape(1024, 512))
    W["L3_ffn_sWuf"] = np.ascontiguousarray(W["L3_ffn_sWu"].reshape(1024, 512))
    W["L3_ffn_sWdf"] = np.ascontiguousarray(W["L3_ffn_sWd"].reshape(512, 1024))
    BLOCK_META.clear()
    for name, (kin, mblk) in BLOCK_SPECS.items():
        ncols = W[name].shape[1]
        BLOCK_META[name] = (kin, mblk, ncols)
        W[name] = _blockify(W[name], kin, mblk)
    return W


SCALAR_CONSTS = [1e-5, 1e-6, 1e-4, 2.0, -2.0, 0.5]


def host_consts():
    C = {}
    C["c_ones_col"] = np.ones((128, 1), np.float32)
    C["c_ones_row"] = np.ones((1, T), np.float32)
    C["c_scal"] = np.tile(np.asarray(SCALAR_CONSTS, np.float32)[None, :], (128, 1))
    C["c_ident"] = np.eye(128, dtype=np.float32)
    pos = np.arange(T, dtype=np.float64)
    invf = 1.0 / (10000.0 ** (np.arange(0, D, 2, dtype=np.float64) / D))
    ang = pos[None, :] * invf[:, None]
    cos = np.cos(ang)
    sin = np.sin(ang)
    cos_fm = np.repeat(cos, 2, axis=0).astype(np.float32)
    ssin_fm = np.empty((D, T), np.float32)
    ssin_fm[0::2] = -sin
    ssin_fm[1::2] = sin
    C["c_cos_l3a"] = np.ascontiguousarray(cos_fm)
    C["c_ssin_l3a"] = np.ascontiguousarray(ssin_fm)
    p = np.arange(128)[:, None]
    f = np.arange(512)[None, :]
    tri = np.zeros((128, 4 * 512), np.float32)
    for j in range(4):
        tri[:, j * 512:(j + 1) * 512] = ((p + j * 128) <= f).astype(np.float32)
    C["c_tri512"] = tri
    fidx = np.arange(33, dtype=np.float64)
    n64 = np.arange(64, dtype=np.float64)
    ang2 = 2.0 * np.pi * fidx[:, None] * n64[None, :] / 64.0
    C["c_fftc"] = np.cos(ang2).astype(np.float32)
    C["c_ffts"] = np.sin(ang2).astype(np.float32)
    wf = np.full(33, 2.0 / 64.0)
    wf[0] = 1.0 / 64.0
    wf[32] = 1.0 / 64.0
    C["c_fftw"] = wf.astype(np.float32)
    th = 100000.0
    fr = 1.0 / th ** (np.arange(0, 32, 2, dtype=np.float64) / 32)
    ang3 = np.arange(T, dtype=np.float64)[None, :] * fr[:, None]
    c3 = np.cos(ang3)
    s3 = np.sin(ang3)
    cos_ext = np.ones((128, T), np.float64)
    ssin_ext = np.zeros((128, T), np.float64)
    for half in range(2):
        r0 = 32 + 64 * half
        cos_ext[r0 + 0:r0 + 32:2] = c3
        cos_ext[r0 + 1:r0 + 32:2] = c3
        ssin_ext[r0 + 0:r0 + 32:2] = -s3
        ssin_ext[r0 + 1:r0 + 32:2] = s3
    C["c_cos_l3f"] = cos_ext.astype(np.float32)
    C["c_ssin_l3f"] = ssin_ext.astype(np.float32)
    return C


STAGE_FNS = {
    "l1f_a": st_l1f,
    "l1f_b": st_l1f,
    "l3a": st_l3a,
    "l2a": st_l2a,
    "l2f_a": st_l2f,
    "l2f_b": st_l2f,
    "l0a": st_l0a,
    "l1a": st_l1a,
    "l0f_a": lambda c, W: st_l0f(c, W, 0),
    "l0f_b": lambda c, W: st_l0f(c, W, 1),
    "l3f_a": lambda c, W: st_l3f(c, W, 0),
    "l3f_b": lambda c, W: st_l3f(c, W, 1),
}

STAGE_WEIGHTS = {
    "l1f_a": ["L1_ffn_w1", "L1_ffn_b1c", "L1_ffn_w2", "L1_ffn_b2"],
    "l1f_b": ["L1_ffn_w1", "L1_ffn_b1c", "L1_ffn_w2", "L1_ffn_b2"],
    "l3a": ["L3_attn_wq", "L3_attn_wk", "L3_attn_wv"],
    "l2a": ["L2_attn_w_qkv", "L2_attn_w_out", "L2_attn_b_out",
            "L2_attn_mask_logits"],
    "l2f_a": ["L2_ffn_router_w", "L2_ffn_w1", "L2_ffn_w2"],
    "l2f_b": ["L2_ffn_router_w", "L2_ffn_w1", "L2_ffn_w2"],
    "l0a": ["L0_attn_in_proj_w", "L0_attn_in_proj_bc", "L0_attn_conv_wc",
            "L0_attn_conv_bc", "L0_attn_dt_w", "L0_attn_dt_bc", "L0_attn_b_w",
            "L0_attn_c_w", "L0_attn_A_logc", "L0_attn_D_paramc",
            "L0_attn_out_w", "L0_attn_out_b"],
    "l1a": ["L1_attn_W_in", "L1_attn_conv_wc", "L1_attn_conv_bc",
            "L1_attn_W_xp", "L1_attn_w_dtc", "L1_attn_b_dtc", "L1_attn_A_logc",
            "L1_attn_D_paramc", "L1_attn_W_out", "L1_attn_norm_wc"],
    "l0f_a": ["L0_ffn_Wr", "L0_ffn_W1f", "L0_ffn_b1c", "L0_ffn_W2f",
              "L0_ffn_b2r"],
    "l0f_b": ["L0_ffn_Wr", "L0_ffn_W1f", "L0_ffn_b1c", "L0_ffn_W2f",
              "L0_ffn_b2r"],
    "l3f_a": ["L3_ffn_w_ln1c", "L3_ffn_w_ln2c", "L3_ffn_Wq", "L3_ffn_Wkva", "L3_ffn_Wkvb", "L3_ffn_Wkvb_pad", "L3_ffn_Wo_nope", "L3_ffn_Wgate", "L3_ffn_expert_bias", "L3_ffn_Wgf", "L3_ffn_Wuf", "L3_ffn_Wdf", "L3_ffn_sWgf", "L3_ffn_sWuf", "L3_ffn_sWdf"],
    "l3f_b": ["L3_ffn_w_ln1c", "L3_ffn_w_ln2c", "L3_ffn_Wq", "L3_ffn_Wkva", "L3_ffn_Wkvb", "L3_ffn_Wkvb_pad", "L3_ffn_Wo_nope", "L3_ffn_Wgate", "L3_ffn_expert_bias", "L3_ffn_Wgf", "L3_ffn_Wuf", "L3_ffn_Wdf", "L3_ffn_sWgf", "L3_ffn_sWuf", "L3_ffn_sWdf"],
}

ALL_STAGES = ["l0f_a", "l0a", "l0f_b", "l1f_a", "l1a", "l1f_b",
              "l2f_a", "l2a", "l2f_b", "l3f_a", "l3a", "l3f_b"]

GROUPS = None


def emit_weight_cache(c, Waps, Wmeta):
    sv = c.sem("sv")
    dldw = [c.sem("dwcl0"), c.sem("dwcl1")]
    dstw = [c.sem("dwcs0"), c.sem("dwcs1")]
    with c.scope() as sc:
        bufs = [c.sb_in(sc, c.fresh("wcf"), [128, 4096], F32) for _ in range(2)]
        obufs = [c.sb_in(sc, c.fresh("wcb"), [128, 4096], BF16) for _ in range(2)]
        jobs = []
        for name, arr in Wmeta.items():
            if name in BLOCK_META and arr.ndim == 3:
                bf = c.dram(name + "_bf", list(arr.shape), BF16)
                c.wbf[name] = bf
                for n in range(arr.shape[0]):
                    jobs.append((Waps[name], bf, n, arr.shape[2]))
        hist = {}
        for j, (srcap, bf, n, kcm) in enumerate(jobs):
            slot = j % 2
            wl = [(sv, hist[j - 2])] if j - 2 in hist else []
            c.dma("sync", bufs[slot][:, 0:kcm], srcap[n], waits=wl,
                  inc=dldw[slot])
            c.op("vector", "tensor_copy", obufs[slot][:, 0:kcm],
                 bufs[slot][:, 0:kcm],
                 waits=[(dldw[slot], dldw[slot].n),
                        (dstw[slot], dstw[slot].n)], inc=sv)
            cv = sv.n
            c.dma("sync", bf[n], obufs[slot][:, 0:kcm],
                  waits=[(sv, cv)], inc=dstw[slot])
            hist[j] = cv
        for s_ in dstw:
            c.wait("sync", s_, s_.n)
        c.barrier()


def build_group_program(stage_names, Wmeta):
    nc = bass.Bass("TRN2", target_bir_lowering=False, debug=False)
    c = Ctx(nc)
    xio = nc.dram_tensor("xio", [D, T], F32, kind="ExternalInput").ap()
    xout = nc.dram_tensor("xout", [D, T], F32, kind="ExternalOutput").ap()
    c.xdr = xout
    Waps = {name: nc.dram_tensor(name, list(arr.shape), F32,
                                 kind="ExternalInput").ap()
            for name, arr in Wmeta.items()}
    with c.estack:
        ones_col = c.sb("ones_col", [128, 1], BF16)
        ones_colf = c.sb("ones_colf", [128, 1], F32)
        ones_row = c.sb("ones_row", [1, T], BF16)
        c.consts["ones_col"] = ones_col
        c.consts["ones_colf"] = ones_colf
        c.consts["ones_row"] = ones_row
        dx = c.sem("dx")
        c.dma("gpsimd", ones_col[:], Waps["c_ones_col"], inc=dx)
        c.dma("sync", ones_colf[:], Waps["c_ones_col"], inc=dx)
        c.dma("gpsimd", ones_row[:], Waps["c_ones_row"], inc=dx)
        scal = c.sb("scal", [128, len(SCALAR_CONSTS)], F32)
        c.dma("sync", scal[:], Waps["c_scal"], inc=dx)
        identb = c.sb("identb", [128, 128], BF16)
        c.dma("gpsimd", identb[:], Waps["c_ident"], inc=dx)
        c.consts["identb"] = identb
        for j, v in enumerate(SCALAR_CONSTS):
            nc.const_aps.register(F32, float(v), scal[:, j:j + 1])
        c.dma("sync", xout, xio, inc=dx)
        c.barrier()
        blocked = [n for n, arr in Wmeta.items()
                   if n in BLOCK_META and getattr(arr, "ndim", 0) == 3]
        first_use = {}
        for sname in stage_names:
            for n in sorted(STAGE_WEIGHTS[sname]):
                if n in blocked and n not in first_use:
                    first_use[n] = sname
        gate = {}
        gi = [0]

        def emit_cache(snames):
            for sname in snames:
                names = [n for n in blocked if first_use.get(n) == sname]
                if not names:
                    continue
                ds = c.sem(f"dwb{gi[0]}")
                gi[0] += 1
                for name in names:
                    arr = Wmeta[name]
                    bf = c.dram(name + "_bf", list(arr.shape), BF16)
                    c.wbf[name] = bf
                    for nb in range(arr.shape[0]):
                        c.dma("gpsimd", bf[nb], Waps[name][nb], inc=ds)
                for name in names:
                    c.wbf_gate[name] = (ds, ds.n)

        emit_cache(stage_names)
        for sname in stage_names:
            STAGE_FNS[sname](c, Waps)
        build_block_program(c)
    return nc


def kernel(**inputs):
    W = host_prep(inputs)
    consts = host_consts()
    x = W.pop("L0_attn_x")
    xs = [np.ascontiguousarray(x[b].T) for b in range(NCORES)]

    groups = GROUPS or [[s for s in ALL_STAGES if s in STAGE_FNS]]
    for g in groups:
        wnames = sorted({w for s in g for w in STAGE_WEIGHTS[s]})
        Wmeta = {n: W[n] for n in wnames}
        Wmeta.update(consts)
        nc = build_group_program(g, Wmeta)
        in_maps = []
        for b in range(NCORES):
            m = {"xio": xs[b]}
            m.update(Wmeta)
            in_maps.append(m)
        res = run_bass_kernel_spmd(nc, in_maps, core_ids=list(range(NCORES)))
        xs = [np.asarray(res.results[b]["xout"]) for b in range(NCORES)]

    out = np.stack([xs[b].T for b in range(NCORES)], axis=0)
    return np.ascontiguousarray(out.astype(np.float32))
